# version 04
# baseline (speedup 1.0000x reference)
import contextlib
import sys
if "/opt/trn_rl_repo" not in sys.path:
    sys.path.insert(0, "/opt/trn_rl_repo")
import numpy as np
import ml_dtypes
import concourse.bass as bass
import concourse.mybir as mybir
from concourse.bass_utils import run_bass_kernel_spmd

F32 = mybir.dt.float32
BF16 = mybir.dt.bfloat16
AF = mybir.ActivationFunctionType
ALU = mybir.AluOpType
AX = mybir.AxisListType

B, N, D = 4, 4096, 1024
T = 2048
NT = 512
C = D // 128
NT3 = 256
NEG = -1.0e9
EPS = 1e-5
BF = ml_dtypes.bfloat16


def _cols(v):
    v = np.asarray(v, np.float32)
    return np.ascontiguousarray(v.reshape(-1, 128).T)


def own_token_idx(h):
    return np.concatenate([np.arange(128) + 128 * (2 * i + h) for i in range(16)])


def prep_inputs(inp):
    f = {k: np.asarray(v, np.float32) for k, v in inp.items()}
    x = f["L0_attn_x"]

    sh = {}
    sh["wq"] = f["L0_attn_Wq"]
    sh["wk"] = f["L0_attn_Wk"]
    sh["wv"] = f["L0_attn_Wv"]
    sh["wfc"] = f["L0_attn_Wfc"]
    sh["bfc_c"] = _cols(f["L0_attn_bfc"])
    sh["g0_row"] = np.ascontiguousarray(f["L0_attn_gamma"][None, :])
    sh["b0_row"] = np.ascontiguousarray(f["L0_attn_beta"][None, :])
    sh["w1"] = f["L0_ffn_W1"]
    sh["b1_c"] = _cols(f["L0_ffn_b1"])
    sh["w2"] = f["L0_ffn_W2"]
    sh["b2_c"] = _cols(f["L0_ffn_b2"])
    sh["ln1w_c"] = _cols(f["L1_attn_ln1_w"])
    sh["ln1b_c"] = _cols(f["L1_attn_ln1_b"])
    sh["wqkT"] = np.ascontiguousarray(f["L1_attn_Wqk"].T)
    sh["bqk_c"] = _cols(f["L1_attn_bqk"])
    sh["wvT1"] = np.ascontiguousarray(f["L1_attn_Wv"].T)
    sh["bv_row"] = np.ascontiguousarray(f["L1_attn_bv"][None, :])
    sh["wgT"] = np.ascontiguousarray(f["L1_attn_Wg"].T)
    sh["bg_row"] = np.ascontiguousarray(f["L1_attn_bg"][None, :])
    sh["bg_c"] = _cols(f["L1_attn_bg"])
    sh["woT"] = np.ascontiguousarray(f["L1_attn_Wo"].T)
    sh["bo_c"] = _cols(f["L1_attn_bo"])
    sh["ln2w_c"] = _cols(f["L1_attn_ln2_w"])
    sh["ln2b_c"] = _cols(f["L1_attn_ln2_b"])
    gam, bet = f["L1_attn_gamma"], f["L1_attn_beta"]
    sh["gh_c"] = np.ascontiguousarray(gam.reshape(4, 2, 128).transpose(2, 0, 1).reshape(128, 8))
    sh["bh_c"] = np.ascontiguousarray(bet.reshape(4, 2, 128).transpose(2, 0, 1).reshape(128, 8))
    sh["g3_row"] = np.ascontiguousarray(gam[3][None, :])
    sh["b3_row"] = np.ascontiguousarray(bet[3][None, :])
    idx = np.arange(64)
    dist = np.clip(np.abs(idx[:, None] - idx[None, :]), 0, 32)
    sh["bias64"] = np.ascontiguousarray(f["L1_attn_pos_emb"][dist]).astype(np.float32)
    sh["wgmT"] = np.ascontiguousarray(f["L1_ffn_Wg"].T)
    sh["w1T"] = np.ascontiguousarray(f["L1_ffn_W1"].transpose(0, 2, 1))
    sh["w2T"] = np.ascontiguousarray(f["L1_ffn_W2"].transpose(0, 2, 1))
    sh["w3T"] = np.ascontiguousarray(f["L1_ffn_W3"].transpose(0, 2, 1))
    sh["a1T"] = np.ascontiguousarray(f["L1_ffn_A1"].transpose(0, 2, 1))
    sh["a2T"] = np.ascontiguousarray(f["L1_ffn_A2"].transpose(0, 2, 1))
    sh["a3T"] = np.ascontiguousarray(f["L1_ffn_A3"].transpose(0, 2, 1))
    sh["b1l"] = np.ascontiguousarray(f["L1_ffn_B1"].transpose(0, 2, 1))
    sh["b2l"] = np.ascontiguousarray(f["L1_ffn_B2"].transpose(0, 2, 1))
    sh["b3l"] = np.ascontiguousarray(f["L1_ffn_B3"].transpose(0, 2, 1))
    sh["bm1_c"] = np.ascontiguousarray(np.stack([_cols(v) for v in f["L1_ffn_b1"]]))
    sh["bm2_c"] = np.ascontiguousarray(np.stack([_cols(v) for v in f["L1_ffn_b2"]]))
    sh["bm3_c"] = np.ascontiguousarray(np.stack([_cols(v) for v in f["L1_ffn_b3"]]))
    sh["bm2_row"] = np.ascontiguousarray(f["L1_ffn_b2"])
    sh["onesf"] = np.ones((1, 128), np.float32)
    sh["ident"] = np.eye(128, dtype=np.float32)

    half = 128
    inv_freq = (10000.0 ** (-np.arange(half, dtype=np.float32) / half)).astype(np.float32)

    ins = []
    for c in range(8):
        b, h = c // 2, c % 2
        d = dict(sh)
        oi = own_token_idx(h)
        xt = np.ascontiguousarray(x[b].T)
        d["xT_all"] = xt
        d["xT_own"] = np.ascontiguousarray(xt[:, oi])
        masks = np.zeros((16, 2, 128, 128), np.float32)
        vsel = np.zeros((16, 2, 128, 128), np.float32)
        for i in range(16):
            g = 2 * i + h
            for jj, j in enumerate((2 * i, 2 * i + 1)):
                kk = np.arange(128)[None, :] + 128 * j
                qq = np.arange(128)[:, None] + 128 * g
                masks[i, jj] = np.where(kk <= qq, 0.0, NEG)
                if j == g:
                    vsel[i, jj] = np.eye(128, dtype=np.float32)
        d["masks"] = masks
        d["vsel"] = vsel
        pos = oi.astype(np.float32)
        sinu = pos[:, None] * inv_freq[None, :]
        d["cosF"] = np.ascontiguousarray(np.cos(sinu).T.astype(np.float32))
        d["sinF"] = np.ascontiguousarray(np.sin(sinu).T.astype(np.float32))
        d["cosT"] = np.ascontiguousarray(np.cos(sinu).astype(np.float32))
        d["sinT"] = np.ascontiguousarray(np.sin(sinu).astype(np.float32))
        ins.append(d)
    return ins



ENGS = ("sync", "tensor", "vector", "scalar", "gpsimd")


class SemC:
    __slots__ = ("h", "n", "name")
    def __init__(self, h, name):
        self.h, self.n, self.name = h, 0, name


class Prog:
    def __init__(self, nc):
        self.nc = nc
        self.ops = {e: [] for e in ENGS}

    def sem(self, name):
        return SemC(self.nc.alloc_semaphore(name), name)

    def on(self, eng, fn, wait=(), inc=()):
        w = [(s, (s.n if v is None else v)) for (s, v) in wait]
        self.ops[eng].append((w, fn, list(inc)))
        for s, k in inc:
            s.n += k

    def emit(self, block):
        for ename in ENGS:
            ops = self.ops[ename]
            def section(eng, ops=ops):
                last_wait = {}
                for (waits, fn, incs) in ops:
                    for s, v in waits:
                        if v > 0 and last_wait.get(id(s), 0) < v:
                            eng.wait_ge(s.h, v)
                            last_wait[id(s)] = v
                    instr = fn(eng) if fn is not None else None
                    if incs:
                        if instr is None:
                            instr = eng.nop()
                        for s, k in incs:
                            instr.then_inc(s.h, k)
            getattr(block, ename)(section)


class Ctx:
    def __init__(self, nc, pr, hin, hscratch, out):
        self.nc, self.pr, self.hin, self.hscratch, self.out = nc, pr, hin, hscratch, out
        self.stack = contextlib.ExitStack()
        self.pstack = None
        self.LD = pr.sem("LD")
        self.LG = pr.sem("LG")
        self.ST = pr.sem("ST")
        self.TC = pr.sem("TC")
        self.VC = pr.sem("VC")
        self.SC = pr.sem("SC")
        self.CC = pr.sem("CC")
        self.consts = {}

    def sbuf(self, name, shape, dtype):
        st = self.pstack if self.pstack is not None else self.stack
        return st.enter_context(self.nc.sbuf_tensor(name, list(shape), dtype))

    def psum(self, name, shape, dtype=F32):
        st = self.pstack if self.pstack is not None else self.stack
        return st.enter_context(self.nc.psum_tensor(name, list(shape), dtype))

    def barrier(self):
        sems = [self.LD, self.LG, self.ST, self.TC, self.VC, self.SC, self.CC]
        w = [(s, None) for s in sems]
        for e in ENGS:
            self.pr.on(e, None, wait=w)


def fm_tile(hbm_ap, t0, nt):
    return hbm_ap[:, t0:t0 + nt].rearrange("(c p) n -> p c n", p=128)


def tm_tile(hbm_ap, t0, nt):
    return hbm_ap[t0:t0 + nt, :].rearrange("(s p) d -> p s d", p=128)



def emit_ln_fm(ctx, bufs, x32v, x16v, xn16v, nt, w_col=None, b_col=None, t2_out=None,
               wait0=(), add_in=None, stats_fp32=False):
    pr, cst = ctx.pr, ctx.consts
    nC = x32v.shape[1]
    inv = 1.0 / (128 * nC)
    sq, rowb, t1 = bufs.get("sq"), bufs["row"], bufs["t1"]
    t1b = bufs.get("t1b", t1)
    row = lambda k: rowb[0:1, k * nt:(k + 1) * nt]
    ps_m, ps_v, ps_mb, ps_rb = bufs["st0"], bufs["st1"], bufs["bc0"], bufs["bc1"]

    if stats_fp32:
        base_w = list(wait0) + [(ctx.LD, None), (ctx.LG, None)]
        def mm_x(e):
            for c in range(nC):
                i = e.matmul(ps_m[0:1, 0:nt], cst["ones32c"][:, 0:1], x32v[:, c, :],
                             start=(c == 0), stop=(c == nC - 1))
            return i
        pr.on("tensor", mm_x, wait=base_w, inc=[(ctx.TC, 1)])
        for c in range(nC):
            pr.on("vector", lambda e, c=c: e.tensor_tensor(
                t1b[:, 0:nt], x32v[:, c, :], x32v[:, c, :], ALU.mult),
                wait=base_w + [(ctx.TC, ctx.TC.n)], inc=[(ctx.VC, 1)])
            v_c = ctx.VC.n
            pr.on("tensor", lambda e, c=c: e.matmul(
                ps_v[0:1, 0:nt], cst["ones32c"][:, 0:1], t1b[:, 0:nt],
                start=(c == 0), stop=(c == nC - 1)), wait=[(ctx.VC, v_c)],
                inc=[(ctx.TC, 1)])
        t_st = ctx.TC.n
    else:
        pr.on("vector", lambda e: e.tensor_tensor(sq[:, 0:nC, 0:nt], x32v, x32v, ALU.mult),
              wait=list(wait0) + [(ctx.LD, None), (ctx.LG, None)], inc=[(ctx.VC, 1)])
        v_sq = ctx.VC.n

        def mm_stats(e):
            for c in range(nC):
                e.matmul(ps_m[0:1, 0:nt], cst["ones16c"][:, 0:1], x16v[:, c, :],
                         start=(c == 0), stop=(c == nC - 1))
            for c in range(nC):
                i = e.matmul(ps_v[0:1, 0:nt], cst["ones16c"][:, 0:1], sq[:, c, 0:nt],
                             start=(c == 0), stop=(c == nC - 1))
            return i
        pr.on("tensor", mm_stats, wait=[(ctx.VC, v_sq), (ctx.LD, None), (ctx.LG, None)],
              inc=[(ctx.TC, 1)])
        t_st = ctx.TC.n

    def row_math(e):
        e.tensor_scalar(row(0), ps_m[0:1, 0:nt], inv, None, ALU.mult)
        e.tensor_tensor(row(3), row(0), row(0), ALU.mult)
        e.scalar_tensor_tensor(row(1), ps_v[0:1, 0:nt], inv,
                               row(3), ALU.mult, ALU.subtract)
        return e.tensor_scalar(row(3), row(1), EPS, None, ALU.add)
    pr.on("vector", row_math, wait=[(ctx.TC, t_st)], inc=[(ctx.VC, 1)])
    v_row = ctx.VC.n

    def row_rstd(e):
        e.activation(row(1), row(3), AF.Ln)
        return e.activation(row(2), row(1), AF.Exp, scale=-0.5)
    pr.on("scalar", row_rstd, wait=[(ctx.VC, v_row)], inc=[(ctx.SC, 1)])
    s_r = ctx.SC.n

    def mm_bcast(e):
        e.matmul(ps_mb[:, 0:nt], cst["ones32"][0:1, :], row(0), start=True, stop=True)
        return e.matmul(ps_rb[:, 0:nt], cst["ones32"][0:1, :], row(2), start=True, stop=True)
    pr.on("tensor", mm_bcast, wait=[(ctx.SC, s_r), (ctx.VC, v_row)], inc=[(ctx.TC, 1)])
    t_bc = ctx.TC.n

    def apply(e):
        for c in range(nC):
            e.tensor_tensor(t1[:, 0:nt], x32v[:, c, :], ps_mb[:, 0:nt], ALU.subtract)
            if w_col is not None and add_in is not None:
                e.tensor_tensor(t1b[:, 0:nt], t1[:, 0:nt], ps_rb[:, 0:nt], ALU.mult)
                e.tensor_scalar(t1[:, 0:nt], t1b[:, 0:nt],
                                w_col[:, c:c + 1], b_col[:, c:c + 1], ALU.mult, ALU.add)
                i = e.tensor_tensor(xn16v[:, c, :], t1[:, 0:nt], add_in[:, c, :], ALU.add)
            elif w_col is not None:
                e.tensor_tensor(t1b[:, 0:nt], t1[:, 0:nt], ps_rb[:, 0:nt], ALU.mult)
                i = e.tensor_scalar(xn16v[:, c, :], t1b[:, 0:nt],
                                    w_col[:, c:c + 1], b_col[:, c:c + 1], ALU.mult, ALU.add)
            elif t2_out is not None:
                e.tensor_tensor(t2_out[:, c, :], t1[:, 0:nt], ps_rb[:, 0:nt], ALU.mult)
                i = e.tensor_copy(xn16v[:, c, :], t2_out[:, c, :])
            else:
                i = e.tensor_tensor(xn16v[:, c, :], t1[:, 0:nt], ps_rb[:, 0:nt], ALU.mult)
        return i
    pr.on("vector", apply, wait=[(ctx.TC, t_bc)], inc=[(ctx.VC, 1)])
    return ctx.VC.n


class ProjPool:
    def __init__(self, ctx, name, n, nt, dtype=F32):
        self.ctx = ctx
        self.tiles = [ctx.psum(f"{name}{i}", [128, nt], dtype) for i in range(n)]
        self.free_at = [(None, 0)] * n
        self.i = 0

    def acquire(self):
        j = self.i % len(self.tiles)
        self.i += 1
        return j, self.tiles[j], self.free_at[j]

    def release(self, j, sem, count):
        self.free_at[j] = (sem, count)


def emit_proj_fm(ctx, w_sb, act16, nt, pool, ready, evac, n_out_chunks, extra_wait=(),
                 evac_eng="scalar", evac_wait=()):
    pr = ctx.pr
    nC = act16.shape[1]
    for mo in range(n_out_chunks):
        j, ps, free = pool.acquire()
        w = list(ready) + list(extra_wait)
        if free[0] is not None:
            w.append((free[0], free[1]))

        def mm(e, mo=mo, ps=ps):
            for c in range(nC):
                i = e.matmul(ps[:, 0:nt], w_sb[:, c, mo * 128:(mo + 1) * 128],
                             act16[:, c, 0:nt], start=(c == 0), stop=(c == nC - 1))
            return i
        pr.on("tensor", mm, wait=w, inc=[(ctx.TC, 1)])
        t_mm = ctx.TC.n
        esem = ctx.SC if evac_eng == "scalar" else ctx.VC
        pr.on(evac_eng, lambda e, mo=mo, ps=ps: evac(e, mo, ps),
              wait=[(ctx.TC, t_mm)] + list(evac_wait), inc=[(esem, 1)])
        pool.release(j, esem, esem.n)
    return esem.n



SCRATCH = {
    "s_xn": ([D, T], BF16, "p0", ["p3"]),
    "s_q": ([D, T], BF16, "p0", ["p1"]),
    "s_k": ([D, N], BF16, "p0", ["p1"]),
    "s_v": ([N, D], BF16, "p0", ["p1"]),
    "s_af": ([T, D], BF16, "p1", ["p2"]),
    "s_x1a": ([D, T], F32, "p2", ["p3"]),
    "s_x1": ([D, T], F32, "p3", ["p4", "p5"]),
    "s_xn1": ([D, T], BF16, "p4", ["p6"]),
    "s_u": ([D, T], BF16, "p4", ["p5"]),
    "s_qkf": ([256, T], BF16, "p4", ["p5"]),
    "s_vt": ([T, 2048], BF16, "p4", ["p5"]),
    "s_lkv": ([256, 2048], F32, "p4", ["p5"]),
    "s_x2a": ([T, D], F32, "p5", ["p6"]),
    "s_gates": ([128, 128], F32, "p4", ["p6"]),
}


def build_program(phases):
    nc = bass.Bass("TRN2", target_bir_lowering=False)
    pr = Prog(nc)
    hbm = {}
    in_names = []

    def hin(name, shape, dtype=F32):
        if name not in hbm:
            hbm[name] = nc.dram_tensor(name, list(shape), dtype, kind="ExternalInput").ap()
            in_names.append(name)
        return hbm[name]

    def hscratch(name):
        if name not in hbm:
            shape, dt, prod, cons = SCRATCH[name]
            if prod in phases:
                kind = "Internal" if all(cp in phases for cp in cons) else "ExternalOutput"
            else:
                kind = "ExternalInput"
                in_names.append(name)
            hbm[name] = nc.dram_tensor(name, list(shape), dt, kind=kind).ap()
        return hbm[name]

    out = None
    if "p6" in phases:
        out = nc.dram_tensor("out_x2", [T, D], F32, kind="ExternalOutput").ap()

    ctx = Ctx(nc, pr, hin, hscratch, out)
    with ctx.stack:
        _emit_prolog(ctx)
        for ph in phases:
            ctx.barrier()
            with contextlib.ExitStack() as ps:
                ctx.pstack = ps
                PHASE_EMITTERS[ph](ctx)
                ctx.pstack = None
        _emit_epilog(ctx)
        with nc.Block() as block:
            pr.emit(block)
    return nc, in_names


def _emit_prolog(ctx):
    pr, hin = ctx.pr, ctx.hin
    cst = ctx.consts
    cst["ones32"] = ctx.sbuf("c_ones32", [1, 128], F32)
    cst["ones16"] = ctx.sbuf("c_ones16", [1, 128], BF16)
    cst["ones16c"] = ctx.sbuf("c_ones16c", [128, 1], BF16)
    cst["ones32c"] = ctx.sbuf("c_ones32c", [128, 1], F32)
    cst["ident32"] = ctx.sbuf("c_ident32", [128, 128], F32)
    cst["ident16"] = ctx.sbuf("c_ident16", [128, 128], BF16)
    onesf = hin("onesf", [1, 128])
    ident = hin("ident", [128, 128])
    pr.on("sync", lambda e: e.dma_start(out=cst["ones32"][:, :], in_=onesf), inc=[(ctx.LD, 16)])
    pr.on("sync", lambda e: e.dma_start(out=cst["ident32"][:, :], in_=ident), inc=[(ctx.LD, 16)])
    pr.on("gpsimd", lambda e: e.dma_start(out=cst["ones16"][:, :], in_=onesf), inc=[(ctx.LG, 16)])
    pr.on("gpsimd", lambda e: e.dma_start(
        out=cst["ones16c"][:, :], in_=onesf[0:1, 0:1].partition_broadcast(128)),
        inc=[(ctx.LG, 16)])
    pr.on("sync", lambda e: e.dma_start(
        out=cst["ones32c"][:, :], in_=onesf[0:1, 0:1].partition_broadcast(128)),
        inc=[(ctx.LD, 16)])
    pr.on("gpsimd", lambda e: e.dma_start(out=cst["ident16"][:, :], in_=ident), inc=[(ctx.LG, 16)])


def _emit_epilog(ctx):
    pr = ctx.pr
    pr.on("sync", None, wait=[(ctx.ST, None), (ctx.LD, None)])
    pr.on("gpsimd", None, wait=[(ctx.ST, None), (ctx.LG, None)])



def p0(ctx):
    pr = ctx.pr
    xa = ctx.hin("xT_all", [D, N])
    xo = ctx.hin("xT_own", [D, T])
    s_k, s_v = ctx.hscratch("s_k"), ctx.hscratch("s_v")
    s_q, s_xn = ctx.hscratch("s_q"), ctx.hscratch("s_xn")

    w_sb = {}
    for nm in ("wq", "wk", "wv"):
        src = ctx.hin(nm, [D, D])
        w_sb[nm] = ctx.sbuf(f"p0_{nm}", [128, C, D], BF16)
        pr.on("gpsimd", lambda e, dst=w_sb[nm], src=src: e.dma_start(
            out=dst[:, :, :], in_=src.rearrange("(c p) o -> p c o", p=128)),
            inc=[(ctx.LG, 16)])

    xa32 = [ctx.sbuf(f"p0_xa32_{i}", [128, C, NT], F32) for i in range(2)]
    xa16 = [ctx.sbuf(f"p0_xa16_{i}", [128, C, NT], BF16) for i in range(2)]
    ln_ps = {"st0": ctx.psum("p0_ps_s0", [128, NT]),
             "st1": ctx.psum("p0_ps_s1", [128, NT]),
             "bc0": ctx.psum("p0_ps_b0", [128, NT]),
             "bc1": ctx.psum("p0_ps_b1", [128, NT])}
    lnb = [dict(ln_ps,
                sq=ctx.sbuf(f"p0_sq_{i}", [128, C, NT], BF16),
                row=ctx.sbuf(f"p0_row_{i}", [1, 4 * NT], F32),
                t1=ctx.sbuf(f"p0_t1_{i}", [128, NT], F32)) for i in range(2)]
    xn16 = [ctx.sbuf(f"p0_xn_{i}", [128, C, NT], BF16) for i in range(2)]
    kbuf = [ctx.sbuf(f"p0_kb_{i}", [128, C, NT], BF16) for i in range(2)]
    vbuf = [ctx.sbuf(f"p0_vb_{i}", [128, NT // 128, D], BF16) for i in range(2)]
    pool = ProjPool(ctx, "p0_po", 2, NT)

    buf_ok = [[], []]

    def do_tile(src, t, own):
        p = t % 2
        w_guard = list(buf_ok[p])
        pr.on("sync", lambda e, t=t, p=p, src=src: e.dma_start(
            out=xa32[p][:, :, :], in_=fm_tile(src, t * NT, NT)),
            wait=w_guard, inc=[(ctx.LD, 16)])
        pr.on("gpsimd", lambda e, t=t, p=p, src=src: e.dma_start(
            out=xa16[p][:, :, :], in_=fm_tile(src, t * NT, NT)),
            wait=w_guard, inc=[(ctx.LG, 16)])
        v_xn = emit_ln_fm(ctx, lnb[p], xa32[p][:, :, :], xa16[p][:, :, :],
                          xn16[p][:, :, :], NT, wait0=w_guard)
        ready = [(ctx.VC, v_xn)]
        st_marks = []
        if own:
            pr.on("sync", lambda e, t=t, p=p: e.dma_start(
                out=fm_tile(s_xn, t * NT, NT), in_=xn16[p][:, :, :]),
                wait=ready, inc=[(ctx.ST, 16)])
            st_marks.append(ctx.ST.n)

            def ev_q(e, mo, ps, p=p):
                return e.activation(kbuf[p][:, mo, 0:NT], ps[:, 0:NT], AF.Copy,
                                    scale=1.0 / 32.0)
            emit_proj_fm(ctx, w_sb["wq"], xn16[p][:, :, :], NT, pool, ready, ev_q, C)
            s_done = ctx.SC.n
            pr.on("sync", lambda e, t=t, p=p: e.dma_start(
                out=fm_tile(s_q, t * NT, NT), in_=kbuf[p][:, :, :]),
                wait=[(ctx.SC, s_done)], inc=[(ctx.ST, 16)])
            st_marks.append(ctx.ST.n)
        else:
            def ev_k(e, mo, ps, p=p):
                return e.activation(kbuf[p][:, mo, 0:NT], ps[:, 0:NT], AF.Copy)
            emit_proj_fm(ctx, w_sb["wk"], xn16[p][:, :, :], NT, pool, ready, ev_k, C)
            s_done = ctx.SC.n
            pr.on("sync", lambda e, t=t, p=p: e.dma_start(
                out=fm_tile(s_k, t * NT, NT), in_=kbuf[p][:, :, :]),
                wait=[(ctx.SC, s_done)], inc=[(ctx.ST, 16)])
            st_marks.append(ctx.ST.n)
            for sub in range(NT // 128):
                for half in range(2):
                    j, ps, free = pool.acquire()
                    w = list(ready) + ([free] if free[0] is not None else [])

                    def mmv(e, sub=sub, half=half, ps=ps, p=p):
                        for c in range(C):
                            i = e.matmul(ps[:, 0:NT],
                                         xn16[p][:, c, sub * 128:(sub + 1) * 128],
                                         w_sb["wv"][:, c, half * NT:(half + 1) * NT],
                                         start=(c == 0), stop=(c == C - 1))
                        return i
                    pr.on("tensor", mmv, wait=w, inc=[(ctx.TC, 1)])
                    t_mm = ctx.TC.n
                    pr.on("scalar", lambda e, sub=sub, half=half, ps=ps, p=p: e.activation(
                        vbuf[p][:, sub, half * NT:(half + 1) * NT], ps[:, 0:NT], AF.Copy),
                        wait=[(ctx.TC, t_mm)], inc=[(ctx.SC, 1)])
                    pool.release(j, ctx.SC, ctx.SC.n)
            s_done = ctx.SC.n
            pr.on("sync", lambda e, t=t, p=p: e.dma_start(
                out=tm_tile(s_v, t * NT, NT), in_=vbuf[p][:, :, :]),
                wait=[(ctx.SC, s_done)], inc=[(ctx.ST, 16)])
            st_marks.append(ctx.ST.n)
        buf_ok[p] = [(ctx.TC, ctx.TC.n), (ctx.VC, ctx.VC.n), (ctx.SC, ctx.SC.n),
                     (ctx.ST, max(st_marks))]

    for t in range(N // NT):
        do_tile(xa, t, own=False)
    for t in range(T // NT):
        do_tile(xo, t, own=True)


P1_DEBUG = False


def p1(ctx):
    pr = ctx.pr
    cst = ctx.consts
    s_k, s_v = ctx.hscratch("s_k"), ctx.hscratch("s_v")
    s_q, s_af = ctx.hscratch("s_q"), ctx.hscratch("s_af")
    masks = ctx.hin("masks", [16, 2, 128, 128])
    vsel = ctx.hin("vsel", [16, 2, 128, 128])
    g0r = ctx.hin("g0_row", [1, D])
    b0r = ctx.hin("b0_row", [1, D])

    k_sb = ctx.sbuf("p1_k", [128, C, N], BF16)
    v_sb = ctx.sbuf("p1_v", [128, N // 128, D], BF16)
    q_sb = [ctx.sbuf(f"p1_q{i}", [128, C, 128], BF16) for i in range(2)]
    mk_sb = [ctx.sbuf(f"p1_mk{i}", [128, 2, 128], F32) for i in range(2)]
    vs_sb = [ctx.sbuf(f"p1_vs{i}", [128, 2, 128], BF16) for i in range(2)]
    srow_b = [ctx.sbuf(f"p1_srow{i}", [128, N], F32) for i in range(2)]
    prow_b = [ctx.sbuf(f"p1_prow{i}", [128, N], BF16) for i in range(2)]
    prow2_b = [ctx.sbuf(f"p1_prow2{i}", [128, N], BF16) for i in range(2)]
    mch_b = [ctx.sbuf(f"p1_mch{i}", [128, 16], F32) for i in range(2)]
    mneg_b = [ctx.sbuf(f"p1_mneg{i}", [128, 1], F32) for i in range(2)]
    lcol_b = [ctx.sbuf(f"p1_l{i}", [128, 1], F32) for i in range(2)]
    rcol_b = [ctx.sbuf(f"p1_r{i}", [128, 1], F32) for i in range(2)]
    pt_sb = [ctx.sbuf(f"p1_pt{i}", [128, 128], BF16) for i in range(2)]
    afb = ctx.sbuf("p1_af", [128, D], BF16)
    g0b = ctx.sbuf("p1_g0b", [128, D], BF16)
    b0b = ctx.sbuf("p1_b0b", [128, D], BF16)
    m2c = ctx.sbuf("p1_m2c", [128, 1], F32)
    v2c = ctx.sbuf("p1_v2c", [128, 1], F32)
    m2a = ctx.sbuf("p1_m2a", [128, 1], F32)
    v2a = ctx.sbuf("p1_v2a", [128, 1], F32)
    v2b = ctx.sbuf("p1_v2b", [128, 1], F32)
    ps_s = [ctx.psum(f"p1_ps_s{i}", [128, 512]) for i in range(2)]
    ps_o = ctx.psum("p1_ps_o", [128, D])
    ps_t = [ctx.psum(f"p1_ps_t{i}", [128, 128], BF16) for i in range(2)]

    pr.on("gpsimd", lambda e: e.dma_start(
        out=k_sb[:, :, :], in_=s_k.rearrange("(c p) n -> p c n", p=128)), inc=[(ctx.LG, 16)])
    pr.on("gpsimd", lambda e: e.dma_start(
        out=v_sb[:, :, :], in_=s_v.rearrange("(j p) d -> p j d", p=128)), inc=[(ctx.LG, 16)])
    pr.on("gpsimd", lambda e: e.dma_start(
        out=g0b[:, :], in_=g0r.partition_broadcast(128)), inc=[(ctx.LG, 16)])
    pr.on("gpsimd", lambda e: e.dma_start(
        out=b0b[:, :], in_=b0r.partition_broadcast(128)), inc=[(ctx.LG, 16)])

    qload_guard = [[], []]
    ps_s_free = [(None, 0), (None, 0)]
    pt_ps_free = [(None, 0), (None, 0)]
    pt_sb_free = [(None, 0), (None, 0)]
    srow_guard = [[], []]
    st_af_prev = [0]

    for i in range(16):
        nk = 2 * i + 2
        ncol = nk * 128
        nkc = (ncol + 511) // 512
        pq = i % 2
        srow, prow, prow2 = srow_b[pq], prow_b[pq], prow2_b[pq]
        mch, mneg, lcol, rcol = mch_b[pq], mneg_b[pq], lcol_b[pq], rcol_b[pq]
        fin32 = srow[:, 0:D]
        fin32b = srow[:, D:2 * D]
        pr.on("sync", lambda e, i=i, pq=pq: e.dma_start(
            out=q_sb[pq][:, :, :],
            in_=s_q[:, i * 128:(i + 1) * 128].rearrange("(c p) n -> p c n", p=128)),
            wait=qload_guard[pq], inc=[(ctx.LD, 16)])
        pr.on("sync", lambda e, i=i, pq=pq: e.dma_start(
            out=mk_sb[pq][:, :, :], in_=masks[i].rearrange("j q k -> q j k")),
            wait=qload_guard[pq], inc=[(ctx.LD, 16)])
        pr.on("gpsimd", lambda e, i=i, pq=pq: e.dma_start(
            out=vs_sb[pq][:, :, :], in_=vsel[i].rearrange("j q k -> q j k")),
            wait=qload_guard[pq], inc=[(ctx.LG, 16)])
        ld_q = ctx.LD.n
        lg_q = ctx.LG.n

        for kc in range(nkc):
            c0 = kc * 512
            cw = min(512, ncol - c0)
            jp, ps, free = kc % 2, ps_s[kc % 2], ps_s_free[kc % 2]
            w = [(ctx.LD, ld_q), (ctx.LG, None)]
            if free[0] is not None:
                w.append(free)

            def mm_s(e, i=i, pq=pq, ps=ps, c0=c0, cw=cw):
                for c in range(C):
                    r = e.matmul(ps[:, 0:cw], q_sb[pq][:, c, :], k_sb[:, c, c0:c0 + cw],
                                 start=(c == 0), stop=(c == C - 1))
                return r
            pr.on("tensor", mm_s, wait=w, inc=[(ctx.TC, 1)])
            t_s = ctx.TC.n

            last = (kc == nkc - 1)

            def ev_s(e, ps=ps, c0=c0, cw=cw, last=last, pq=pq, i=i, ncol=ncol, kc=kc):
                e.tensor_copy(srow[:, c0:c0 + cw], ps[:, 0:cw])
                if last:
                    e.tensor_tensor(srow[:, ncol - 256:ncol], srow[:, ncol - 256:ncol],
                                    mk_sb[pq][:, :, :].rearrange("p j k -> p (j k)"), ALU.add)
                return e.reduce_max(mch[:, kc:kc + 1], srow[:, c0:c0 + cw], axis=AX.X)
            pr.on("vector", ev_s, wait=[(ctx.TC, t_s)] + srow_guard[pq],
                  inc=[(ctx.VC, 1)])
            ps_s_free[jp] = (ctx.VC, ctx.VC.n)

        def red_m(e, nkc=nkc):
            if nkc < 16:
                e.memset(mch[:, nkc:16], -1.0e30)
            e.drain()
            return e.reduce_max(mneg[:, 0:1], mch[:, 0:16], axis=AX.X, negate=True)
        pr.on("vector", red_m, inc=[(ctx.VC, 1)])
        v_m = ctx.VC.n
        pr.on("scalar", lambda e, ncol=ncol: e.activation(
            prow[:, 0:ncol], srow[:, 0:ncol], AF.Exp, bias=mneg[:, 0:1]),
            wait=[(ctx.VC, v_m), (ctx.TC, ctx.TC.n)],
            inc=[(ctx.SC, 1)])
        s_exp = ctx.SC.n

        def p_fix(e, ncol=ncol, pq=pq):
            e.reduce_sum(lcol[:, 0:1], prow[:, 0:ncol], axis=AX.X)
            e.drain()
            e.reciprocal(rcol[:, 0:1], lcol[:, 0:1])
            e.drain()
            if ncol > 256:
                e.tensor_scalar(prow2[:, 0:ncol - 256], prow[:, 0:ncol - 256],
                                rcol[:, 0:1], None, ALU.mult)
            return e.scalar_tensor_tensor(
                prow2[:, ncol - 256:ncol], prow[:, ncol - 256:ncol], rcol[:, 0:1],
                vs_sb[pq][:, :, :].rearrange("p j k -> p (j k)"), ALU.mult, ALU.add)
        pr.on("vector", p_fix, wait=[(ctx.SC, s_exp), (ctx.LG, lg_q),
                                     (ctx.TC, ctx.TC.n)], inc=[(ctx.VC, 1)])
        v_p = ctx.VC.n

        mm_done_at = {}
        for j in range(nk):
            jp = j % 2
            w = [(ctx.VC, v_p)]
            if pt_ps_free[jp][0] is not None:
                w.append(pt_ps_free[jp])
            pr.on("tensor", lambda e, j=j, jp=jp: e.transpose(
                ps_t[jp][:, :], prow2[:, j * 128:(j + 1) * 128], cst["ident16"][:, :]),
                wait=w, inc=[(ctx.TC, 1)])
            t_t = ctx.TC.n
            wv = [(ctx.TC, t_t)]
            if pt_sb_free[jp][0] is not None:
                wv.append(pt_sb_free[jp])
            pr.on("vector", lambda e, jp=jp: e.tensor_copy(pt_sb[jp][:, :], ps_t[jp][:, :]),
                  wait=wv, inc=[(ctx.VC, 1)])
            pt_ps_free[jp] = (ctx.VC, ctx.VC.n)
            mm_done_at[j] = (ctx.VC.n, jp)
            if j >= 1:
                _emit_pv_mm(ctx, i, j - 1, nk, mm_done_at, v_sb, pt_sb, ps_o, pt_sb_free)
        _emit_pv_mm(ctx, i, nk - 1, nk, mm_done_at, v_sb, pt_sb, ps_o, pt_sb_free)
        t_pv = ctx.TC.n

        def ln2(e):
            e.tensor_copy(fin32, ps_o[:, :])
            e.reduce_sum(m2a[:, 0:1], fin32, axis=AX.X)
            e.drain()
            e.tensor_scalar(m2c[:, 0:1], m2a[:, 0:1], 1.0 / D, None, ALU.mult)
            e.drain()
            e.tensor_scalar(fin32b, fin32, m2c[:, 0:1], None, ALU.subtract)
            e.tensor_tensor(fin32, fin32b, fin32b, ALU.mult)
            e.reduce_sum(v2a[:, 0:1], fin32, axis=AX.X)
            e.drain()
            return e.tensor_scalar(v2c[:, 0:1], v2a[:, 0:1], 1.0 / D, EPS, ALU.mult, ALU.add)
        pr.on("vector", ln2, wait=[(ctx.TC, t_pv), (ctx.LD, None)], inc=[(ctx.VC, 1)])
        v_ln = ctx.VC.n

        def rstd2(e):
            e.activation(v2b[:, 0:1], v2c[:, 0:1], AF.Ln)
            e.drain()
            return e.activation(v2c[:, 0:1], v2b[:, 0:1], AF.Exp, scale=-0.5)
        pr.on("scalar", rstd2, wait=[(ctx.VC, v_ln)], inc=[(ctx.SC, 1)])
        s_r2 = ctx.SC.n

        def fin_aff(e):
            e.scalar_tensor_tensor(fin32, fin32b, v2c[:, 0:1], g0b[:, :],
                                   ALU.mult, ALU.mult)
            return e.tensor_tensor(afb[:, :], fin32, b0b[:, :], ALU.add)
        pr.on("vector", fin_aff, wait=[(ctx.SC, s_r2), (ctx.ST, st_af_prev[0])],
              inc=[(ctx.VC, 1)])
        v_fin = ctx.VC.n

        pr.on("sync", lambda e, i=i: e.dma_start(
            out=s_af[i * 128:(i + 1) * 128, :], in_=afb[:, :]),
            wait=[(ctx.VC, v_fin)], inc=[(ctx.ST, 16)])
        st_af_prev[0] = ctx.ST.n
        if P1_DEBUG and i == 1:
            nc = ctx.nc
            d_s = nc.dram_tensor("dbg_srow", [128, 512], F32, kind="ExternalOutput").ap()
            d_p = nc.dram_tensor("dbg_prow", [128, 512], BF16, kind="ExternalOutput").ap()
            d_m = nc.dram_tensor("dbg_m", [128, 4], F32, kind="ExternalOutput").ap()
            d_o = nc.dram_tensor("dbg_o", [128, D], F32, kind="ExternalOutput").ap()
            d_l = nc.dram_tensor("dbg_l", [3, 128, 1], F32, kind="ExternalOutput").ap()
            def dbg(e):
                e.dma_start(out=d_s, in_=srow[:, 0:512]).then_inc(ctx.ST.h, 16)
                e.dma_start(out=d_p, in_=prow[:, 0:512]).then_inc(ctx.ST.h, 16)
                e.dma_start(out=d_m, in_=mch[:, 0:4]).then_inc(ctx.ST.h, 16)
                e.dma_start(out=d_l[0], in_=lcol[:, 0:1]).then_inc(ctx.ST.h, 16)
                e.dma_start(out=d_l[1], in_=rcol[:, 0:1]).then_inc(ctx.ST.h, 16)
                e.dma_start(out=d_l[2], in_=mneg[:, 0:1]).then_inc(ctx.ST.h, 16)
                return e.dma_start(out=d_o, in_=fin32[:, :])
            pr.on("sync", dbg, wait=[(ctx.VC, ctx.VC.n)], inc=[(ctx.ST, 16)])
            ctx.ST.n += 80
            stall = ctx.ST.n
            pr.on("vector", None, wait=[(ctx.ST, stall)], inc=[(ctx.VC, 1)])
            pr.on("scalar", None, wait=[(ctx.ST, stall)], inc=[(ctx.SC, 1)])
            pr.on("tensor", None, wait=[(ctx.ST, stall)], inc=[(ctx.TC, 1)])
        qload_guard[pq] = [(ctx.TC, ctx.TC.n), (ctx.VC, ctx.VC.n), (ctx.SC, ctx.SC.n)]


def _emit_pv_mm(ctx, i, j, nk, mm_done_at, v_sb, pt_sb, ps_o, pt_sb_free):
    pr = ctx.pr
    v_c, jp = mm_done_at[j]

    def mm(e, j=j, jp=jp, nk=nk):
        e.matmul(ps_o[:, 0:512], pt_sb[jp][:, :], v_sb[:, j, 0:512],
                 start=(j == 0), stop=(j == nk - 1))
        return e.matmul(ps_o[:, 512:1024], pt_sb[jp][:, :], v_sb[:, j, 512:1024],
                        start=(j == 0), stop=(j == nk - 1))
    pr.on("tensor", mm, wait=[(ctx.VC, v_c)], inc=[(ctx.TC, 1)])
    pt_sb_free[jp] = (ctx.TC, ctx.TC.n)


def p2(ctx):
    pr, cst = ctx.pr, ctx.consts
    s_af = ctx.hscratch("s_af")
    s_x1a = ctx.hscratch("s_x1a")
    xo = ctx.hin("xT_own", [D, T])
    wfc = ctx.hin("wfc", [D, D])
    bfc = ctx.hin("bfc_c", [128, 8])

    w_sb = ctx.sbuf("p2_wfc", [128, C, D], BF16)
    bfc_sb = ctx.sbuf("p2_bfc", [128, 8], F32)
    pr.on("gpsimd", lambda e: e.dma_start(
        out=w_sb[:, :, :], in_=wfc.rearrange("(c p) o -> p c o", p=128)), inc=[(ctx.LG, 16)])
    lg_w2 = ctx.LG.n
    pr.on("sync", lambda e: e.dma_start(out=bfc_sb[:, :], in_=bfc), inc=[(ctx.LD, 16)])

    af_tm = [ctx.sbuf(f"p2_af{i}", [128, 4, D], BF16) for i in range(2)]
    af_fm = [ctx.sbuf(f"p2_aff{i}", [128, C, NT], BF16) for i in range(2)]
    x32 = [ctx.sbuf(f"p2_x{i}", [128, C, NT], F32) for i in range(2)]
    x1a = [ctx.sbuf(f"p2_x1a{i}", [128, C, NT], F32) for i in range(2)]
    ps_t = [ctx.psum(f"p2_pt{i}", [128, 128], BF16) for i in range(2)]
    pool = ProjPool(ctx, "p2_po", 2, NT)

    buf_ok = [[], []]
    pt_free = [(None, 0), (None, 0)]
    for t in range(T // NT):
        p = t % 2
        w_guard = list(buf_ok[p])
        pr.on("sync", lambda e, t=t, p=p: e.dma_start(
            out=af_tm[p][:, :, :], in_=tm_tile(s_af, t * NT, NT)),
            wait=w_guard, inc=[(ctx.LD, 16)])
        pr.on("sync", lambda e, t=t, p=p: e.dma_start(
            out=x32[p][:, :, :], in_=fm_tile(xo, t * NT, NT)),
            wait=w_guard, inc=[(ctx.LD, 16)])
        ld_t = ctx.LD.n
        for sub in range(4):
            for c in range(C):
                jp = (sub * C + c) % 2
                w = [(ctx.LD, ld_t)]
                if pt_free[jp][0] is not None:
                    w.append(pt_free[jp])
                pr.on("tensor", lambda e, p=p, sub=sub, c=c, jp=jp: e.transpose(
                    ps_t[jp][:, :], af_tm[p][:, sub, c * 128:(c + 1) * 128],
                    cst["ident16"][:, :]), wait=w, inc=[(ctx.TC, 1)])
                t_t = ctx.TC.n
                pr.on("vector", lambda e, p=p, sub=sub, c=c, jp=jp: e.tensor_copy(
                    af_fm[p][:, c, sub * 128:(sub + 1) * 128], ps_t[jp][:, :]),
                    wait=[(ctx.TC, t_t)], inc=[(ctx.VC, 1)])
                pt_free[jp] = (ctx.VC, ctx.VC.n)
        v_tr = ctx.VC.n

        def ev_fc(e, mo, ps, p=p):
            return e.scalar_tensor_tensor(x1a[p][:, mo, 0:NT], ps[:, 0:NT],
                                          bfc_sb[:, mo:mo + 1], x32[p][:, mo, 0:NT],
                                          ALU.add, ALU.add)
        emit_proj_fm(ctx, w_sb, af_fm[p][:, :, :], NT, pool,
                     [(ctx.VC, v_tr), (ctx.LG, lg_w2)], ev_fc, C,
                     evac_eng="vector", evac_wait=[(ctx.LD, ld_t)])
        v_add = ctx.VC.n
        pr.on("sync", lambda e, t=t, p=p: e.dma_start(
            out=fm_tile(s_x1a, t * NT, NT), in_=x1a[p][:, :, :]),
            wait=[(ctx.VC, v_add)], inc=[(ctx.ST, 16)])
        buf_ok[p] = [(ctx.TC, ctx.TC.n), (ctx.VC, ctx.VC.n), (ctx.SC, ctx.SC.n),
                     (ctx.ST, ctx.ST.n)]


def p3(ctx):
    pr = ctx.pr
    s_xn = ctx.hscratch("s_xn")
    s_x1a = ctx.hscratch("s_x1a")
    s_x1 = ctx.hscratch("s_x1")
    w1 = ctx.hin("w1", [D, 4096])
    w2 = ctx.hin("w2", [4096, D])
    b1 = ctx.hin("b1_c", [128, 32])
    b2 = ctx.hin("b2_c", [128, 8])

    w1_sb = ctx.sbuf("p3_w1", [128, C, 4096], BF16)
    w2_sb = ctx.sbuf("p3_w2", [128, 32, D], BF16)
    b1_sb = ctx.sbuf("p3_b1", [128, 32], F32)
    b2_sb = ctx.sbuf("p3_b2", [128, 8], F32)
    pr.on("gpsimd", lambda e: e.dma_start(
        out=w1_sb[:, :, :], in_=w1.rearrange("(c p) o -> p c o", p=128)), inc=[(ctx.LG, 16)])
    pr.on("gpsimd", lambda e: e.dma_start(
        out=w2_sb[:, :, :], in_=w2.rearrange("(c p) o -> p c o", p=128)), inc=[(ctx.LG, 16)])
    lg_w3 = ctx.LG.n
    pr.on("sync", lambda e: e.dma_start(out=b1_sb[:, :], in_=b1), inc=[(ctx.LD, 16)])
    pr.on("sync", lambda e: e.dma_start(out=b2_sb[:, :], in_=b2), inc=[(ctx.LD, 16)])

    xn_sb = [ctx.sbuf(f"p3_xn{i}", [128, C, NT3], BF16) for i in range(2)]
    h_sb = ctx.sbuf("p3_h", [128, 32, NT3], BF16)
    x1a32 = ctx.sbuf("p3_x1a", [128, C, NT3], F32)
    x1b = ctx.sbuf("p3_x1b", [128, C, NT3], F32)
    pool = ProjPool(ctx, "p3_po", 2, NT3)
    h_guard = [0]
    x1a_guard = [[]]
    x1b_guard = [[]]

    buf_ok = [[], []]
    for t in range(T // NT3):
        p = t % 2
        w_guard = list(buf_ok[p])
        pr.on("sync", lambda e, t=t, p=p: e.dma_start(
            out=xn_sb[p][:, :, :], in_=fm_tile(s_xn, t * NT3, NT3)),
            wait=w_guard, inc=[(ctx.LD, 16)])
        pr.on("sync", lambda e, t=t, p=p: e.dma_start(
            out=x1a32[:, :, :], in_=fm_tile(s_x1a, t * NT3, NT3)),
            wait=w_guard + x1a_guard[0], inc=[(ctx.LD, 16)])
        ld_t = ctx.LD.n

        def ev_h(e, mo, ps, p=p):
            return e.activation(h_sb[:, mo, 0:NT3], ps[:, 0:NT3], AF.Relu,
                                bias=b1_sb[:, mo:mo + 1])
        emit_proj_fm(ctx, w1_sb, xn_sb[p][:, :, :], NT3, pool,
                     [(ctx.LD, ld_t), (ctx.TC, h_guard[0]), (ctx.LG, lg_w3)], ev_h, 32)
        s_h = ctx.SC.n

        def ev_y(e, mo, ps, p=p):
            return e.scalar_tensor_tensor(x1b[:, mo, 0:NT3], ps[:, 0:NT3],
                                          b2_sb[:, mo:mo + 1], x1a32[:, mo, 0:NT3],
                                          ALU.add, ALU.add)
        emit_proj_fm(ctx, w2_sb, h_sb[:, :, :], NT3, pool, [(ctx.SC, s_h)], ev_y, C,
                     evac_eng="vector", evac_wait=[(ctx.LD, ld_t)] + x1b_guard[0])
        v_add = ctx.VC.n
        h_guard[0] = ctx.TC.n
        x1a_guard[0] = [(ctx.VC, v_add)]
        pr.on("sync", lambda e, t=t, p=p: e.dma_start(
            out=fm_tile(s_x1, t * NT3, NT3), in_=x1b[:, :, :]),
            wait=[(ctx.VC, v_add)], inc=[(ctx.ST, 16)])
        x1b_guard[0] = [(ctx.ST, ctx.ST.n)]
        buf_ok[p] = [(ctx.TC, ctx.TC.n), (ctx.VC, ctx.VC.n), (ctx.SC, ctx.SC.n),
                     (ctx.ST, ctx.ST.n)]


def p4(ctx):
    pr, cst = ctx.pr, ctx.consts
    nc = ctx.nc
    s_x1 = ctx.hscratch("s_x1")
    s_xn1 = ctx.hscratch("s_xn1")
    s_u = ctx.hscratch("s_u")
    s_qkf = ctx.hscratch("s_qkf")
    s_vt = ctx.hscratch("s_vt")
    s_lkv = ctx.hscratch("s_lkv")
    wqkT = ctx.hin("wqkT", [D, 256])
    wvT1 = ctx.hin("wvT1", [D, 2048])
    bqk = ctx.hin("bqk_c", [128, 2])
    ln1w = ctx.hin("ln1w_c", [128, 8])
    ln1b = ctx.hin("ln1b_c", [128, 8])
    g3r = ctx.hin("g3_row", [1, 256])
    b3r = ctx.hin("b3_row", [1, 256])
    bvr = ctx.hin("bv_row", [1, 2048])
    cosT = ctx.hin("cosT", [T, 128])
    sinT = ctx.hin("sinT", [T, 128])
    cc_in = nc.dram_tensor("cc_in", [256, 2048], F32, kind="Internal").ap()
    cc_out = nc.dram_tensor("cc_out", [256, 2048], F32, kind="Internal").ap()

    s_gates = ctx.hscratch("s_gates")
    wgmT = ctx.hin("wgmT", [D, 8])
    wgm_sb = ctx.sbuf("p4_wgm", [128, C, 8], BF16)
    gt_sc = ctx.sbuf("p4_gt_sc", [128, 8], F32)
    gt_m1 = ctx.sbuf("p4_gt_m1", [128, 1], F32)
    gt_m1n = ctx.sbuf("p4_gt_m1n", [128, 1], F32)
    gt_m2 = ctx.sbuf("p4_gt_m2", [128, 1], F32)
    gt_tk = ctx.sbuf("p4_gt_tk", [128, 8], F32)
    gt_s2 = ctx.sbuf("p4_gt_s2", [128, 8], F32)
    gt_kp = ctx.sbuf("p4_gt_kp", [128, 8], F32)
    gt_eu = ctx.sbuf("p4_gt_eu", [128, 8], F32)
    gt_gk = ctx.sbuf("p4_gt_gk", [128, 8], F32)
    gt_dn = ctx.sbuf("p4_gt_dn", [128, 1], F32)
    gt_rd = ctx.sbuf("p4_gt_rd", [128, 1], F32)
    gt_out = ctx.sbuf("p4_gt_out", [128, 16, 8], F32)
    wqk_sb = ctx.sbuf("p4_wqk", [128, C, 256], BF16)
    wv_sb = ctx.sbuf("p4_wv", [128, C, 2048], BF16)
    bqk_sb = ctx.sbuf("p4_bqk", [128, 2], F32)
    ln1w_sb = ctx.sbuf("p4_l1w", [128, 8], F32)
    ln1b_sb = ctx.sbuf("p4_l1b", [128, 8], F32)
    g3b = ctx.sbuf("p4_g3b", [128, 256], F32)
    b3b = ctx.sbuf("p4_b3b", [128, 256], F32)
    bvb = ctx.sbuf("p4_bvb", [128, 2048], F32)
    cosT_sb = ctx.sbuf("p4_cosT", [128, 16, 128], BF16)
    sinT_sb = ctx.sbuf("p4_sinT", [128, 16, 128], BF16)
    pr.on("gpsimd", lambda e: e.dma_start(
        out=wgm_sb[:, :, :], in_=wgmT.rearrange("(c p) o -> p c o", p=128)), inc=[(ctx.LG, 16)])
    pr.on("gpsimd", lambda e: e.dma_start(
        out=wqk_sb[:, :, :], in_=wqkT.rearrange("(c p) o -> p c o", p=128)), inc=[(ctx.LG, 16)])
    pr.on("gpsimd", lambda e: e.dma_start(
        out=wv_sb[:, :, :], in_=wvT1.rearrange("(c p) o -> p c o", p=128)), inc=[(ctx.LG, 16)])
    pr.on("gpsimd", lambda e: e.dma_start(
        out=cosT_sb[:, :, :], in_=cosT.rearrange("(s p) f -> p s f", p=128)), inc=[(ctx.LG, 16)])
    pr.on("gpsimd", lambda e: e.dma_start(
        out=sinT_sb[:, :, :], in_=sinT.rearrange("(s p) f -> p s f", p=128)), inc=[(ctx.LG, 16)])
    pr.on("sync", lambda e: e.dma_start(out=bqk_sb[:, :], in_=bqk), inc=[(ctx.LD, 16)])
    pr.on("sync", lambda e: e.dma_start(out=ln1w_sb[:, :], in_=ln1w), inc=[(ctx.LD, 16)])
    pr.on("sync", lambda e: e.dma_start(out=ln1b_sb[:, :], in_=ln1b), inc=[(ctx.LD, 16)])
    pr.on("sync", lambda e: e.dma_start(out=g3b[:, :], in_=g3r.partition_broadcast(128)), inc=[(ctx.LD, 16)])
    pr.on("sync", lambda e: e.dma_start(out=b3b[:, :], in_=b3r.partition_broadcast(128)), inc=[(ctx.LD, 16)])
    pr.on("sync", lambda e: e.dma_start(out=bvb[:, :], in_=bvr.partition_broadcast(128)), inc=[(ctx.LD, 16)])
    lg_w = ctx.LG.n
    ld_w = ctx.LD.n

    x32 = ctx.sbuf("p4_x32", [128, C, NT], F32)
    x16 = ctx.sbuf("p4_x16", [128, C, NT], BF16)
    t2 = ctx.sbuf("p4_t2", [128, C, NT], F32)
    xn1_16 = ctx.sbuf("p4_xn1", [128, C, NT], BF16)
    u16 = ctx.sbuf("p4_u", [128, C, NT], BF16)
    qkf = ctx.sbuf("p4_qkf", [128, 2, NT], BF16)
    qkt = ctx.sbuf("p4_qkt", [128, 4, 256], BF16)
    qg = ctx.sbuf("p4_qg", [128, 256], BF16)
    qh = ctx.sbuf("p4_qh", [128, 256], BF16)
    lkr = ctx.sbuf("p4_lkr", [128, 4, 256], BF16)
    rta = ctx.sbuf("p4_rta", [128, 128], BF16)
    rtb = ctx.sbuf("p4_rtb", [128, 128], BF16)
    vtm = ctx.sbuf("p4_vtm", [128, 4, 2048], BF16)
    vtmp = ctx.sbuf("p4_vtmp", [128, 512], F32)
    lkv = ctx.sbuf("p4_lkv", [128, 2, 2048], F32)
    lkv2 = ctx.sbuf("p4_lkv2", [128, 2, 2048], F32)
    lnb = {"sq": ctx.sbuf("p4_sq", [128, C, NT], BF16),
           "row": ctx.sbuf("p4_row", [1, 4 * NT], F32),
           "t1": ctx.sbuf("p4_t1", [128, NT], F32),
           "t1b": ctx.sbuf("p4_t1b", [128, NT], F32),
           "st0": ctx.psum("p4_ps_s0", [128, NT]),
           "st1": ctx.psum("p4_ps_s1", [128, NT]),
           "bc0": ctx.psum("p4_ps_b0", [128, NT]),
           "bc1": ctx.psum("p4_ps_b1", [128, NT])}
    ps_t = [ctx.psum(f"p4_pt{i}", [128, 128], BF16) for i in range(2)]
    pool = ProjPool(ctx, "p4_po", 2, NT)

    buf_ok = [[]]
    pt_free = [(None, 0), (None, 0)]
    for t in range(T // NT):
        w_guard = list(buf_ok[0])
        pr.on("sync", lambda e, t=t: e.dma_start(
            out=x32[:, :, :], in_=fm_tile(s_x1, t * NT, NT)),
            wait=w_guard, inc=[(ctx.LD, 16)])
        pr.on("gpsimd", lambda e, t=t: e.dma_start(
            out=x16[:, :, :], in_=fm_tile(s_x1, t * NT, NT)),
            wait=w_guard, inc=[(ctx.LG, 16)])
        ld_t, lg_t = ctx.LD.n, ctx.LG.n

        v_xn1 = emit_ln_fm(ctx, lnb, x32[:, :, :], x16[:, :, :], xn1_16[:, :, :],
                           NT, t2_out=t2[:, :, :], wait0=w_guard)
        pr.on("sync", lambda e, t=t: e.dma_start(
            out=fm_tile(s_xn1, t * NT, NT), in_=xn1_16[:, :, :]),
            wait=[(ctx.VC, v_xn1)], inc=[(ctx.ST, 16)])
        gt_guard = [0]
        for q in range(4):
            tch = t * 4 + q
            jg, ps_gt, freeg = pool.acquire()

            def mmgt(e, q=q, ps_gt=ps_gt):
                for c in range(C):
                    i = e.matmul(ps_gt[:, 0:8], xn1_16[:, c, q * 128:(q + 1) * 128],
                                 wgm_sb[:, c, 0:8], start=(c == 0), stop=(c == C - 1))
                return i
            wgt = [(ctx.VC, v_xn1), (ctx.LG, lg_w), (ctx.VC, gt_guard[0])]
            if freeg[0] is not None:
                wgt.append(freeg)
            pr.on("tensor", mmgt, wait=wgt, inc=[(ctx.TC, 1)])
            t_gt = ctx.TC.n

            def top2(e, tch=tch, ps_gt=ps_gt):
                e.tensor_copy(gt_sc[:, :], ps_gt[:, 0:8])
                e.drain()
                e.reduce_max(gt_m1[:, 0:1], gt_sc[:, :], axis=AX.X)
                e.drain()
                e.tensor_scalar(gt_m1n[:, 0:1], gt_m1[:, 0:1], -1.0, None, ALU.mult)
                e.tensor_scalar(gt_tk[:, :], gt_sc[:, :], gt_m1[:, 0:1], None, ALU.is_ge)
                e.drain()
                e.scalar_tensor_tensor(gt_s2[:, :], gt_tk[:, :], -1.0e9, gt_sc[:, :],
                                       ALU.mult, ALU.add)
                e.drain()
                e.reduce_max(gt_m2[:, 0:1], gt_s2[:, :], axis=AX.X)
                e.drain()
                return e.tensor_scalar(gt_kp[:, :], gt_sc[:, :], gt_m2[:, 0:1], None, ALU.is_ge)
            pr.on("vector", top2, wait=[(ctx.TC, t_gt)], inc=[(ctx.VC, 1)])
            v_t2 = ctx.VC.n
            pool.release(jg, ctx.VC, v_t2)
            pr.on("scalar", lambda e: e.activation(gt_eu[:, :], gt_sc[:, :], AF.Exp,
                                                   bias=gt_m1n[:, 0:1]),
                  wait=[(ctx.VC, v_t2)], inc=[(ctx.SC, 1)])
            s_eu = ctx.SC.n

            def gfin(e, tch=tch):
                e.tensor_tensor(gt_gk[:, :], gt_eu[:, :], gt_kp[:, :], ALU.mult)
                e.drain()
                e.reduce_sum(gt_dn[:, 0:1], gt_gk[:, :], axis=AX.X)
                e.drain()
                e.reciprocal(gt_rd[:, 0:1], gt_dn[:, 0:1])
                e.drain()
                return e.tensor_scalar(gt_out[:, tch % 4 + (tch // 4) * 4, :],
                                       gt_gk[:, :], gt_rd[:, 0:1], None, ALU.mult)
            pr.on("vector", gfin, wait=[(ctx.SC, s_eu)], inc=[(ctx.VC, 1)])
            gt_guard[0] = ctx.VC.n
        if t == T // NT - 1:
            pass
        v_u = emit_ln_fm(ctx, lnb, t2[:, :, :], xn1_16[:, :, :], u16[:, :, :],
                         NT, w_col=ln1w_sb, b_col=ln1b_sb,
                         wait0=[(ctx.VC, v_xn1), (ctx.TC, ctx.TC.n)])
        pr.on("sync", lambda e, t=t: e.dma_start(
            out=fm_tile(s_u, t * NT, NT), in_=u16[:, :, :]),
            wait=[(ctx.VC, v_u)], inc=[(ctx.ST, 16)])

        def ev_qk(e, mo, ps):
            return e.activation(qkf[:, mo, 0:NT], ps[:, 0:NT], AF.Silu,
                                bias=bqk_sb[:, mo:mo + 1])
        emit_proj_fm(ctx, wqk_sb, u16[:, :, :], NT, pool,
                     [(ctx.VC, v_u), (ctx.LG, lg_w), (ctx.LD, ld_w)], ev_qk, 2)
        s_qk = ctx.SC.n
        pr.on("sync", lambda e, t=t: e.dma_start(
            out=s_qkf[:, t * NT:(t + 1) * NT].rearrange("(c p) n -> p c n", p=128),
            in_=qkf[:, :, :]), wait=[(ctx.SC, s_qk)], inc=[(ctx.ST, 16)])

        for sub in range(4):
            for ch in range(2):
                jp = (sub * 2 + ch) % 2
                w = [(ctx.SC, s_qk)]
                if pt_free[jp][0] is not None:
                    w.append(pt_free[jp])
                pr.on("tensor", lambda e, sub=sub, ch=ch, jp=jp: e.transpose(
                    ps_t[jp][:, :], qkf[:, ch, sub * 128:(sub + 1) * 128],
                    cst["ident16"][:, :]), wait=w, inc=[(ctx.TC, 1)])
                t_t = ctx.TC.n
                pr.on("vector", lambda e, sub=sub, ch=ch, jp=jp: e.tensor_copy(
                    qkt[:, sub, ch * 128:(ch + 1) * 128], ps_t[jp][:, :]),
                    wait=[(ctx.TC, t_t)], inc=[(ctx.VC, 1)])
                pt_free[jp] = (ctx.VC, ctx.VC.n)
        v_qkt = ctx.VC.n

        def mk_lk(e, t=t):
            i = None
            for sub in range(4):
                ts_ = t * 4 + sub
                e.tensor_tensor(qg[:, 0:256], qkt[:, sub, :], g3b[:, :], ALU.mult)
                e.tensor_tensor(qh[:, 0:256], qg[:, 0:256], b3b[:, :], ALU.add)
                e.tensor_tensor(rta[:, :], qh[:, 0:128], cosT_sb[:, ts_, :], ALU.mult)
                e.tensor_tensor(rtb[:, :], qh[:, 128:256], sinT_sb[:, ts_, :], ALU.mult)
                e.tensor_tensor(lkr[:, sub, 0:128], rta[:, :], rtb[:, :], ALU.subtract)
                e.tensor_tensor(rta[:, :], qh[:, 128:256], cosT_sb[:, ts_, :], ALU.mult)
                e.tensor_tensor(rtb[:, :], qh[:, 0:128], sinT_sb[:, ts_, :], ALU.mult)
                i = e.tensor_tensor(lkr[:, sub, 128:256], rta[:, :], rtb[:, :], ALU.add)
            return i
        pr.on("vector", mk_lk, wait=[(ctx.VC, v_qkt), (ctx.LG, lg_w)], inc=[(ctx.VC, 1)])
        v_lk = ctx.VC.n

        for sub in range(4):
            for g in range(4):
                j, ps, free = pool.acquire()
                w = [(ctx.VC, v_u), (ctx.LG, lg_w)]
                if free[0] is not None:
                    w.append(free)

                def mmv(e, sub=sub, g=g, ps=ps):
                    for c in range(C):
                        i = e.matmul(ps[:, 0:512], u16[:, c, sub * 128:(sub + 1) * 128],
                                     wv_sb[:, c, g * 512:(g + 1) * 512],
                                     start=(c == 0), stop=(c == C - 1))
                    return i
                pr.on("tensor", mmv, wait=w, inc=[(ctx.TC, 1)])
                t_mm = ctx.TC.n
                pr.on("vector", lambda e, sub=sub, g=g, ps=ps: e.tensor_tensor(
                    vtmp[:, 0:512], ps[:, 0:512], bvb[:, g * 512:(g + 1) * 512], ALU.add),
                    wait=[(ctx.TC, t_mm), (ctx.LD, ld_w), (ctx.SC, ctx.SC.n)],
                    inc=[(ctx.VC, 1)])
                v_vt = ctx.VC.n
                pr.on("scalar", lambda e, sub=sub, g=g: e.activation(
                    vtm[:, sub, g * 512:(g + 1) * 512], vtmp[:, 0:512], AF.Silu),
                    wait=[(ctx.VC, v_vt)], inc=[(ctx.SC, 1)])
                pool.release(j, ctx.SC, ctx.SC.n)
        s_v = ctx.SC.n
        pr.on("sync", lambda e, t=t: e.dma_start(
            out=tm_tile(s_vt, t * NT, NT), in_=vtm[:, :, :]),
            wait=[(ctx.SC, s_v)], inc=[(ctx.ST, 16)])

        for dc in range(2):
            for g in range(4):
                j, ps, free = pool.acquire()
                w = [(ctx.VC, v_lk), (ctx.SC, s_v)]
                if free[0] is not None:
                    w.append(free)

                def mml(e, dc=dc, g=g, ps=ps):
                    for sub in range(4):
                        i = e.matmul(ps[:, 0:512], lkr[:, sub, dc * 128:(dc + 1) * 128],
                                     vtm[:, sub, g * 512:(g + 1) * 512],
                                     start=(sub == 0), stop=(sub == 3))
                    return i
                pr.on("tensor", mml, wait=w, inc=[(ctx.TC, 1)])
                t_mm = ctx.TC.n
                if t == 0:
                    pr.on("vector", lambda e, dc=dc, g=g, ps=ps: e.tensor_copy(
                        lkv[:, dc, g * 512:(g + 1) * 512], ps[:, 0:512]),
                        wait=[(ctx.TC, t_mm)], inc=[(ctx.VC, 1)])
                else:
                    pr.on("vector", lambda e, dc=dc, g=g, ps=ps: e.tensor_tensor(
                        lkv[:, dc, g * 512:(g + 1) * 512],
                        lkv[:, dc, g * 512:(g + 1) * 512], ps[:, 0:512], ALU.add),
                        wait=[(ctx.TC, t_mm)], inc=[(ctx.VC, 1)])
                pool.release(j, ctx.VC, ctx.VC.n)
        buf_ok[0] = [(ctx.TC, ctx.TC.n), (ctx.VC, ctx.VC.n), (ctx.SC, ctx.SC.n),
                     (ctx.ST, ctx.ST.n)]

    pr.on("sync", lambda e: e.dma_start(
        out=s_gates[:, :], in_=gt_out[:, :, :].rearrange("p a b -> p (a b)")),
        wait=[(ctx.VC, None)], inc=[(ctx.ST, 16)])
    pr.on("vector", lambda e: e.tensor_scalar(
        lkv2[:, :, :], lkv[:, :, :], 1.0 / N, None, ALU.mult), inc=[(ctx.VC, 1)])
    v_sc = ctx.VC.n
    pr.on("gpsimd", lambda e: e.dma_start(
        out=cc_in.rearrange("(c p) o -> p c o", p=128), in_=lkv2[:, :, :]),
        wait=[(ctx.VC, v_sc)], inc=[(ctx.LG, 16)])
    lg_cc0 = ctx.LG.n
    pr.on("gpsimd", lambda e: e.collective_compute(
        "AllReduce", ALU.add, [[0, 1], [2, 3], [4, 5], [6, 7]],
        [cc_in.opt()], [cc_out.opt()]),
        wait=[(ctx.LG, lg_cc0)], inc=[(ctx.CC, 1)])
    pr.on("gpsimd", lambda e: e.dma_start(
        out=lkv2[:, :, :], in_=cc_out.rearrange("(c p) o -> p c o", p=128)),
        wait=[(ctx.CC, 1), (ctx.VC, v_sc)], inc=[(ctx.LG, 16)])
    lg_cc = ctx.LG.n
    pr.on("sync", lambda e: e.dma_start(
        out=s_lkv.rearrange("(c p) o -> p c o", p=128), in_=lkv2[:, :, :]),
        wait=[(ctx.LG, lg_cc)], inc=[(ctx.ST, 16)])


def p5(ctx):
    pr, cst = ctx.pr, ctx.consts
    s_x1 = ctx.hscratch("s_x1")
    s_u = ctx.hscratch("s_u")
    s_qkf = ctx.hscratch("s_qkf")
    s_vt = ctx.hscratch("s_vt")
    s_lkv = ctx.hscratch("s_lkv")
    s_x2a = ctx.hscratch("s_x2a")
    wgT = ctx.hin("wgT", [D, 2048])
    woT = ctx.hin("woT", [2048, D])
    bg = ctx.hin("bg_c", [128, 16])
    bo = ctx.hin("bo_c", [128, 8])
    ln2w = ctx.hin("ln2w_c", [128, 8])
    ln2b = ctx.hin("ln2b_c", [128, 8])
    ghc = ctx.hin("gh_c", [128, 8])
    bhc = ctx.hin("bh_c", [128, 8])
    b64 = ctx.hin("bias64", [64, 64])
    cosF = ctx.hin("cosF", [128, T])
    sinF = ctx.hin("sinF", [128, T])

    wg_sb = ctx.sbuf("p5_wg", [128, C, 2048], BF16)
    wo_sb = ctx.sbuf("p5_wo", [128, 16, D], BF16)
    lkv16 = ctx.sbuf("p5_lkv", [128, 2, 2048], BF16)
    cosF_sb = ctx.sbuf("p5_cosF", [128, NT], BF16)
    sinF_sb = ctx.sbuf("p5_sinF", [128, NT], BF16)
    bg_sb = ctx.sbuf("p5_bg", [128, 16], F32)
    bo_sb = ctx.sbuf("p5_bo", [128, 8], F32)
    l2w_sb = ctx.sbuf("p5_l2w", [128, 8], F32)
    l2b_sb = ctx.sbuf("p5_l2b", [128, 8], F32)
    ghc_sb = ctx.sbuf("p5_ghc", [128, 8], F32)
    bhc_sb = ctx.sbuf("p5_bhc", [128, 8], F32)
    b64_sb = ctx.sbuf("p5_b64", [64, 64], F32)
    pr.on("gpsimd", lambda e: e.dma_start(
        out=wg_sb[:, :, :], in_=wgT.rearrange("(c p) o -> p c o", p=128)), inc=[(ctx.LG, 16)])
    pr.on("gpsimd", lambda e: e.dma_start(
        out=wo_sb[:, :, :], in_=woT.rearrange("(c p) o -> p c o", p=128)), inc=[(ctx.LG, 16)])
    pr.on("gpsimd", lambda e: e.dma_start(
        out=lkv16[:, :, :], in_=s_lkv.rearrange("(c p) o -> p c o", p=128)), inc=[(ctx.LG, 16)])

    for nm, src, dst in (("bg", bg, bg_sb), ("bo", bo, bo_sb), ("l2w", ln2w, l2w_sb),
                         ("l2b", ln2b, l2b_sb), ("ghc", ghc, ghc_sb), ("bhc", bhc, bhc_sb),
                         ("b64", b64, b64_sb)):
        pr.on("sync", lambda e, src=src, dst=dst: e.dma_start(out=dst[:, :], in_=src),
              inc=[(ctx.LD, 16)])
    lg_w = ctx.LG.n
    ld_w = ctx.LD.n

    u16 = ctx.sbuf("p5_u", [128, C, NT], BF16)
    qkf = ctx.sbuf("p5_qkf", [128, 2, NT], BF16)
    x1_32 = ctx.sbuf("p5_x1", [128, C, NT], F32)
    vtm = ctx.sbuf("p5_vtm", [64, 8, 2048], BF16)
    qpre = ctx.sbuf("p5_qpre", [128, 2, NT], BF16)
    qqr = ctx.sbuf("p5_qqr", [128, 2, NT], BF16)
    qkkr = ctx.sbuf("p5_qkkr", [128, 2, NT], BF16)
    lqr = ctx.sbuf("p5_lqr", [128, 2, NT], BF16)
    rta = ctx.sbuf("p5_rta", [128, NT], BF16)
    rtb = ctx.sbuf("p5_rtb", [128, NT], BF16)
    gate_t = [ctx.sbuf(f"p5_gate{i}", [128, NT], BF16) for i in range(2)]
    ag = ctx.sbuf("p5_ag", [64, 64], F32)
    rg = ctx.sbuf("p5_rg", [64, 64], F32)
    a2_sb = ctx.sbuf("p5_a2", [64, 8 * 64], BF16)
    ob = ctx.sbuf("p5_ob", [128, 16, NT], BF16)
    at32 = ctx.sbuf("p5_at32", [128, C, NT], F32)
    x2at = ctx.sbuf("p5_x2at", [128, D], F32)
    lnb = {"row": ctx.sbuf("p5_row", [1, 4 * NT], F32),
           "t1": ctx.sbuf("p5_t1", [128, NT], F32),
           "t1b": ctx.sbuf("p5_t1b", [128, NT], F32),
           "st0": ctx.psum("p5_ps_s0", [128, NT]),
           "st1": ctx.psum("p5_ps_s1", [128, NT]),
           "bc0": ctx.psum("p5_ps_b0", [128, NT]),
           "bc1": ctx.psum("p5_ps_b1", [128, NT])}
    ps_tf = ctx.psum("p5_ps_tf", [128, 128], F32)
    pool = ProjPool(ctx, "p5_po", 2, NT)

    buf_ok = [[]]
    tf_free = [(None, 0)]
    for t in range(T // NT):
        w_guard = list(buf_ok[0])
        pr.on("sync", lambda e, t=t: e.dma_start(
            out=u16[:, :, :], in_=fm_tile(s_u, t * NT, NT)),
            wait=w_guard, inc=[(ctx.LD, 16)])
        pr.on("sync", lambda e, t=t: e.dma_start(
            out=qkf[:, :, :],
            in_=s_qkf[:, t * NT:(t + 1) * NT].rearrange("(c p) n -> p c n", p=128)),
            wait=w_guard, inc=[(ctx.LD, 16)])
        pr.on("sync", lambda e, t=t: e.dma_start(
            out=x1_32[:, :, :], in_=fm_tile(s_x1, t * NT, NT)),
            wait=w_guard, inc=[(ctx.LD, 16)])
        pr.on("sync", lambda e, t=t: e.dma_start(
            out=vtm[:, :, :],
            in_=s_vt[t * NT:(t + 1) * NT, :].rearrange("(s p) d -> p s d", p=64)),
            wait=w_guard, inc=[(ctx.LD, 16)])
        pr.on("gpsimd", lambda e, t=t: e.dma_start(
            out=cosF_sb[:, :], in_=cosF[:, t * NT:(t + 1) * NT]),
            wait=w_guard, inc=[(ctx.LG, 16)])
        pr.on("gpsimd", lambda e, t=t: e.dma_start(
            out=sinF_sb[:, :], in_=sinF[:, t * NT:(t + 1) * NT]),
            wait=w_guard, inc=[(ctx.LG, 16)])
        ld_t = ctx.LD.n
        lg_t = ctx.LG.n

        def mk_heads(e, t=t):
            i = None
            for head, dst in ((0, qqr), (2, qkkr), (1, lqr)):
                for ch in range(2):
                    e.tensor_scalar(qpre[:, ch, 0:NT], qkf[:, ch, 0:NT],
                                    ghc_sb[:, head * 2 + ch:head * 2 + ch + 1],
                                    bhc_sb[:, head * 2 + ch:head * 2 + ch + 1],
                                    ALU.mult, ALU.add)
                cs = cosF_sb[:, 0:NT]
                sn = sinF_sb[:, 0:NT]
                e.tensor_tensor(rta[:, 0:NT], qpre[:, 0, 0:NT], cs, ALU.mult)
                e.tensor_tensor(rtb[:, 0:NT], qpre[:, 1, 0:NT], sn, ALU.mult)
                e.tensor_tensor(dst[:, 0, 0:NT], rta[:, 0:NT], rtb[:, 0:NT], ALU.subtract)
                e.tensor_tensor(rta[:, 0:NT], qpre[:, 1, 0:NT], cs, ALU.mult)
                e.tensor_tensor(rtb[:, 0:NT], qpre[:, 0, 0:NT], sn, ALU.mult)
                i = e.tensor_tensor(dst[:, 1, 0:NT], rta[:, 0:NT], rtb[:, 0:NT], ALU.add)
            return i
        pr.on("vector", mk_heads, wait=[(ctx.LD, ld_t), (ctx.LG, lg_t),
                                        (ctx.TC, ctx.TC.n)], inc=[(ctx.VC, 1)])
        v_heads = ctx.VC.n

        for g in range(8):
            j, ps, free = pool.acquire()
            w = [(ctx.VC, v_heads)]
            if free[0] is not None:
                w.append(free)

            def mmq(e, g=g, ps=ps):
                for dc in range(2):
                    i = e.matmul(ps[0:64, 0:64], qkkr[:, dc, g * 64:(g + 1) * 64],
                                 qqr[:, dc, g * 64:(g + 1) * 64],
                                 start=(dc == 0), stop=(dc == 1))
                return i
            pr.on("tensor", mmq, wait=w, inc=[(ctx.TC, 1)])
            t_q = ctx.TC.n

            def ev_q(e, g=g, ps=ps):
                e.scalar_tensor_tensor(ag[0:64, 0:64], ps[0:64, 0:64], 1.0 / 64.0,
                                       b64_sb[0:64, 0:64], ALU.mult, ALU.add)
                e.drain()
                e.tensor_relu(rg[0:64, 0:64], ag[0:64, 0:64])
                e.drain()
                return e.tensor_tensor(a2_sb[0:64, g * 64:(g + 1) * 64],
                                       rg[0:64, 0:64], rg[0:64, 0:64], ALU.mult)
            pr.on("vector", ev_q, wait=[(ctx.TC, t_q), (ctx.LD, ld_w)], inc=[(ctx.VC, 1)])
            pool.release(j, ctx.VC, ctx.VC.n)
        v_a2 = ctx.VC.n

        gate_free = [(None, 0), (None, 0)]
        for ec in range(16):
            gp = ec % 2
            jg, psg, freeg = pool.acquire()
            wg = [(ctx.LD, ld_t), (ctx.LG, lg_w)]
            if freeg[0] is not None:
                wg.append(freeg)

            def mmg(e, ec=ec, psg=psg):
                for c in range(C):
                    i = e.matmul(psg[:, 0:NT], wg_sb[:, c, ec * 128:(ec + 1) * 128],
                                 u16[:, c, 0:NT], start=(c == 0), stop=(c == C - 1))
                return i
            pr.on("tensor", mmg, wait=wg, inc=[(ctx.TC, 1)])
            t_g = ctx.TC.n
            wgv = [(ctx.TC, t_g)]
            if gate_free[gp][0] is not None:
                wgv.append(gate_free[gp])
            pr.on("scalar", lambda e, ec=ec, psg=psg, gp=gp: e.activation(
                gate_t[gp][:, 0:NT], psg[:, 0:NT], AF.Silu, bias=bg_sb[:, ec:ec + 1]),
                wait=wgv, inc=[(ctx.SC, 1)])
            pool.release(jg, ctx.SC, ctx.SC.n)
            s_g = ctx.SC.n

            jo, pso, freeo = pool.acquire()
            w = [(ctx.VC, v_a2), (ctx.LG, lg_w)]
            if freeo[0] is not None:
                w.append(freeo)

            def mmo(e, ec=ec, ps=pso):
                for dc in range(2):
                    e.matmul(ps[:, 0:NT], lkv16[:, dc, ec * 128:(ec + 1) * 128],
                             lqr[:, dc, 0:NT], start=(dc == 0), stop=False)
                for g in range(8):
                    i = e.matmul(ps[:, g * 64:(g + 1) * 64],
                                 vtm[0:64, g, ec * 128:(ec + 1) * 128],
                                 a2_sb[0:64, g * 64:(g + 1) * 64],
                                 start=False, stop=(g == 7))
                return i
            pr.on("tensor", mmo, wait=w, inc=[(ctx.TC, 1)])
            t_o = ctx.TC.n
            pr.on("vector", lambda e, ec=ec, ps=pso, gp=gp: e.scalar_tensor_tensor(
                ob[:, ec, 0:NT], ps[:, 0:NT], 1.0, gate_t[gp][:, 0:NT],
                ALU.mult, ALU.mult),
                wait=[(ctx.TC, t_o), (ctx.SC, s_g)], inc=[(ctx.VC, 1)])
            gate_free[gp] = (ctx.VC, ctx.VC.n)
            pool.release(jo, ctx.VC, ctx.VC.n)
        v_ob = ctx.VC.n

        def ev_wo(e, mo, ps):
            return e.tensor_scalar(at32[:, mo, 0:NT], ps[:, 0:NT],
                                   bo_sb[:, mo:mo + 1], None, ALU.add)
        emit_proj_fm(ctx, wo_sb, ob[:, :, :], NT, pool, [(ctx.VC, v_ob)], ev_wo, C,
                     evac_eng="vector")
        s_at = ctx.VC.n

        v_x2 = emit_ln_fm(ctx, lnb, at32[:, :, :], None, at32[:, :, :],
                          NT, w_col=l2w_sb, b_col=l2b_sb, add_in=x1_32[:, :, :],
                          wait0=[(ctx.VC, s_at), (ctx.TC, ctx.TC.n)], stats_fp32=True)

        for sub in range(4):
            st_prev = ctx.ST.n
            for c in range(C):
                w = [(ctx.VC, v_x2)]
                if tf_free[0][0] is not None:
                    w.append(tf_free[0])
                pr.on("tensor", lambda e, sub=sub, c=c: e.transpose(
                    ps_tf[:, :], at32[:, c, sub * 128:(sub + 1) * 128],
                    cst["ident32"][:, :]), wait=w, inc=[(ctx.TC, 1)])
                t_t = ctx.TC.n
                pr.on("vector", lambda e, sub=sub, c=c: e.tensor_copy(
                    x2at[:, c * 128:(c + 1) * 128], ps_tf[:, :]),
                    wait=[(ctx.TC, t_t), (ctx.ST, st_prev)], inc=[(ctx.VC, 1)])
                tf_free[0] = (ctx.VC, ctx.VC.n)
            v_tr = ctx.VC.n
            pr.on("sync", lambda e, t=t, sub=sub: e.dma_start(
                out=s_x2a[t * NT + sub * 128:t * NT + (sub + 1) * 128, :],
                in_=x2at[:, :]), wait=[(ctx.VC, v_tr)], inc=[(ctx.ST, 16)])
        buf_ok[0] = [(ctx.TC, ctx.TC.n), (ctx.VC, ctx.VC.n), (ctx.SC, ctx.SC.n),
                     (ctx.ST, ctx.ST.n)]


def p6(ctx):
    pr, cst = ctx.pr, ctx.consts
    s_xn1 = ctx.hscratch("s_xn1")
    s_x2a = ctx.hscratch("s_x2a")
    s_gates = ctx.hscratch("s_gates")
    out = ctx.out
    w1T = ctx.hin("w1T", [8, D, 2048])
    w2T = ctx.hin("w2T", [8, 2048, D])
    w3T = ctx.hin("w3T", [8, D, 2048])
    a1T = ctx.hin("a1T", [8, D, 16])
    a2T = ctx.hin("a2T", [8, 2048, 16])
    a3T = ctx.hin("a3T", [8, D, 16])
    b1l = ctx.hin("b1l", [8, 16, 2048])
    b2l = ctx.hin("b2l", [8, 16, D])
    b3l = ctx.hin("b3l", [8, 16, 2048])
    bm1 = ctx.hin("bm1_c", [8, 128, 16])
    bm2r = ctx.hin("bm2_row", [8, D])
    bm3 = ctx.hin("bm3_c", [8, 128, 16])

    xn1_sb = ctx.sbuf("p6_xn1", [128, C, T], BF16)
    gates = ctx.sbuf("p6_gates", [128, 16, 8], F32)
    pr.on("gpsimd", lambda e: e.dma_start(
        out=xn1_sb[:, :, :], in_=s_xn1.rearrange("(c p) n -> p c n", p=128)),
        inc=[(ctx.LG, 16)])
    pr.on("sync", lambda e: e.dma_start(
        out=gates[:, :, :].rearrange("p a b -> p (a b)"), in_=s_gates[:, :]),
        inc=[(ctx.LD, 16)])
    lg_0 = ctx.LG.n
    ld_g = ctx.LD.n

    w1e = ctx.sbuf("p6_w1e", [128, C, 2048], BF16)
    w3e = ctx.sbuf("p6_w3e", [128, C, 2048], BF16)
    w2e = ctx.sbuf("p6_w2e", [128, 16, D], BF16)
    a1e = ctx.sbuf("p6_a1e", [128, C, 16], BF16)
    a3e = ctx.sbuf("p6_a3e", [128, C, 16], BF16)
    a2e = ctx.sbuf("p6_a2e", [128, 16, 16], BF16)
    b1le = ctx.sbuf("p6_b1le", [16, 2048], BF16)
    b3le = ctx.sbuf("p6_b3le", [16, 2048], BF16)
    b2le = ctx.sbuf("p6_b2le", [16, D], BF16)
    bm1e = ctx.sbuf("p6_bm1e", [128, 16], F32)
    bm3e = ctx.sbuf("p6_bm3e", [128, 16], F32)
    bm2b = ctx.sbuf("p6_bm2b", [128, D], F32)
    al1 = ctx.sbuf("p6_al1", [16, 2048], BF16)
    al3 = ctx.sbuf("p6_al3", [16, 2048], BF16)
    al2 = ctx.sbuf("p6_al2", [16, 1024], BF16)
    glu = ctx.sbuf("p6_glu", [128, 16, 1024], BF16)
    s1t = [ctx.sbuf(f"p6_s1t{i}", [128, 512], BF16) for i in range(2)]
    h3t = [ctx.sbuf(f"p6_h3t{i}", [128, 512], BF16) for i in range(2)]
    ytmp = ctx.sbuf("p6_ytmp", [128, D], F32)
    x2t = [ctx.sbuf(f"p6_x2t{i}", [128, D], F32) for i in range(2)]
    yout = ctx.sbuf("p6_yout", [128, D], F32)
    ps_al = [ctx.psum(f"p6_ps_al{i}", [16, 512], F32) for i in range(2)]
    ps_y = ctx.psum("p6_ps_y", [128, D], F32)
    pool = ProjPool(ctx, "p6_po", 3, 512)

    lg13 = [0] * 8
    lga = [0] * 8
    lg2 = [0] * 8
    ldb13 = [0] * 8
    ldb2 = [0] * 8

    def load_w13(ex, wait):
        for dst, src in ((w1e, w1T), (w3e, w3T)):
            pr.on("gpsimd", lambda e, dst=dst, src=src, ex=ex: e.dma_start(
                out=dst[:, :, :], in_=src[ex].rearrange("(c p) o -> p c o", p=128)),
                wait=wait, inc=[(ctx.LG, 16)])
        for dst, src in ((b1le, b1l), (b3le, b3l)):
            pr.on("gpsimd", lambda e, dst=dst, src=src, ex=ex: e.dma_start(
                out=dst[:, :], in_=src[ex]), wait=wait, inc=[(ctx.LG, 16)])
        lg13[ex] = ctx.LG.n
        for dst, src in ((bm1e, bm1), (bm3e, bm3)):
            pr.on("sync", lambda e, dst=dst, src=src, ex=ex: e.dma_start(
                out=dst[:, :], in_=src[ex]), wait=wait, inc=[(ctx.LD, 16)])
        ldb13[ex] = ctx.LD.n

    def load_a(ex, wait):
        for dst, src in ((a1e, a1T), (a3e, a3T)):
            pr.on("gpsimd", lambda e, dst=dst, src=src, ex=ex: e.dma_start(
                out=dst[:, :, :], in_=src[ex].rearrange("(c p) o -> p c o", p=128)),
                wait=wait, inc=[(ctx.LG, 16)])
        lga[ex] = ctx.LG.n

    def load_w2(ex, wait):
        pr.on("gpsimd", lambda e, ex=ex: e.dma_start(
            out=w2e[:, :, :], in_=w2T[ex].rearrange("(c p) o -> p c o", p=128)),
            wait=wait, inc=[(ctx.LG, 16)])
        pr.on("gpsimd", lambda e, ex=ex: e.dma_start(
            out=a2e[:, :, :], in_=a2T[ex].rearrange("(c p) o -> p c o", p=128)),
            wait=wait, inc=[(ctx.LG, 16)])
        pr.on("gpsimd", lambda e, ex=ex: e.dma_start(
            out=b2le[:, :], in_=b2l[ex]), wait=wait, inc=[(ctx.LG, 16)])
        lg2[ex] = ctx.LG.n
        pr.on("sync", lambda e, ex=ex: e.dma_start(
            out=bm2b[:, :], in_=bm2r[ex:ex + 1, :].partition_broadcast(128)),
            wait=wait, inc=[(ctx.LD, 16)])
        ldb2[ex] = ctx.LD.n

    load_w13(0, [])
    load_a(0, [])
    load_w2(0, [])

    x2_guard = [[]]
    al_free = [(None, 0), (None, 0)]
    for ex in range(8):
        for i4, (asrc, dst, tg) in enumerate(
                [(a, d, tg) for (a, d) in ((a1e, al1), (a3e, al3)) for tg in range(4)]):
            jb = i4 % 2
            off = tg * 512

            def mma(e, asrc=asrc, off=off, jb=jb):
                for c in range(C):
                    i = e.matmul(ps_al[jb][0:16, 0:512], asrc[:, c, 0:16],
                                 xn1_sb[:, c, off:off + 512],
                                 start=(c == 0), stop=(c == C - 1))
                return i
            wA = [(ctx.LG, max(lga[ex], lg_0))]
            if al_free[jb][0] is not None:
                wA.append(al_free[jb])
            if i4 == 0:
                wA.append((ctx.TC, ctx.TC.n))
            pr.on("tensor", mma, wait=wA, inc=[(ctx.TC, 1)])
            t_a = ctx.TC.n
            pr.on("scalar", lambda e, dst=dst, tg=tg, jb=jb: e.activation(
                dst[0:16, tg * 512:(tg + 1) * 512], ps_al[jb][0:16, 0:512], AF.Copy),
                wait=[(ctx.TC, t_a)], inc=[(ctx.SC, 1)])
            al_free[jb] = (ctx.SC, ctx.SC.n)
        s_al = ctx.SC.n
        if ex < 7:
            load_a(ex + 1, [(ctx.TC, ctx.TC.n)])
        for hf in range(2):
            base = hf * 1024

            glu_guard = (ctx.TC, ctx.TC.n)
            for hc in range(16):
                for tg in range(2):
                    off = base + tg * 512
                    p2 = (hc * 2 + tg) % 2
                    j1, ps1, free1 = pool.acquire()
                    w1_ = [(ctx.SC, s_al), (ctx.LG, lg13[ex]), (ctx.LD, ldb13[ex])]
                    if free1[0] is not None:
                        w1_.append(free1)

                    def mm1(e, hc=hc, off=off, tg=tg, ps=ps1):
                        for c in range(C):
                            e.matmul(ps[:, 0:512], w1e[:, c, hc * 128:(hc + 1) * 128],
                                     xn1_sb[:, c, off:off + 512], start=(c == 0), stop=False)
                        return e.matmul(ps[:, 0:512], b1le[0:16, hc * 128:(hc + 1) * 128],
                                        al1[0:16, off:off + 512],
                                        start=False, stop=True)
                    pr.on("tensor", mm1, wait=w1_, inc=[(ctx.TC, 1)])
                    t_1 = ctx.TC.n
                    pr.on("scalar", lambda e, hc=hc, p2=p2, ps=ps1: e.activation(
                        s1t[p2][:, :], ps[:, 0:512], AF.Silu, bias=bm1e[:, hc:hc + 1]),
                        wait=[(ctx.TC, t_1), (ctx.VC, ctx.VC.n)], inc=[(ctx.SC, 1)])
                    pool.release(j1, ctx.SC, ctx.SC.n)
                    s_1 = ctx.SC.n

                    j3, ps3, free3 = pool.acquire()
                    w3_ = [(ctx.SC, s_al), (ctx.LG, lg13[ex]), (ctx.LD, ldb13[ex])]
                    if free3[0] is not None:
                        w3_.append(free3)

                    def mm3(e, hc=hc, off=off, tg=tg, ps=ps3):
                        for c in range(C):
                            e.matmul(ps[:, 0:512], w3e[:, c, hc * 128:(hc + 1) * 128],
                                     xn1_sb[:, c, off:off + 512], start=(c == 0), stop=False)
                        return e.matmul(ps[:, 0:512], b3le[0:16, hc * 128:(hc + 1) * 128],
                                        al3[0:16, off:off + 512],
                                        start=False, stop=True)
                    pr.on("tensor", mm3, wait=w3_, inc=[(ctx.TC, 1)])
                    t_3 = ctx.TC.n
                    pr.on("vector", lambda e, hc=hc, p2=p2, ps=ps3: e.tensor_scalar(
                        h3t[p2][:, :], ps[:, 0:512], bm3e[:, hc:hc + 1], None, ALU.add),
                        wait=[(ctx.TC, t_3), (ctx.SC, ctx.SC.n)], inc=[(ctx.VC, 1)])
                    pool.release(j3, ctx.VC, ctx.VC.n)
                    v_3 = ctx.VC.n
                    pr.on("vector", lambda e, hc=hc, tg=tg, p2=p2: e.tensor_tensor(
                        glu[:, hc, tg * 512:(tg + 1) * 512], s1t[p2][:, :],
                        h3t[p2][:, :], ALU.mult),
                        wait=[(ctx.VC, v_3), (ctx.SC, s_1), glu_guard], inc=[(ctx.VC, 1)])
            v_glu = ctx.VC.n
            if hf == 1 and ex < 7:
                load_w13(ex + 1, [(ctx.TC, ctx.TC.n), (ctx.SC, ctx.SC.n),
                                  (ctx.VC, ctx.VC.n)])

            for tg in range(2):
                jb = tg % 2

                def mma2(e, tg=tg, jb=jb):
                    for hc in range(16):
                        i = e.matmul(ps_al[jb][0:16, 0:512], a2e[:, hc, 0:16],
                                     glu[:, hc, tg * 512:(tg + 1) * 512],
                                     start=(hc == 0), stop=(hc == 15))
                    return i
                wC = [(ctx.VC, v_glu), (ctx.LG, lg2[ex])]
                if al_free[jb][0] is not None:
                    wC.append(al_free[jb])
                pr.on("tensor", mma2, wait=wC, inc=[(ctx.TC, 1)])
                t_a2 = ctx.TC.n
                pr.on("scalar", lambda e, tg=tg, jb=jb: e.activation(
                    al2[0:16, tg * 512:(tg + 1) * 512], ps_al[jb][0:16, 0:512], AF.Copy),
                    wait=[(ctx.TC, t_a2)], inc=[(ctx.SC, 1)])
                al_free[jb] = (ctx.SC, ctx.SC.n)
            s_al2 = ctx.SC.n

            x2t_guard = [(None, 0), (None, 0)]
            for tc in range(8):
                gch = hf * 8 + tc
                p2 = tc % 2
                gbase = base + tc * 128
                wx = [(ctx.ST, None)] + x2_guard[0]
                if x2t_guard[p2][0] is not None:
                    wx.append(x2t_guard[p2])
                pr.on("sync", lambda e, gbase=gbase, p2=p2: e.dma_start(
                    out=x2t[p2][:, :], in_=s_x2a[gbase:gbase + 128, :]),
                    wait=wx, inc=[(ctx.LD, 16)])
                ld_x2 = ctx.LD.n

                def mmy(e, tc=tc):
                    i = None
                    for ng in range(2):
                        for hc in range(16):
                            e.matmul(ps_y[:, ng * 512:(ng + 1) * 512],
                                     glu[:, hc, tc * 128:(tc + 1) * 128],
                                     w2e[:, hc, ng * 512:(ng + 1) * 512],
                                     start=(hc == 0), stop=False)
                        i = e.matmul(ps_y[:, ng * 512:(ng + 1) * 512],
                                     al2[0:16, tc * 128:(tc + 1) * 128],
                                     b2le[0:16, ng * 512:(ng + 1) * 512],
                                     start=False, stop=True)
                    return i
                pr.on("tensor", mmy, wait=[(ctx.SC, s_al2), (ctx.VC, ctx.VC.n),
                                           (ctx.LG, lg2[ex]), (ctx.LD, ldb2[ex])],
                      inc=[(ctx.TC, 1)])
                t_y = ctx.TC.n

                def ev_y(e, gch=gch, p2=p2, ex=ex):
                    e.tensor_tensor(ytmp[:, :], ps_y[:, 0:D], bm2b[:, :], ALU.add)
                    return e.scalar_tensor_tensor(
                        yout[:, :], ytmp[:, :], gates[:, gch, ex:ex + 1],
                        x2t[p2][:, :], ALU.mult, ALU.add)
                pr.on("vector", ev_y, wait=[(ctx.TC, t_y), (ctx.LD, max(ld_x2, ld_g)),
                                            (ctx.ST, None)],
                      inc=[(ctx.VC, 1)])
                v_y = ctx.VC.n
                x2t_guard[p2] = (ctx.VC, v_y)
                dst = out if ex == 7 else s_x2a
                pr.on("sync", lambda e, gbase=gbase, dst=dst: e.dma_start(
                    out=dst[gbase:gbase + 128, :], in_=yout[:, :]),
                    wait=[(ctx.VC, v_y)], inc=[(ctx.ST, 16)])
            x2_guard[0] = [(ctx.VC, ctx.VC.n)]
            if hf == 1 and ex < 7:
                load_w2(ex + 1, [(ctx.TC, ctx.TC.n), (ctx.VC, ctx.VC.n)])


PHASE_EMITTERS = {"p0": p0, "p1": p1, "p2": p2, "p3": p3, "p4": p4, "p5": p5, "p6": p6}



LAUNCH_GROUPS = [["p0", "p1", "p2", "p3", "p4", "p5", "p6"]]
NCORES = 8


def kernel(**inputs):
    per_core = prep_inputs(inputs)
    env = [dict(d) for d in per_core]
    for grp in LAUNCH_GROUPS:
        nc, in_names = build_program(grp)
        in_maps = [{k: np.ascontiguousarray(env[c][k]) for k in in_names}
                   for c in range(NCORES)]
        res = run_bass_kernel_spmd(nc, in_maps, core_ids=list(range(NCORES)))
        for c in range(NCORES):
            for k, v in res.results[c].items():
                env[c][k] = v
    out = np.zeros((B, N, D), np.float32)
    for c in range(NCORES):
        b, h = c // 2, c % 2
        out[b, own_token_idx(h), :] = env[c]["out_x2"]
    return out


# version 05
# speedup vs baseline: 1.0497x; 1.0497x over previous
import contextlib
import sys
if "/opt/trn_rl_repo" not in sys.path:
    sys.path.insert(0, "/opt/trn_rl_repo")
import numpy as np
import ml_dtypes
import concourse.bass as bass
import concourse.mybir as mybir
from concourse.bass_utils import run_bass_kernel_spmd

F32 = mybir.dt.float32
BF16 = mybir.dt.bfloat16
AF = mybir.ActivationFunctionType
ALU = mybir.AluOpType
AX = mybir.AxisListType

B, N, D = 4, 4096, 1024
T = 2048
NT = 512
C = D // 128
NT3 = 256
NEG = -1.0e9
EPS = 1e-5
BF = ml_dtypes.bfloat16


def _cols(v):
    v = np.asarray(v, np.float32)
    return np.ascontiguousarray(v.reshape(-1, 128).T)


def own_token_idx(h):
    return np.concatenate([np.arange(128) + 128 * (2 * i + h) for i in range(16)])


def prep_inputs(inp):
    f = {k: np.asarray(v, np.float32) for k, v in inp.items()}
    x = f["L0_attn_x"]

    sh = {}
    sh["wq"] = f["L0_attn_Wq"]
    sh["wk"] = f["L0_attn_Wk"]
    sh["wv"] = f["L0_attn_Wv"]
    sh["wfc"] = f["L0_attn_Wfc"]
    sh["bfc_c"] = _cols(f["L0_attn_bfc"])
    sh["g0_row"] = np.ascontiguousarray(f["L0_attn_gamma"][None, :])
    sh["b0_row"] = np.ascontiguousarray(f["L0_attn_beta"][None, :])
    sh["w1"] = f["L0_ffn_W1"]
    sh["b1_c"] = _cols(f["L0_ffn_b1"])
    sh["w2"] = f["L0_ffn_W2"]
    sh["b2_c"] = _cols(f["L0_ffn_b2"])
    sh["ln1w_c"] = _cols(f["L1_attn_ln1_w"])
    sh["ln1b_c"] = _cols(f["L1_attn_ln1_b"])
    sh["wqkT"] = np.ascontiguousarray(f["L1_attn_Wqk"].T)
    sh["bqk_c"] = _cols(f["L1_attn_bqk"])
    sh["wvT1"] = np.ascontiguousarray(f["L1_attn_Wv"].T)
    sh["bv_row"] = np.ascontiguousarray(f["L1_attn_bv"][None, :])
    sh["wgT"] = np.ascontiguousarray(f["L1_attn_Wg"].T)
    sh["bg_row"] = np.ascontiguousarray(f["L1_attn_bg"][None, :])
    sh["bg_c"] = _cols(f["L1_attn_bg"])
    sh["woT"] = np.ascontiguousarray(f["L1_attn_Wo"].T)
    sh["bo_c"] = _cols(f["L1_attn_bo"])
    sh["ln2w_c"] = _cols(f["L1_attn_ln2_w"])
    sh["ln2b_c"] = _cols(f["L1_attn_ln2_b"])
    gam, bet = f["L1_attn_gamma"], f["L1_attn_beta"]
    sh["gh_c"] = np.ascontiguousarray(gam.reshape(4, 2, 128).transpose(2, 0, 1).reshape(128, 8))
    sh["bh_c"] = np.ascontiguousarray(bet.reshape(4, 2, 128).transpose(2, 0, 1).reshape(128, 8))
    sh["g3_row"] = np.ascontiguousarray(gam[3][None, :])
    sh["b3_row"] = np.ascontiguousarray(bet[3][None, :])
    idx = np.arange(64)
    dist = np.clip(np.abs(idx[:, None] - idx[None, :]), 0, 32)
    sh["bias64"] = np.ascontiguousarray(f["L1_attn_pos_emb"][dist]).astype(np.float32)
    sh["wgmT"] = np.ascontiguousarray(f["L1_ffn_Wg"].T)
    sh["w1T"] = np.ascontiguousarray(f["L1_ffn_W1"].transpose(0, 2, 1))
    sh["w2T"] = np.ascontiguousarray(f["L1_ffn_W2"].transpose(0, 2, 1))
    sh["w3T"] = np.ascontiguousarray(f["L1_ffn_W3"].transpose(0, 2, 1))
    sh["a1T"] = np.ascontiguousarray(f["L1_ffn_A1"].transpose(0, 2, 1))
    sh["a2T"] = np.ascontiguousarray(f["L1_ffn_A2"].transpose(0, 2, 1))
    sh["a3T"] = np.ascontiguousarray(f["L1_ffn_A3"].transpose(0, 2, 1))
    sh["b1l"] = np.ascontiguousarray(f["L1_ffn_B1"].transpose(0, 2, 1))
    sh["b2l"] = np.ascontiguousarray(f["L1_ffn_B2"].transpose(0, 2, 1))
    sh["b3l"] = np.ascontiguousarray(f["L1_ffn_B3"].transpose(0, 2, 1))
    sh["bm1_c"] = np.ascontiguousarray(np.stack([_cols(v) for v in f["L1_ffn_b1"]]))
    sh["bm2_c"] = np.ascontiguousarray(np.stack([_cols(v) for v in f["L1_ffn_b2"]]))
    sh["bm3_c"] = np.ascontiguousarray(np.stack([_cols(v) for v in f["L1_ffn_b3"]]))
    sh["bm2_row"] = np.ascontiguousarray(f["L1_ffn_b2"])
    sh["onesf"] = np.ones((1, 128), np.float32)
    sh["ident"] = np.eye(128, dtype=np.float32)

    half = 128
    inv_freq = (10000.0 ** (-np.arange(half, dtype=np.float32) / half)).astype(np.float32)

    ins = []
    for c in range(8):
        b, h = c // 2, c % 2
        d = dict(sh)
        oi = own_token_idx(h)
        xt = np.ascontiguousarray(x[b].T)
        d["xT_all"] = xt
        d["xT_own"] = np.ascontiguousarray(xt[:, oi])
        masks = np.zeros((16, 2, 128, 128), np.float32)
        vsel = np.zeros((16, 2, 128, 128), np.float32)
        for i in range(16):
            g = 2 * i + h
            for jj, j in enumerate((2 * i, 2 * i + 1)):
                kk = np.arange(128)[None, :] + 128 * j
                qq = np.arange(128)[:, None] + 128 * g
                masks[i, jj] = np.where(kk <= qq, 0.0, NEG)
                if j == g:
                    vsel[i, jj] = np.eye(128, dtype=np.float32)
        d["masks"] = masks
        d["vsel"] = vsel
        pos = oi.astype(np.float32)
        sinu = pos[:, None] * inv_freq[None, :]
        d["cosF"] = np.ascontiguousarray(np.cos(sinu).T.astype(np.float32))
        d["sinF"] = np.ascontiguousarray(np.sin(sinu).T.astype(np.float32))
        d["cosT"] = np.ascontiguousarray(np.cos(sinu).astype(np.float32))
        d["sinT"] = np.ascontiguousarray(np.sin(sinu).astype(np.float32))
        ins.append(d)
    return ins



ENGS = ("sync", "tensor", "vector", "scalar", "gpsimd")


class SemC:
    __slots__ = ("h", "n", "name")
    def __init__(self, h, name):
        self.h, self.n, self.name = h, 0, name


class Prog:
    def __init__(self, nc):
        self.nc = nc
        self.ops = {e: [] for e in ENGS}

    def sem(self, name):
        return SemC(self.nc.alloc_semaphore(name), name)

    def on(self, eng, fn, wait=(), inc=()):
        w = [(s, (s.n if v is None else v)) for (s, v) in wait]
        self.ops[eng].append((w, fn, list(inc)))
        for s, k in inc:
            s.n += k

    def emit(self, block):
        for ename in ENGS:
            ops = self.ops[ename]
            def section(eng, ops=ops):
                last_wait = {}
                for (waits, fn, incs) in ops:
                    for s, v in waits:
                        if v > 0 and last_wait.get(id(s), 0) < v:
                            eng.wait_ge(s.h, v)
                            last_wait[id(s)] = v
                    instr = fn(eng) if fn is not None else None
                    if incs:
                        if instr is None:
                            instr = eng.nop()
                        for s, k in incs:
                            instr.then_inc(s.h, k)
            getattr(block, ename)(section)


class Ctx:
    def __init__(self, nc, pr, hin, hscratch, out):
        self.nc, self.pr, self.hin, self.hscratch, self.out = nc, pr, hin, hscratch, out
        self.stack = contextlib.ExitStack()
        self.pstack = None
        self.LD = pr.sem("LD")
        self.LG = pr.sem("LG")
        self.ST = pr.sem("ST")
        self.TC = pr.sem("TC")
        self.VC = pr.sem("VC")
        self.SC = pr.sem("SC")
        self.CC = pr.sem("CC")
        self.consts = {}

    def sbuf(self, name, shape, dtype):
        st = self.pstack if self.pstack is not None else self.stack
        return st.enter_context(self.nc.sbuf_tensor(name, list(shape), dtype))

    def psum(self, name, shape, dtype=F32):
        st = self.pstack if self.pstack is not None else self.stack
        return st.enter_context(self.nc.psum_tensor(name, list(shape), dtype))

    def barrier(self):
        sems = [self.LD, self.LG, self.ST, self.TC, self.VC, self.SC, self.CC]
        w = [(s, None) for s in sems]
        for e in ENGS:
            self.pr.on(e, None, wait=w)


def fm_tile(hbm_ap, t0, nt):
    return hbm_ap[:, t0:t0 + nt].rearrange("(c p) n -> p c n", p=128)


def tm_tile(hbm_ap, t0, nt):
    return hbm_ap[t0:t0 + nt, :].rearrange("(s p) d -> p s d", p=128)



def emit_ln_fm(ctx, bufs, x32v, x16v, xn16v, nt, w_col=None, b_col=None, t2_out=None,
               wait0=(), add_in=None, stats_fp32=False):
    pr, cst = ctx.pr, ctx.consts
    nC = x32v.shape[1]
    inv = 1.0 / (128 * nC)
    sq, rowb, t1 = bufs.get("sq"), bufs["row"], bufs["t1"]
    t1b = bufs.get("t1b", t1)
    row = lambda k: rowb[0:1, k * nt:(k + 1) * nt]
    ps_m, ps_v, ps_mb, ps_rb = bufs["st0"], bufs["st1"], bufs["bc0"], bufs["bc1"]

    if stats_fp32:
        base_w = list(wait0) + [(ctx.LD, None), (ctx.LG, None)]
        def mm_x(e):
            for c in range(nC):
                i = e.matmul(ps_m[0:1, 0:nt], cst["ones32c"][:, 0:1], x32v[:, c, :],
                             start=(c == 0), stop=(c == nC - 1))
            return i
        pr.on("tensor", mm_x, wait=base_w, inc=[(ctx.TC, 1)])
        for c in range(nC):
            pr.on("vector", lambda e, c=c: e.tensor_tensor(
                t1b[:, 0:nt], x32v[:, c, :], x32v[:, c, :], ALU.mult),
                wait=base_w + [(ctx.TC, ctx.TC.n)], inc=[(ctx.VC, 1)])
            v_c = ctx.VC.n
            pr.on("tensor", lambda e, c=c: e.matmul(
                ps_v[0:1, 0:nt], cst["ones32c"][:, 0:1], t1b[:, 0:nt],
                start=(c == 0), stop=(c == nC - 1)), wait=[(ctx.VC, v_c)],
                inc=[(ctx.TC, 1)])
        t_st = ctx.TC.n
    else:
        pr.on("vector", lambda e: e.tensor_tensor(sq[:, 0:nC, 0:nt], x32v, x32v, ALU.mult),
              wait=list(wait0) + [(ctx.LD, None), (ctx.LG, None)], inc=[(ctx.VC, 1)])
        v_sq = ctx.VC.n

        def mm_stats(e):
            for c in range(nC):
                e.matmul(ps_m[0:1, 0:nt], cst["ones16c"][:, 0:1], x16v[:, c, :],
                         start=(c == 0), stop=(c == nC - 1))
            for c in range(nC):
                i = e.matmul(ps_v[0:1, 0:nt], cst["ones16c"][:, 0:1], sq[:, c, 0:nt],
                             start=(c == 0), stop=(c == nC - 1))
            return i
        pr.on("tensor", mm_stats, wait=[(ctx.VC, v_sq), (ctx.LD, None), (ctx.LG, None)],
              inc=[(ctx.TC, 1)])
        t_st = ctx.TC.n

    def row_math(e):
        e.tensor_scalar(row(0), ps_m[0:1, 0:nt], inv, None, ALU.mult)
        e.tensor_tensor(row(3), row(0), row(0), ALU.mult)
        e.scalar_tensor_tensor(row(1), ps_v[0:1, 0:nt], inv,
                               row(3), ALU.mult, ALU.subtract)
        return e.tensor_scalar(row(3), row(1), EPS, None, ALU.add)
    pr.on("vector", row_math, wait=[(ctx.TC, t_st)], inc=[(ctx.VC, 1)])
    v_row = ctx.VC.n

    def row_rstd(e):
        e.activation(row(1), row(3), AF.Ln)
        return e.activation(row(2), row(1), AF.Exp, scale=-0.5)
    pr.on("scalar", row_rstd, wait=[(ctx.VC, v_row)], inc=[(ctx.SC, 1)])
    s_r = ctx.SC.n

    def mm_bcast(e):
        e.matmul(ps_mb[:, 0:nt], cst["ones32"][0:1, :], row(0), start=True, stop=True)
        return e.matmul(ps_rb[:, 0:nt], cst["ones32"][0:1, :], row(2), start=True, stop=True)
    pr.on("tensor", mm_bcast, wait=[(ctx.SC, s_r), (ctx.VC, v_row)], inc=[(ctx.TC, 1)])
    t_bc = ctx.TC.n

    def apply(e):
        for c in range(nC):
            e.tensor_tensor(t1[:, 0:nt], x32v[:, c, :], ps_mb[:, 0:nt], ALU.subtract)
            if w_col is not None and add_in is not None:
                e.tensor_tensor(t1b[:, 0:nt], t1[:, 0:nt], ps_rb[:, 0:nt], ALU.mult)
                e.tensor_scalar(t1[:, 0:nt], t1b[:, 0:nt],
                                w_col[:, c:c + 1], b_col[:, c:c + 1], ALU.mult, ALU.add)
                i = e.tensor_tensor(xn16v[:, c, :], t1[:, 0:nt], add_in[:, c, :], ALU.add)
            elif w_col is not None:
                e.tensor_tensor(t1b[:, 0:nt], t1[:, 0:nt], ps_rb[:, 0:nt], ALU.mult)
                i = e.tensor_scalar(xn16v[:, c, :], t1b[:, 0:nt],
                                    w_col[:, c:c + 1], b_col[:, c:c + 1], ALU.mult, ALU.add)
            elif t2_out is not None:
                e.tensor_tensor(t2_out[:, c, :], t1[:, 0:nt], ps_rb[:, 0:nt], ALU.mult)
                i = e.tensor_copy(xn16v[:, c, :], t2_out[:, c, :])
            else:
                i = e.tensor_tensor(xn16v[:, c, :], t1[:, 0:nt], ps_rb[:, 0:nt], ALU.mult)
        return i
    pr.on("vector", apply, wait=[(ctx.TC, t_bc)], inc=[(ctx.VC, 1)])
    return ctx.VC.n


class ProjPool:
    def __init__(self, ctx, name, n, nt, dtype=F32):
        self.ctx = ctx
        self.tiles = [ctx.psum(f"{name}{i}", [128, nt], dtype) for i in range(n)]
        self.free_at = [(None, 0)] * n
        self.i = 0

    def acquire(self):
        j = self.i % len(self.tiles)
        self.i += 1
        return j, self.tiles[j], self.free_at[j]

    def release(self, j, sem, count):
        self.free_at[j] = (sem, count)


def emit_proj_fm(ctx, w_sb, act16, nt, pool, ready, evac, n_out_chunks, extra_wait=(),
                 evac_eng="scalar", evac_wait=()):
    pr = ctx.pr
    nC = act16.shape[1]
    for mo in range(n_out_chunks):
        j, ps, free = pool.acquire()
        w = list(ready) + list(extra_wait)
        if free[0] is not None:
            w.append((free[0], free[1]))

        def mm(e, mo=mo, ps=ps):
            for c in range(nC):
                i = e.matmul(ps[:, 0:nt], w_sb[:, c, mo * 128:(mo + 1) * 128],
                             act16[:, c, 0:nt], start=(c == 0), stop=(c == nC - 1))
            return i
        pr.on("tensor", mm, wait=w, inc=[(ctx.TC, 1)])
        t_mm = ctx.TC.n
        esem = ctx.SC if evac_eng == "scalar" else ctx.VC
        pr.on(evac_eng, lambda e, mo=mo, ps=ps: evac(e, mo, ps),
              wait=[(ctx.TC, t_mm)] + list(evac_wait), inc=[(esem, 1)])
        pool.release(j, esem, esem.n)
    return esem.n



SCRATCH = {
    "s_xn": ([D, T], BF16, "p0", ["p3"]),
    "s_q": ([D, T], BF16, "p0", ["p1"]),
    "s_k": ([D, N], BF16, "p0", ["p1"]),
    "s_v": ([N, D], BF16, "p0", ["p1"]),
    "s_af": ([T, D], BF16, "p1", ["p2"]),
    "s_x1a": ([D, T], F32, "p2", ["p3"]),
    "s_x1": ([D, T], F32, "p3", ["p4", "p5"]),
    "s_xn1": ([D, T], BF16, "p4", ["p6"]),
    "s_u": ([D, T], BF16, "p4", ["p5"]),
    "s_qkf": ([256, T], BF16, "p4", ["p5"]),
    "s_vt": ([T, 2048], BF16, "p4", ["p5"]),
    "s_lkv": ([256, 2048], F32, "p4", ["p5"]),
    "s_x2a": ([T, D], F32, "p5", ["p6"]),
    "s_gates": ([128, 128], F32, "p4", ["p6"]),
}


def build_program(phases):
    nc = bass.Bass("TRN2", target_bir_lowering=False)
    pr = Prog(nc)
    hbm = {}
    in_names = []

    def hin(name, shape, dtype=F32):
        if name not in hbm:
            hbm[name] = nc.dram_tensor(name, list(shape), dtype, kind="ExternalInput").ap()
            in_names.append(name)
        return hbm[name]

    def hscratch(name):
        if name not in hbm:
            shape, dt, prod, cons = SCRATCH[name]
            if prod in phases:
                kind = "Internal" if all(cp in phases for cp in cons) else "ExternalOutput"
            else:
                kind = "ExternalInput"
                in_names.append(name)
            hbm[name] = nc.dram_tensor(name, list(shape), dt, kind=kind).ap()
        return hbm[name]

    out = None
    if "p6" in phases:
        out = nc.dram_tensor("out_x2", [T, D], F32, kind="ExternalOutput").ap()

    ctx = Ctx(nc, pr, hin, hscratch, out)
    with ctx.stack:
        _emit_prolog(ctx)
        for ph in phases:
            ctx.barrier()
            with contextlib.ExitStack() as ps:
                ctx.pstack = ps
                PHASE_EMITTERS[ph](ctx)
                ctx.pstack = None
        _emit_epilog(ctx)
        with nc.Block() as block:
            pr.emit(block)
    return nc, in_names


def _emit_prolog(ctx):
    pr, hin = ctx.pr, ctx.hin
    cst = ctx.consts
    cst["ones32"] = ctx.sbuf("c_ones32", [1, 128], F32)
    cst["ones16"] = ctx.sbuf("c_ones16", [1, 128], BF16)
    cst["ones16c"] = ctx.sbuf("c_ones16c", [128, 1], BF16)
    cst["ones32c"] = ctx.sbuf("c_ones32c", [128, 1], F32)
    cst["ident32"] = ctx.sbuf("c_ident32", [128, 128], F32)
    cst["ident16"] = ctx.sbuf("c_ident16", [128, 128], BF16)
    onesf = hin("onesf", [1, 128])
    ident = hin("ident", [128, 128])
    pr.on("sync", lambda e: e.dma_start(out=cst["ones32"][:, :], in_=onesf), inc=[(ctx.LD, 16)])
    pr.on("sync", lambda e: e.dma_start(out=cst["ident32"][:, :], in_=ident), inc=[(ctx.LD, 16)])
    pr.on("gpsimd", lambda e: e.dma_start(out=cst["ones16"][:, :], in_=onesf), inc=[(ctx.LG, 16)])
    pr.on("gpsimd", lambda e: e.dma_start(
        out=cst["ones16c"][:, :], in_=onesf[0:1, 0:1].partition_broadcast(128)),
        inc=[(ctx.LG, 16)])
    pr.on("sync", lambda e: e.dma_start(
        out=cst["ones32c"][:, :], in_=onesf[0:1, 0:1].partition_broadcast(128)),
        inc=[(ctx.LD, 16)])
    pr.on("gpsimd", lambda e: e.dma_start(out=cst["ident16"][:, :], in_=ident), inc=[(ctx.LG, 16)])


def _emit_epilog(ctx):
    pr = ctx.pr
    pr.on("sync", None, wait=[(ctx.ST, None), (ctx.LD, None)])
    pr.on("gpsimd", None, wait=[(ctx.ST, None), (ctx.LG, None)])



def p0(ctx):
    pr = ctx.pr
    xa = ctx.hin("xT_all", [D, N])
    xo = ctx.hin("xT_own", [D, T])
    s_k, s_v = ctx.hscratch("s_k"), ctx.hscratch("s_v")
    s_q, s_xn = ctx.hscratch("s_q"), ctx.hscratch("s_xn")

    w_sb = {}
    for nm in ("wq", "wk", "wv"):
        src = ctx.hin(nm, [D, D])
        w_sb[nm] = ctx.sbuf(f"p0_{nm}", [128, C, D], BF16)
        pr.on("gpsimd", lambda e, dst=w_sb[nm], src=src: e.dma_start(
            out=dst[:, :, :], in_=src.rearrange("(c p) o -> p c o", p=128)),
            inc=[(ctx.LG, 16)])

    xa32 = [ctx.sbuf(f"p0_xa32_{i}", [128, C, NT], F32) for i in range(2)]
    xa16 = [ctx.sbuf(f"p0_xa16_{i}", [128, C, NT], BF16) for i in range(2)]
    ln_ps = {"st0": ctx.psum("p0_ps_s0", [128, NT]),
             "st1": ctx.psum("p0_ps_s1", [128, NT]),
             "bc0": ctx.psum("p0_ps_b0", [128, NT]),
             "bc1": ctx.psum("p0_ps_b1", [128, NT])}
    lnb = [dict(ln_ps,
                sq=ctx.sbuf(f"p0_sq_{i}", [128, C, NT], BF16),
                row=ctx.sbuf(f"p0_row_{i}", [1, 4 * NT], F32),
                t1=ctx.sbuf(f"p0_t1_{i}", [128, NT], F32)) for i in range(2)]
    xn16 = [ctx.sbuf(f"p0_xn_{i}", [128, C, NT], BF16) for i in range(2)]
    kbuf = [ctx.sbuf(f"p0_kb_{i}", [128, C, NT], BF16) for i in range(2)]
    vbuf = [ctx.sbuf(f"p0_vb_{i}", [128, NT // 128, D], BF16) for i in range(2)]
    pool = ProjPool(ctx, "p0_po", 2, NT)

    buf_ok = [[], []]

    def do_tile(src, t, own):
        p = t % 2
        w_guard = list(buf_ok[p])
        pr.on("sync", lambda e, t=t, p=p, src=src: e.dma_start(
            out=xa32[p][:, :, :], in_=fm_tile(src, t * NT, NT)),
            wait=w_guard, inc=[(ctx.LD, 16)])
        pr.on("gpsimd", lambda e, t=t, p=p, src=src: e.dma_start(
            out=xa16[p][:, :, :], in_=fm_tile(src, t * NT, NT)),
            wait=w_guard, inc=[(ctx.LG, 16)])
        v_xn = emit_ln_fm(ctx, lnb[p], xa32[p][:, :, :], xa16[p][:, :, :],
                          xn16[p][:, :, :], NT, wait0=w_guard)
        ready = [(ctx.VC, v_xn)]
        st_marks = []
        if own:
            pr.on("sync", lambda e, t=t, p=p: e.dma_start(
                out=fm_tile(s_xn, t * NT, NT), in_=xn16[p][:, :, :]),
                wait=ready, inc=[(ctx.ST, 16)])
            st_marks.append(ctx.ST.n)

            def ev_q(e, mo, ps, p=p):
                return e.activation(kbuf[p][:, mo, 0:NT], ps[:, 0:NT], AF.Copy,
                                    scale=1.0 / 32.0)
            emit_proj_fm(ctx, w_sb["wq"], xn16[p][:, :, :], NT, pool, ready, ev_q, C)
            s_done = ctx.SC.n
            pr.on("sync", lambda e, t=t, p=p: e.dma_start(
                out=fm_tile(s_q, t * NT, NT), in_=kbuf[p][:, :, :]),
                wait=[(ctx.SC, s_done)], inc=[(ctx.ST, 16)])
            st_marks.append(ctx.ST.n)
        else:
            def ev_k(e, mo, ps, p=p):
                return e.activation(kbuf[p][:, mo, 0:NT], ps[:, 0:NT], AF.Copy)
            emit_proj_fm(ctx, w_sb["wk"], xn16[p][:, :, :], NT, pool, ready, ev_k, C)
            s_done = ctx.SC.n
            pr.on("sync", lambda e, t=t, p=p: e.dma_start(
                out=fm_tile(s_k, t * NT, NT), in_=kbuf[p][:, :, :]),
                wait=[(ctx.SC, s_done)], inc=[(ctx.ST, 16)])
            st_marks.append(ctx.ST.n)
            for sub in range(NT // 128):
                for half in range(2):
                    j, ps, free = pool.acquire()
                    w = list(ready) + ([free] if free[0] is not None else [])

                    def mmv(e, sub=sub, half=half, ps=ps, p=p):
                        for c in range(C):
                            i = e.matmul(ps[:, 0:NT],
                                         xn16[p][:, c, sub * 128:(sub + 1) * 128],
                                         w_sb["wv"][:, c, half * NT:(half + 1) * NT],
                                         start=(c == 0), stop=(c == C - 1))
                        return i
                    pr.on("tensor", mmv, wait=w, inc=[(ctx.TC, 1)])
                    t_mm = ctx.TC.n
                    pr.on("scalar", lambda e, sub=sub, half=half, ps=ps, p=p: e.activation(
                        vbuf[p][:, sub, half * NT:(half + 1) * NT], ps[:, 0:NT], AF.Copy),
                        wait=[(ctx.TC, t_mm)], inc=[(ctx.SC, 1)])
                    pool.release(j, ctx.SC, ctx.SC.n)
            s_done = ctx.SC.n
            pr.on("sync", lambda e, t=t, p=p: e.dma_start(
                out=tm_tile(s_v, t * NT, NT), in_=vbuf[p][:, :, :]),
                wait=[(ctx.SC, s_done)], inc=[(ctx.ST, 16)])
            st_marks.append(ctx.ST.n)
        buf_ok[p] = [(ctx.TC, ctx.TC.n), (ctx.VC, ctx.VC.n), (ctx.SC, ctx.SC.n),
                     (ctx.ST, max(st_marks))]

    for t in range(N // NT):
        do_tile(xa, t, own=False)
    for t in range(T // NT):
        do_tile(xo, t, own=True)


P1_DEBUG = False


def p1(ctx):
    pr = ctx.pr
    cst = ctx.consts
    s_k, s_v = ctx.hscratch("s_k"), ctx.hscratch("s_v")
    s_q, s_af = ctx.hscratch("s_q"), ctx.hscratch("s_af")
    masks = ctx.hin("masks", [16, 2, 128, 128])
    vsel = ctx.hin("vsel", [16, 2, 128, 128])
    g0r = ctx.hin("g0_row", [1, D])
    b0r = ctx.hin("b0_row", [1, D])

    k_sb = ctx.sbuf("p1_k", [128, C, N], BF16)
    v_sb = ctx.sbuf("p1_v", [128, N // 128, D], BF16)
    q_sb = [ctx.sbuf(f"p1_q{i}", [128, C, 128], BF16) for i in range(2)]
    mk_sb = [ctx.sbuf(f"p1_mk{i}", [128, 2, 128], F32) for i in range(2)]
    vs_sb = [ctx.sbuf(f"p1_vs{i}", [128, 2, 128], BF16) for i in range(2)]
    srow_b = [ctx.sbuf(f"p1_srow{i}", [128, N], F32) for i in range(2)]
    prow_b = [ctx.sbuf(f"p1_prow{i}", [128, N], BF16) for i in range(2)]
    prow2_b = [ctx.sbuf(f"p1_prow2{i}", [128, N], BF16) for i in range(2)]
    mch_b = [ctx.sbuf(f"p1_mch{i}", [128, 16], F32) for i in range(2)]
    mneg_b = [ctx.sbuf(f"p1_mneg{i}", [128, 1], F32) for i in range(2)]
    lcol_b = [ctx.sbuf(f"p1_l{i}", [128, 1], F32) for i in range(2)]
    rcol_b = [ctx.sbuf(f"p1_r{i}", [128, 1], F32) for i in range(2)]
    pt_sb = [ctx.sbuf(f"p1_pt{i}", [128, 128], BF16) for i in range(2)]
    afb = ctx.sbuf("p1_af", [128, D], BF16)
    g0b = ctx.sbuf("p1_g0b", [128, D], BF16)
    b0b = ctx.sbuf("p1_b0b", [128, D], BF16)
    m2c = ctx.sbuf("p1_m2c", [128, 1], F32)
    v2c = ctx.sbuf("p1_v2c", [128, 1], F32)
    m2a = ctx.sbuf("p1_m2a", [128, 1], F32)
    v2a = ctx.sbuf("p1_v2a", [128, 1], F32)
    v2b = ctx.sbuf("p1_v2b", [128, 1], F32)
    ps_s = [ctx.psum(f"p1_ps_s{i}", [128, 512]) for i in range(2)]
    ps_o = ctx.psum("p1_ps_o", [128, D])
    ps_t = [ctx.psum(f"p1_ps_t{i}", [128, 128], BF16) for i in range(2)]

    pr.on("gpsimd", lambda e: e.dma_start(
        out=k_sb[:, :, :], in_=s_k.rearrange("(c p) n -> p c n", p=128)), inc=[(ctx.LG, 16)])
    pr.on("gpsimd", lambda e: e.dma_start(
        out=v_sb[:, :, :], in_=s_v.rearrange("(j p) d -> p j d", p=128)), inc=[(ctx.LG, 16)])
    pr.on("gpsimd", lambda e: e.dma_start(
        out=g0b[:, :], in_=g0r.partition_broadcast(128)), inc=[(ctx.LG, 16)])
    pr.on("gpsimd", lambda e: e.dma_start(
        out=b0b[:, :], in_=b0r.partition_broadcast(128)), inc=[(ctx.LG, 16)])

    qload_guard = [[], []]
    ps_s_free = [(None, 0), (None, 0)]
    pt_ps_free = [(None, 0), (None, 0)]
    pt_sb_free = [(None, 0), (None, 0)]
    srow_guard = [[], []]
    st_af_prev = [0]

    for i in range(16):
        nk = 2 * i + 2
        ncol = nk * 128
        nkc = (ncol + 511) // 512
        pq = i % 2
        srow, prow, prow2 = srow_b[pq], prow_b[pq], prow2_b[pq]
        mch, mneg, lcol, rcol = mch_b[pq], mneg_b[pq], lcol_b[pq], rcol_b[pq]
        fin32 = srow[:, 0:D]
        fin32b = srow[:, D:2 * D]
        pr.on("sync", lambda e, i=i, pq=pq: e.dma_start(
            out=q_sb[pq][:, :, :],
            in_=s_q[:, i * 128:(i + 1) * 128].rearrange("(c p) n -> p c n", p=128)),
            wait=qload_guard[pq], inc=[(ctx.LD, 16)])
        pr.on("sync", lambda e, i=i, pq=pq: e.dma_start(
            out=mk_sb[pq][:, :, :], in_=masks[i].rearrange("j q k -> q j k")),
            wait=qload_guard[pq], inc=[(ctx.LD, 16)])
        pr.on("gpsimd", lambda e, i=i, pq=pq: e.dma_start(
            out=vs_sb[pq][:, :, :], in_=vsel[i].rearrange("j q k -> q j k")),
            wait=qload_guard[pq], inc=[(ctx.LG, 16)])
        ld_q = ctx.LD.n
        lg_q = ctx.LG.n

        for kc in range(nkc):
            c0 = kc * 512
            cw = min(512, ncol - c0)
            jp, ps, free = kc % 2, ps_s[kc % 2], ps_s_free[kc % 2]
            w = [(ctx.LD, ld_q), (ctx.LG, None)]
            if free[0] is not None:
                w.append(free)

            def mm_s(e, i=i, pq=pq, ps=ps, c0=c0, cw=cw):
                for c in range(C):
                    r = e.matmul(ps[:, 0:cw], q_sb[pq][:, c, :], k_sb[:, c, c0:c0 + cw],
                                 start=(c == 0), stop=(c == C - 1))
                return r
            pr.on("tensor", mm_s, wait=w, inc=[(ctx.TC, 1)])
            t_s = ctx.TC.n

            last = (kc == nkc - 1)

            def ev_s(e, ps=ps, c0=c0, cw=cw, last=last, pq=pq, i=i, ncol=ncol, kc=kc):
                e.tensor_copy(srow[:, c0:c0 + cw], ps[:, 0:cw])
                if last:
                    e.tensor_tensor(srow[:, ncol - 256:ncol], srow[:, ncol - 256:ncol],
                                    mk_sb[pq][:, :, :].rearrange("p j k -> p (j k)"), ALU.add)
                return e.reduce_max(mch[:, kc:kc + 1], srow[:, c0:c0 + cw], axis=AX.X)
            pr.on("vector", ev_s, wait=[(ctx.TC, t_s)] + srow_guard[pq],
                  inc=[(ctx.VC, 1)])
            ps_s_free[jp] = (ctx.VC, ctx.VC.n)

        def red_m(e, nkc=nkc):
            if nkc < 16:
                e.memset(mch[:, nkc:16], -1.0e30)
            e.drain()
            return e.reduce_max(mneg[:, 0:1], mch[:, 0:16], axis=AX.X, negate=True)
        pr.on("vector", red_m, inc=[(ctx.VC, 1)])
        v_m = ctx.VC.n
        pr.on("scalar", lambda e, ncol=ncol: e.activation(
            prow[:, 0:ncol], srow[:, 0:ncol], AF.Exp, bias=mneg[:, 0:1]),
            wait=[(ctx.VC, v_m), (ctx.TC, ctx.TC.n)],
            inc=[(ctx.SC, 1)])
        s_exp = ctx.SC.n

        def p_fix(e, ncol=ncol, pq=pq):
            e.reduce_sum(lcol[:, 0:1], prow[:, 0:ncol], axis=AX.X)
            e.drain()
            e.reciprocal(rcol[:, 0:1], lcol[:, 0:1])
            e.drain()
            if ncol > 256:
                e.tensor_scalar(prow2[:, 0:ncol - 256], prow[:, 0:ncol - 256],
                                rcol[:, 0:1], None, ALU.mult)
            return e.scalar_tensor_tensor(
                prow2[:, ncol - 256:ncol], prow[:, ncol - 256:ncol], rcol[:, 0:1],
                vs_sb[pq][:, :, :].rearrange("p j k -> p (j k)"), ALU.mult, ALU.add)
        pr.on("vector", p_fix, wait=[(ctx.SC, s_exp), (ctx.LG, lg_q),
                                     (ctx.TC, ctx.TC.n)], inc=[(ctx.VC, 1)])
        v_p = ctx.VC.n

        mm_done_at = {}
        for j in range(nk):
            jp = j % 2
            w = [(ctx.VC, v_p)]
            if pt_ps_free[jp][0] is not None:
                w.append(pt_ps_free[jp])
            pr.on("tensor", lambda e, j=j, jp=jp: e.transpose(
                ps_t[jp][:, :], prow2[:, j * 128:(j + 1) * 128], cst["ident16"][:, :]),
                wait=w, inc=[(ctx.TC, 1)])
            t_t = ctx.TC.n
            wv = [(ctx.TC, t_t)]
            if pt_sb_free[jp][0] is not None:
                wv.append(pt_sb_free[jp])
            pr.on("vector", lambda e, jp=jp: e.tensor_copy(pt_sb[jp][:, :], ps_t[jp][:, :]),
                  wait=wv, inc=[(ctx.VC, 1)])
            pt_ps_free[jp] = (ctx.VC, ctx.VC.n)
            mm_done_at[j] = (ctx.VC.n, jp)
            if j >= 1:
                _emit_pv_mm(ctx, i, j - 1, nk, mm_done_at, v_sb, pt_sb, ps_o, pt_sb_free)
        _emit_pv_mm(ctx, i, nk - 1, nk, mm_done_at, v_sb, pt_sb, ps_o, pt_sb_free)
        t_pv = ctx.TC.n

        def ln2(e):
            e.tensor_copy(fin32, ps_o[:, :])
            e.reduce_sum(m2a[:, 0:1], fin32, axis=AX.X)
            e.drain()
            e.tensor_scalar(m2c[:, 0:1], m2a[:, 0:1], 1.0 / D, None, ALU.mult)
            e.drain()
            e.tensor_scalar(fin32b, fin32, m2c[:, 0:1], None, ALU.subtract)
            e.tensor_tensor(fin32, fin32b, fin32b, ALU.mult)
            e.reduce_sum(v2a[:, 0:1], fin32, axis=AX.X)
            e.drain()
            return e.tensor_scalar(v2c[:, 0:1], v2a[:, 0:1], 1.0 / D, EPS, ALU.mult, ALU.add)
        pr.on("vector", ln2, wait=[(ctx.TC, t_pv), (ctx.LD, None)], inc=[(ctx.VC, 1)])
        v_ln = ctx.VC.n

        def rstd2(e):
            e.activation(v2b[:, 0:1], v2c[:, 0:1], AF.Ln)
            e.drain()
            return e.activation(v2c[:, 0:1], v2b[:, 0:1], AF.Exp, scale=-0.5)
        pr.on("scalar", rstd2, wait=[(ctx.VC, v_ln)], inc=[(ctx.SC, 1)])
        s_r2 = ctx.SC.n

        def fin_aff(e):
            e.scalar_tensor_tensor(fin32, fin32b, v2c[:, 0:1], g0b[:, :],
                                   ALU.mult, ALU.mult)
            return e.tensor_tensor(afb[:, :], fin32, b0b[:, :], ALU.add)
        pr.on("vector", fin_aff, wait=[(ctx.SC, s_r2), (ctx.ST, st_af_prev[0])],
              inc=[(ctx.VC, 1)])
        v_fin = ctx.VC.n

        pr.on("sync", lambda e, i=i: e.dma_start(
            out=s_af[i * 128:(i + 1) * 128, :], in_=afb[:, :]),
            wait=[(ctx.VC, v_fin)], inc=[(ctx.ST, 16)])
        st_af_prev[0] = ctx.ST.n
        if P1_DEBUG and i == 1:
            nc = ctx.nc
            d_s = nc.dram_tensor("dbg_srow", [128, 512], F32, kind="ExternalOutput").ap()
            d_p = nc.dram_tensor("dbg_prow", [128, 512], BF16, kind="ExternalOutput").ap()
            d_m = nc.dram_tensor("dbg_m", [128, 4], F32, kind="ExternalOutput").ap()
            d_o = nc.dram_tensor("dbg_o", [128, D], F32, kind="ExternalOutput").ap()
            d_l = nc.dram_tensor("dbg_l", [3, 128, 1], F32, kind="ExternalOutput").ap()
            def dbg(e):
                e.dma_start(out=d_s, in_=srow[:, 0:512]).then_inc(ctx.ST.h, 16)
                e.dma_start(out=d_p, in_=prow[:, 0:512]).then_inc(ctx.ST.h, 16)
                e.dma_start(out=d_m, in_=mch[:, 0:4]).then_inc(ctx.ST.h, 16)
                e.dma_start(out=d_l[0], in_=lcol[:, 0:1]).then_inc(ctx.ST.h, 16)
                e.dma_start(out=d_l[1], in_=rcol[:, 0:1]).then_inc(ctx.ST.h, 16)
                e.dma_start(out=d_l[2], in_=mneg[:, 0:1]).then_inc(ctx.ST.h, 16)
                return e.dma_start(out=d_o, in_=fin32[:, :])
            pr.on("sync", dbg, wait=[(ctx.VC, ctx.VC.n)], inc=[(ctx.ST, 16)])
            ctx.ST.n += 80
            stall = ctx.ST.n
            pr.on("vector", None, wait=[(ctx.ST, stall)], inc=[(ctx.VC, 1)])
            pr.on("scalar", None, wait=[(ctx.ST, stall)], inc=[(ctx.SC, 1)])
            pr.on("tensor", None, wait=[(ctx.ST, stall)], inc=[(ctx.TC, 1)])
        qload_guard[pq] = [(ctx.TC, ctx.TC.n), (ctx.VC, ctx.VC.n), (ctx.SC, ctx.SC.n)]


def _emit_pv_mm(ctx, i, j, nk, mm_done_at, v_sb, pt_sb, ps_o, pt_sb_free):
    pr = ctx.pr
    v_c, jp = mm_done_at[j]

    def mm(e, j=j, jp=jp, nk=nk):
        e.matmul(ps_o[:, 0:512], pt_sb[jp][:, :], v_sb[:, j, 0:512],
                 start=(j == 0), stop=(j == nk - 1))
        return e.matmul(ps_o[:, 512:1024], pt_sb[jp][:, :], v_sb[:, j, 512:1024],
                        start=(j == 0), stop=(j == nk - 1))
    pr.on("tensor", mm, wait=[(ctx.VC, v_c)], inc=[(ctx.TC, 1)])
    pt_sb_free[jp] = (ctx.TC, ctx.TC.n)


def p2(ctx):
    pr, cst = ctx.pr, ctx.consts
    s_af = ctx.hscratch("s_af")
    s_x1a = ctx.hscratch("s_x1a")
    xo = ctx.hin("xT_own", [D, T])
    wfc = ctx.hin("wfc", [D, D])
    bfc = ctx.hin("bfc_c", [128, 8])

    w_sb = ctx.sbuf("p2_wfc", [128, C, D], BF16)
    bfc_sb = ctx.sbuf("p2_bfc", [128, 8], F32)
    pr.on("gpsimd", lambda e: e.dma_start(
        out=w_sb[:, :, :], in_=wfc.rearrange("(c p) o -> p c o", p=128)), inc=[(ctx.LG, 16)])
    lg_w2 = ctx.LG.n
    pr.on("sync", lambda e: e.dma_start(out=bfc_sb[:, :], in_=bfc), inc=[(ctx.LD, 16)])

    af_tm = [ctx.sbuf(f"p2_af{i}", [128, 4, D], BF16) for i in range(2)]
    af_fm = [ctx.sbuf(f"p2_aff{i}", [128, C, NT], BF16) for i in range(2)]
    x32 = [ctx.sbuf(f"p2_x{i}", [128, C, NT], F32) for i in range(2)]
    x1a = [ctx.sbuf(f"p2_x1a{i}", [128, C, NT], F32) for i in range(2)]
    ps_t = [ctx.psum(f"p2_pt{i}", [128, 128], BF16) for i in range(2)]
    pool = ProjPool(ctx, "p2_po", 2, NT)

    buf_ok = [[], []]
    pt_free = [(None, 0), (None, 0)]
    for t in range(T // NT):
        p = t % 2
        w_guard = list(buf_ok[p])
        pr.on("sync", lambda e, t=t, p=p: e.dma_start(
            out=af_tm[p][:, :, :], in_=tm_tile(s_af, t * NT, NT)),
            wait=w_guard, inc=[(ctx.LD, 16)])
        pr.on("sync", lambda e, t=t, p=p: e.dma_start(
            out=x32[p][:, :, :], in_=fm_tile(xo, t * NT, NT)),
            wait=w_guard, inc=[(ctx.LD, 16)])
        ld_t = ctx.LD.n
        for sub in range(4):
            for c in range(C):
                jp = (sub * C + c) % 2
                w = [(ctx.LD, ld_t)]
                if pt_free[jp][0] is not None:
                    w.append(pt_free[jp])
                pr.on("tensor", lambda e, p=p, sub=sub, c=c, jp=jp: e.transpose(
                    ps_t[jp][:, :], af_tm[p][:, sub, c * 128:(c + 1) * 128],
                    cst["ident16"][:, :]), wait=w, inc=[(ctx.TC, 1)])
                t_t = ctx.TC.n
                pr.on("vector", lambda e, p=p, sub=sub, c=c, jp=jp: e.tensor_copy(
                    af_fm[p][:, c, sub * 128:(sub + 1) * 128], ps_t[jp][:, :]),
                    wait=[(ctx.TC, t_t)], inc=[(ctx.VC, 1)])
                pt_free[jp] = (ctx.VC, ctx.VC.n)
        v_tr = ctx.VC.n

        def ev_fc(e, mo, ps, p=p):
            return e.scalar_tensor_tensor(x1a[p][:, mo, 0:NT], ps[:, 0:NT],
                                          bfc_sb[:, mo:mo + 1], x32[p][:, mo, 0:NT],
                                          ALU.add, ALU.add)
        emit_proj_fm(ctx, w_sb, af_fm[p][:, :, :], NT, pool,
                     [(ctx.VC, v_tr), (ctx.LG, lg_w2)], ev_fc, C,
                     evac_eng="vector", evac_wait=[(ctx.LD, ld_t)])
        v_add = ctx.VC.n
        pr.on("sync", lambda e, t=t, p=p: e.dma_start(
            out=fm_tile(s_x1a, t * NT, NT), in_=x1a[p][:, :, :]),
            wait=[(ctx.VC, v_add)], inc=[(ctx.ST, 16)])
        buf_ok[p] = [(ctx.TC, ctx.TC.n), (ctx.VC, ctx.VC.n), (ctx.SC, ctx.SC.n),
                     (ctx.ST, ctx.ST.n)]


def p3(ctx):
    pr = ctx.pr
    s_xn = ctx.hscratch("s_xn")
    s_x1a = ctx.hscratch("s_x1a")
    s_x1 = ctx.hscratch("s_x1")
    w1 = ctx.hin("w1", [D, 4096])
    w2 = ctx.hin("w2", [4096, D])
    b1 = ctx.hin("b1_c", [128, 32])
    b2 = ctx.hin("b2_c", [128, 8])

    w1_sb = ctx.sbuf("p3_w1", [128, C, 4096], BF16)
    w2_sb = ctx.sbuf("p3_w2", [128, 32, D], BF16)
    b1_sb = ctx.sbuf("p3_b1", [128, 32], F32)
    b2_sb = ctx.sbuf("p3_b2", [128, 8], F32)
    pr.on("gpsimd", lambda e: e.dma_start(
        out=w1_sb[:, :, :], in_=w1.rearrange("(c p) o -> p c o", p=128)), inc=[(ctx.LG, 16)])
    pr.on("gpsimd", lambda e: e.dma_start(
        out=w2_sb[:, :, :], in_=w2.rearrange("(c p) o -> p c o", p=128)), inc=[(ctx.LG, 16)])
    lg_w3 = ctx.LG.n
    pr.on("sync", lambda e: e.dma_start(out=b1_sb[:, :], in_=b1), inc=[(ctx.LD, 16)])
    pr.on("sync", lambda e: e.dma_start(out=b2_sb[:, :], in_=b2), inc=[(ctx.LD, 16)])

    xn_sb = [ctx.sbuf(f"p3_xn{i}", [128, C, NT3], BF16) for i in range(2)]
    h_sb = ctx.sbuf("p3_h", [128, 32, NT3], BF16)
    x1a32 = ctx.sbuf("p3_x1a", [128, C, NT3], F32)
    x1b = ctx.sbuf("p3_x1b", [128, C, NT3], F32)
    pool = ProjPool(ctx, "p3_po", 2, NT3)
    h_guard = [0]
    x1a_guard = [[]]
    x1b_guard = [[]]

    buf_ok = [[], []]
    for t in range(T // NT3):
        p = t % 2
        w_guard = list(buf_ok[p])
        pr.on("sync", lambda e, t=t, p=p: e.dma_start(
            out=xn_sb[p][:, :, :], in_=fm_tile(s_xn, t * NT3, NT3)),
            wait=w_guard, inc=[(ctx.LD, 16)])
        pr.on("sync", lambda e, t=t, p=p: e.dma_start(
            out=x1a32[:, :, :], in_=fm_tile(s_x1a, t * NT3, NT3)),
            wait=w_guard + x1a_guard[0], inc=[(ctx.LD, 16)])
        ld_t = ctx.LD.n

        def ev_h(e, mo, ps, p=p):
            return e.activation(h_sb[:, mo, 0:NT3], ps[:, 0:NT3], AF.Relu,
                                bias=b1_sb[:, mo:mo + 1])
        emit_proj_fm(ctx, w1_sb, xn_sb[p][:, :, :], NT3, pool,
                     [(ctx.LD, ld_t), (ctx.TC, h_guard[0]), (ctx.LG, lg_w3)], ev_h, 32)
        s_h = ctx.SC.n

        def ev_y(e, mo, ps, p=p):
            return e.scalar_tensor_tensor(x1b[:, mo, 0:NT3], ps[:, 0:NT3],
                                          b2_sb[:, mo:mo + 1], x1a32[:, mo, 0:NT3],
                                          ALU.add, ALU.add)
        emit_proj_fm(ctx, w2_sb, h_sb[:, :, :], NT3, pool, [(ctx.SC, s_h)], ev_y, C,
                     evac_eng="vector", evac_wait=[(ctx.LD, ld_t)] + x1b_guard[0])
        v_add = ctx.VC.n
        h_guard[0] = ctx.TC.n
        x1a_guard[0] = [(ctx.VC, v_add)]
        pr.on("sync", lambda e, t=t, p=p: e.dma_start(
            out=fm_tile(s_x1, t * NT3, NT3), in_=x1b[:, :, :]),
            wait=[(ctx.VC, v_add)], inc=[(ctx.ST, 16)])
        x1b_guard[0] = [(ctx.ST, ctx.ST.n)]
        buf_ok[p] = [(ctx.TC, ctx.TC.n), (ctx.VC, ctx.VC.n), (ctx.SC, ctx.SC.n),
                     (ctx.ST, ctx.ST.n)]


def p4(ctx):
    pr, cst = ctx.pr, ctx.consts
    nc = ctx.nc
    s_x1 = ctx.hscratch("s_x1")
    s_xn1 = ctx.hscratch("s_xn1")
    s_u = ctx.hscratch("s_u")
    s_qkf = ctx.hscratch("s_qkf")
    s_vt = ctx.hscratch("s_vt")
    s_lkv = ctx.hscratch("s_lkv")
    wqkT = ctx.hin("wqkT", [D, 256])
    wvT1 = ctx.hin("wvT1", [D, 2048])
    bqk = ctx.hin("bqk_c", [128, 2])
    ln1w = ctx.hin("ln1w_c", [128, 8])
    ln1b = ctx.hin("ln1b_c", [128, 8])
    g3r = ctx.hin("g3_row", [1, 256])
    b3r = ctx.hin("b3_row", [1, 256])
    bvr = ctx.hin("bv_row", [1, 2048])
    cosT = ctx.hin("cosT", [T, 128])
    sinT = ctx.hin("sinT", [T, 128])
    cc_in = nc.dram_tensor("cc_in", [256, 2048], F32, kind="Internal").ap()
    cc_out = nc.dram_tensor("cc_out", [256, 2048], F32, kind="Internal").ap()

    s_gates = ctx.hscratch("s_gates")
    wgmT = ctx.hin("wgmT", [D, 8])
    wgm_sb = ctx.sbuf("p4_wgm", [128, C, 8], BF16)
    gt_sc = ctx.sbuf("p4_gt_sc", [128, 8], F32)
    gt_m1 = ctx.sbuf("p4_gt_m1", [128, 1], F32)
    gt_m1n = ctx.sbuf("p4_gt_m1n", [128, 1], F32)
    gt_m2 = ctx.sbuf("p4_gt_m2", [128, 1], F32)
    gt_tk = ctx.sbuf("p4_gt_tk", [128, 8], F32)
    gt_s2 = ctx.sbuf("p4_gt_s2", [128, 8], F32)
    gt_kp = ctx.sbuf("p4_gt_kp", [128, 8], F32)
    gt_eu = ctx.sbuf("p4_gt_eu", [128, 8], F32)
    gt_gk = ctx.sbuf("p4_gt_gk", [128, 8], F32)
    gt_dn = ctx.sbuf("p4_gt_dn", [128, 1], F32)
    gt_rd = ctx.sbuf("p4_gt_rd", [128, 1], F32)
    gt_out = ctx.sbuf("p4_gt_out", [128, 16, 8], F32)
    wqk_sb = ctx.sbuf("p4_wqk", [128, C, 256], BF16)
    wv_sb = ctx.sbuf("p4_wv", [128, C, 2048], BF16)
    bqk_sb = ctx.sbuf("p4_bqk", [128, 2], F32)
    ln1w_sb = ctx.sbuf("p4_l1w", [128, 8], F32)
    ln1b_sb = ctx.sbuf("p4_l1b", [128, 8], F32)
    g3b = ctx.sbuf("p4_g3b", [128, 256], F32)
    b3b = ctx.sbuf("p4_b3b", [128, 256], F32)
    bvb = ctx.sbuf("p4_bvb", [128, 2048], F32)
    cosT_sb = ctx.sbuf("p4_cosT", [128, 16, 128], BF16)
    sinT_sb = ctx.sbuf("p4_sinT", [128, 16, 128], BF16)
    pr.on("gpsimd", lambda e: e.dma_start(
        out=wgm_sb[:, :, :], in_=wgmT.rearrange("(c p) o -> p c o", p=128)), inc=[(ctx.LG, 16)])
    pr.on("gpsimd", lambda e: e.dma_start(
        out=wqk_sb[:, :, :], in_=wqkT.rearrange("(c p) o -> p c o", p=128)), inc=[(ctx.LG, 16)])
    pr.on("gpsimd", lambda e: e.dma_start(
        out=wv_sb[:, :, :], in_=wvT1.rearrange("(c p) o -> p c o", p=128)), inc=[(ctx.LG, 16)])
    pr.on("gpsimd", lambda e: e.dma_start(
        out=cosT_sb[:, :, :], in_=cosT.rearrange("(s p) f -> p s f", p=128)), inc=[(ctx.LG, 16)])
    pr.on("gpsimd", lambda e: e.dma_start(
        out=sinT_sb[:, :, :], in_=sinT.rearrange("(s p) f -> p s f", p=128)), inc=[(ctx.LG, 16)])
    pr.on("sync", lambda e: e.dma_start(out=bqk_sb[:, :], in_=bqk), inc=[(ctx.LD, 16)])
    pr.on("sync", lambda e: e.dma_start(out=ln1w_sb[:, :], in_=ln1w), inc=[(ctx.LD, 16)])
    pr.on("sync", lambda e: e.dma_start(out=ln1b_sb[:, :], in_=ln1b), inc=[(ctx.LD, 16)])
    pr.on("sync", lambda e: e.dma_start(out=g3b[:, :], in_=g3r.partition_broadcast(128)), inc=[(ctx.LD, 16)])
    pr.on("sync", lambda e: e.dma_start(out=b3b[:, :], in_=b3r.partition_broadcast(128)), inc=[(ctx.LD, 16)])
    pr.on("sync", lambda e: e.dma_start(out=bvb[:, :], in_=bvr.partition_broadcast(128)), inc=[(ctx.LD, 16)])
    lg_w = ctx.LG.n
    ld_w = ctx.LD.n

    x32 = ctx.sbuf("p4_x32", [128, C, NT], F32)
    x16 = ctx.sbuf("p4_x16", [128, C, NT], BF16)
    t2 = ctx.sbuf("p4_t2", [128, C, NT], F32)
    xn1_16 = ctx.sbuf("p4_xn1", [128, C, NT], BF16)
    u16 = ctx.sbuf("p4_u", [128, C, NT], BF16)
    qkf = ctx.sbuf("p4_qkf", [128, 2, NT], BF16)
    qkt = ctx.sbuf("p4_qkt", [128, 4, 256], BF16)
    qg = ctx.sbuf("p4_qg", [128, 256], BF16)
    qh = ctx.sbuf("p4_qh", [128, 256], BF16)
    lkr = ctx.sbuf("p4_lkr", [128, 4, 256], BF16)
    rta = ctx.sbuf("p4_rta", [128, 128], BF16)
    rtb = ctx.sbuf("p4_rtb", [128, 128], BF16)
    vtm = ctx.sbuf("p4_vtm", [128, 4, 2048], BF16)
    vtmp = ctx.sbuf("p4_vtmp", [128, 512], F32)
    lkv = ctx.sbuf("p4_lkv", [128, 2, 2048], F32)
    lkv2 = ctx.sbuf("p4_lkv2", [128, 2, 2048], F32)
    lnb = {"sq": ctx.sbuf("p4_sq", [128, C, NT], BF16),
           "row": ctx.sbuf("p4_row", [1, 4 * NT], F32),
           "t1": ctx.sbuf("p4_t1", [128, NT], F32),
           "t1b": ctx.sbuf("p4_t1b", [128, NT], F32),
           "st0": ctx.psum("p4_ps_s0", [128, NT]),
           "st1": ctx.psum("p4_ps_s1", [128, NT]),
           "bc0": ctx.psum("p4_ps_b0", [128, NT]),
           "bc1": ctx.psum("p4_ps_b1", [128, NT])}
    ps_t = [ctx.psum(f"p4_pt{i}", [128, 128], BF16) for i in range(2)]
    pool = ProjPool(ctx, "p4_po", 2, NT)

    buf_ok = [[]]
    pt_free = [(None, 0), (None, 0)]
    for t in range(T // NT):
        w_guard = list(buf_ok[0])
        pr.on("sync", lambda e, t=t: e.dma_start(
            out=x32[:, :, :], in_=fm_tile(s_x1, t * NT, NT)),
            wait=w_guard, inc=[(ctx.LD, 16)])
        pr.on("gpsimd", lambda e, t=t: e.dma_start(
            out=x16[:, :, :], in_=fm_tile(s_x1, t * NT, NT)),
            wait=w_guard, inc=[(ctx.LG, 16)])
        ld_t, lg_t = ctx.LD.n, ctx.LG.n

        v_xn1 = emit_ln_fm(ctx, lnb, x32[:, :, :], x16[:, :, :], xn1_16[:, :, :],
                           NT, t2_out=t2[:, :, :], wait0=w_guard)
        pr.on("sync", lambda e, t=t: e.dma_start(
            out=fm_tile(s_xn1, t * NT, NT), in_=xn1_16[:, :, :]),
            wait=[(ctx.VC, v_xn1)], inc=[(ctx.ST, 16)])
        gt_guard = [0]
        for q in range(4):
            tch = t * 4 + q
            jg, ps_gt, freeg = pool.acquire()

            def mmgt(e, q=q, ps_gt=ps_gt):
                for c in range(C):
                    i = e.matmul(ps_gt[:, 0:8], xn1_16[:, c, q * 128:(q + 1) * 128],
                                 wgm_sb[:, c, 0:8], start=(c == 0), stop=(c == C - 1))
                return i
            wgt = [(ctx.VC, v_xn1), (ctx.LG, lg_w), (ctx.VC, gt_guard[0])]
            if freeg[0] is not None:
                wgt.append(freeg)
            pr.on("tensor", mmgt, wait=wgt, inc=[(ctx.TC, 1)])
            t_gt = ctx.TC.n

            def top2(e, tch=tch, ps_gt=ps_gt):
                e.tensor_copy(gt_sc[:, :], ps_gt[:, 0:8])
                e.drain()
                e.reduce_max(gt_m1[:, 0:1], gt_sc[:, :], axis=AX.X)
                e.drain()
                e.tensor_scalar(gt_m1n[:, 0:1], gt_m1[:, 0:1], -1.0, None, ALU.mult)
                e.tensor_scalar(gt_tk[:, :], gt_sc[:, :], gt_m1[:, 0:1], None, ALU.is_ge)
                e.drain()
                e.scalar_tensor_tensor(gt_s2[:, :], gt_tk[:, :], -1.0e9, gt_sc[:, :],
                                       ALU.mult, ALU.add)
                e.drain()
                e.reduce_max(gt_m2[:, 0:1], gt_s2[:, :], axis=AX.X)
                e.drain()
                return e.tensor_scalar(gt_kp[:, :], gt_sc[:, :], gt_m2[:, 0:1], None, ALU.is_ge)
            pr.on("vector", top2, wait=[(ctx.TC, t_gt)], inc=[(ctx.VC, 1)])
            v_t2 = ctx.VC.n
            pool.release(jg, ctx.VC, v_t2)
            pr.on("scalar", lambda e: e.activation(gt_eu[:, :], gt_sc[:, :], AF.Exp,
                                                   bias=gt_m1n[:, 0:1]),
                  wait=[(ctx.VC, v_t2)], inc=[(ctx.SC, 1)])
            s_eu = ctx.SC.n

            def gfin(e, tch=tch):
                e.tensor_tensor(gt_gk[:, :], gt_eu[:, :], gt_kp[:, :], ALU.mult)
                e.drain()
                e.reduce_sum(gt_dn[:, 0:1], gt_gk[:, :], axis=AX.X)
                e.drain()
                e.reciprocal(gt_rd[:, 0:1], gt_dn[:, 0:1])
                e.drain()
                return e.tensor_scalar(gt_out[:, tch % 4 + (tch // 4) * 4, :],
                                       gt_gk[:, :], gt_rd[:, 0:1], None, ALU.mult)
            pr.on("vector", gfin, wait=[(ctx.SC, s_eu)], inc=[(ctx.VC, 1)])
            gt_guard[0] = ctx.VC.n
        if t == T // NT - 1:
            pass
        v_u = emit_ln_fm(ctx, lnb, t2[:, :, :], xn1_16[:, :, :], u16[:, :, :],
                         NT, w_col=ln1w_sb, b_col=ln1b_sb,
                         wait0=[(ctx.VC, v_xn1), (ctx.TC, ctx.TC.n)])
        pr.on("sync", lambda e, t=t: e.dma_start(
            out=fm_tile(s_u, t * NT, NT), in_=u16[:, :, :]),
            wait=[(ctx.VC, v_u)], inc=[(ctx.ST, 16)])

        def ev_qk(e, mo, ps):
            return e.activation(qkf[:, mo, 0:NT], ps[:, 0:NT], AF.Silu,
                                bias=bqk_sb[:, mo:mo + 1])
        emit_proj_fm(ctx, wqk_sb, u16[:, :, :], NT, pool,
                     [(ctx.VC, v_u), (ctx.LG, lg_w), (ctx.LD, ld_w)], ev_qk, 2)
        s_qk = ctx.SC.n
        pr.on("sync", lambda e, t=t: e.dma_start(
            out=s_qkf[:, t * NT:(t + 1) * NT].rearrange("(c p) n -> p c n", p=128),
            in_=qkf[:, :, :]), wait=[(ctx.SC, s_qk)], inc=[(ctx.ST, 16)])

        for sub in range(4):
            for ch in range(2):
                jp = (sub * 2 + ch) % 2
                w = [(ctx.SC, s_qk)]
                if pt_free[jp][0] is not None:
                    w.append(pt_free[jp])
                pr.on("tensor", lambda e, sub=sub, ch=ch, jp=jp: e.transpose(
                    ps_t[jp][:, :], qkf[:, ch, sub * 128:(sub + 1) * 128],
                    cst["ident16"][:, :]), wait=w, inc=[(ctx.TC, 1)])
                t_t = ctx.TC.n
                pr.on("vector", lambda e, sub=sub, ch=ch, jp=jp: e.tensor_copy(
                    qkt[:, sub, ch * 128:(ch + 1) * 128], ps_t[jp][:, :]),
                    wait=[(ctx.TC, t_t)], inc=[(ctx.VC, 1)])
                pt_free[jp] = (ctx.VC, ctx.VC.n)
        v_qkt = ctx.VC.n

        def mk_lk(e, t=t):
            i = None
            for sub in range(4):
                ts_ = t * 4 + sub
                e.tensor_tensor(qg[:, 0:256], qkt[:, sub, :], g3b[:, :], ALU.mult)
                e.tensor_tensor(qh[:, 0:256], qg[:, 0:256], b3b[:, :], ALU.add)
                e.tensor_tensor(rta[:, :], qh[:, 0:128], cosT_sb[:, ts_, :], ALU.mult)
                e.tensor_tensor(rtb[:, :], qh[:, 128:256], sinT_sb[:, ts_, :], ALU.mult)
                e.tensor_tensor(lkr[:, sub, 0:128], rta[:, :], rtb[:, :], ALU.subtract)
                e.tensor_tensor(rta[:, :], qh[:, 128:256], cosT_sb[:, ts_, :], ALU.mult)
                e.tensor_tensor(rtb[:, :], qh[:, 0:128], sinT_sb[:, ts_, :], ALU.mult)
                i = e.tensor_tensor(lkr[:, sub, 128:256], rta[:, :], rtb[:, :], ALU.add)
            return i
        pr.on("vector", mk_lk, wait=[(ctx.VC, v_qkt), (ctx.LG, lg_w)], inc=[(ctx.VC, 1)])
        v_lk = ctx.VC.n

        for sub in range(4):
            for g in range(4):
                j, ps, free = pool.acquire()
                w = [(ctx.VC, v_u), (ctx.LG, lg_w)]
                if free[0] is not None:
                    w.append(free)

                def mmv(e, sub=sub, g=g, ps=ps):
                    for c in range(C):
                        i = e.matmul(ps[:, 0:512], u16[:, c, sub * 128:(sub + 1) * 128],
                                     wv_sb[:, c, g * 512:(g + 1) * 512],
                                     start=(c == 0), stop=(c == C - 1))
                    return i
                pr.on("tensor", mmv, wait=w, inc=[(ctx.TC, 1)])
                t_mm = ctx.TC.n
                pr.on("vector", lambda e, sub=sub, g=g, ps=ps: e.tensor_tensor(
                    vtmp[:, 0:512], ps[:, 0:512], bvb[:, g * 512:(g + 1) * 512], ALU.add),
                    wait=[(ctx.TC, t_mm), (ctx.LD, ld_w), (ctx.SC, ctx.SC.n)],
                    inc=[(ctx.VC, 1)])
                v_vt = ctx.VC.n
                pr.on("scalar", lambda e, sub=sub, g=g: e.activation(
                    vtm[:, sub, g * 512:(g + 1) * 512], vtmp[:, 0:512], AF.Silu),
                    wait=[(ctx.VC, v_vt)], inc=[(ctx.SC, 1)])
                pool.release(j, ctx.SC, ctx.SC.n)
        s_v = ctx.SC.n
        pr.on("sync", lambda e, t=t: e.dma_start(
            out=tm_tile(s_vt, t * NT, NT), in_=vtm[:, :, :]),
            wait=[(ctx.SC, s_v)], inc=[(ctx.ST, 16)])

        for dc in range(2):
            for g in range(4):
                j, ps, free = pool.acquire()
                w = [(ctx.VC, v_lk), (ctx.SC, s_v)]
                if free[0] is not None:
                    w.append(free)

                def mml(e, dc=dc, g=g, ps=ps):
                    for sub in range(4):
                        i = e.matmul(ps[:, 0:512], lkr[:, sub, dc * 128:(dc + 1) * 128],
                                     vtm[:, sub, g * 512:(g + 1) * 512],
                                     start=(sub == 0), stop=(sub == 3))
                    return i
                pr.on("tensor", mml, wait=w, inc=[(ctx.TC, 1)])
                t_mm = ctx.TC.n
                if t == 0:
                    pr.on("vector", lambda e, dc=dc, g=g, ps=ps: e.tensor_copy(
                        lkv[:, dc, g * 512:(g + 1) * 512], ps[:, 0:512]),
                        wait=[(ctx.TC, t_mm)], inc=[(ctx.VC, 1)])
                else:
                    pr.on("vector", lambda e, dc=dc, g=g, ps=ps: e.tensor_tensor(
                        lkv[:, dc, g * 512:(g + 1) * 512],
                        lkv[:, dc, g * 512:(g + 1) * 512], ps[:, 0:512], ALU.add),
                        wait=[(ctx.TC, t_mm)], inc=[(ctx.VC, 1)])
                pool.release(j, ctx.VC, ctx.VC.n)
        buf_ok[0] = [(ctx.TC, ctx.TC.n), (ctx.VC, ctx.VC.n), (ctx.SC, ctx.SC.n),
                     (ctx.ST, ctx.ST.n)]

    pr.on("sync", lambda e: e.dma_start(
        out=s_gates[:, :], in_=gt_out[:, :, :].rearrange("p a b -> p (a b)")),
        wait=[(ctx.VC, None)], inc=[(ctx.ST, 16)])
    pr.on("vector", lambda e: e.tensor_scalar(
        lkv2[:, :, :], lkv[:, :, :], 1.0 / N, None, ALU.mult), inc=[(ctx.VC, 1)])
    v_sc = ctx.VC.n
    pr.on("gpsimd", lambda e: e.dma_start(
        out=cc_in.rearrange("(c p) o -> p c o", p=128), in_=lkv2[:, :, :]),
        wait=[(ctx.VC, v_sc)], inc=[(ctx.LG, 16)])
    lg_cc0 = ctx.LG.n
    pr.on("gpsimd", lambda e: e.collective_compute(
        "AllReduce", ALU.add, [[0, 1], [2, 3], [4, 5], [6, 7]],
        [cc_in.opt()], [cc_out.opt()]),
        wait=[(ctx.LG, lg_cc0)], inc=[(ctx.CC, 1)])
    pr.on("gpsimd", lambda e: e.dma_start(
        out=lkv2[:, :, :], in_=cc_out.rearrange("(c p) o -> p c o", p=128)),
        wait=[(ctx.CC, 1), (ctx.VC, v_sc)], inc=[(ctx.LG, 16)])
    lg_cc = ctx.LG.n
    pr.on("sync", lambda e: e.dma_start(
        out=s_lkv.rearrange("(c p) o -> p c o", p=128), in_=lkv2[:, :, :]),
        wait=[(ctx.LG, lg_cc)], inc=[(ctx.ST, 16)])


def p5(ctx):
    pr, cst = ctx.pr, ctx.consts
    s_x1 = ctx.hscratch("s_x1")
    s_u = ctx.hscratch("s_u")
    s_qkf = ctx.hscratch("s_qkf")
    s_vt = ctx.hscratch("s_vt")
    s_lkv = ctx.hscratch("s_lkv")
    s_x2a = ctx.hscratch("s_x2a")
    wgT = ctx.hin("wgT", [D, 2048])
    woT = ctx.hin("woT", [2048, D])
    bg = ctx.hin("bg_c", [128, 16])
    bo = ctx.hin("bo_c", [128, 8])
    ln2w = ctx.hin("ln2w_c", [128, 8])
    ln2b = ctx.hin("ln2b_c", [128, 8])
    ghc = ctx.hin("gh_c", [128, 8])
    bhc = ctx.hin("bh_c", [128, 8])
    b64 = ctx.hin("bias64", [64, 64])
    cosF = ctx.hin("cosF", [128, T])
    sinF = ctx.hin("sinF", [128, T])

    wg_sb = ctx.sbuf("p5_wg", [128, C, 2048], BF16)
    wo_sb = ctx.sbuf("p5_wo", [128, 16, D], BF16)
    lkv16 = ctx.sbuf("p5_lkv", [128, 2, 2048], BF16)
    cosF_sb = ctx.sbuf("p5_cosF", [128, NT], BF16)
    sinF_sb = ctx.sbuf("p5_sinF", [128, NT], BF16)
    bg_sb = ctx.sbuf("p5_bg", [128, 16], F32)
    bo_sb = ctx.sbuf("p5_bo", [128, 8], F32)
    l2w_sb = ctx.sbuf("p5_l2w", [128, 8], F32)
    l2b_sb = ctx.sbuf("p5_l2b", [128, 8], F32)
    ghc_sb = ctx.sbuf("p5_ghc", [128, 8], F32)
    bhc_sb = ctx.sbuf("p5_bhc", [128, 8], F32)
    b64_sb = ctx.sbuf("p5_b64", [64, 64], F32)
    pr.on("gpsimd", lambda e: e.dma_start(
        out=wg_sb[:, :, :], in_=wgT.rearrange("(c p) o -> p c o", p=128)), inc=[(ctx.LG, 16)])
    pr.on("gpsimd", lambda e: e.dma_start(
        out=wo_sb[:, :, :], in_=woT.rearrange("(c p) o -> p c o", p=128)), inc=[(ctx.LG, 16)])
    pr.on("gpsimd", lambda e: e.dma_start(
        out=lkv16[:, :, :], in_=s_lkv.rearrange("(c p) o -> p c o", p=128)), inc=[(ctx.LG, 16)])

    for nm, src, dst in (("bg", bg, bg_sb), ("bo", bo, bo_sb), ("l2w", ln2w, l2w_sb),
                         ("l2b", ln2b, l2b_sb), ("ghc", ghc, ghc_sb), ("bhc", bhc, bhc_sb),
                         ("b64", b64, b64_sb)):
        pr.on("sync", lambda e, src=src, dst=dst: e.dma_start(out=dst[:, :], in_=src),
              inc=[(ctx.LD, 16)])
    lg_w = ctx.LG.n
    ld_w = ctx.LD.n

    u16 = ctx.sbuf("p5_u", [128, C, NT], BF16)
    qkf = ctx.sbuf("p5_qkf", [128, 2, NT], BF16)
    x1_32 = ctx.sbuf("p5_x1", [128, C, NT], F32)
    vtm = ctx.sbuf("p5_vtm", [64, 8, 2048], BF16)
    qpre = ctx.sbuf("p5_qpre", [128, 2, NT], BF16)
    qqr = ctx.sbuf("p5_qqr", [128, 2, NT], BF16)
    qkkr = ctx.sbuf("p5_qkkr", [128, 2, NT], BF16)
    lqr = ctx.sbuf("p5_lqr", [128, 2, NT], BF16)
    rta = ctx.sbuf("p5_rta", [128, NT], BF16)
    rtb = ctx.sbuf("p5_rtb", [128, NT], BF16)
    gate_t = [ctx.sbuf(f"p5_gate{i}", [128, NT], BF16) for i in range(2)]
    ag = ctx.sbuf("p5_ag", [64, 64], F32)
    rg = ctx.sbuf("p5_rg", [64, 64], F32)
    a2_sb = ctx.sbuf("p5_a2", [64, 8 * 64], BF16)
    ob = ctx.sbuf("p5_ob", [128, 16, NT], BF16)
    at32 = ctx.sbuf("p5_at32", [128, C, NT], F32)
    x2at = ctx.sbuf("p5_x2at", [128, D], F32)
    lnb = {"row": ctx.sbuf("p5_row", [1, 4 * NT], F32),
           "t1": ctx.sbuf("p5_t1", [128, NT], F32),
           "t1b": ctx.sbuf("p5_t1b", [128, NT], F32),
           "st0": ctx.psum("p5_ps_s0", [128, NT]),
           "st1": ctx.psum("p5_ps_s1", [128, NT]),
           "bc0": ctx.psum("p5_ps_b0", [128, NT]),
           "bc1": ctx.psum("p5_ps_b1", [128, NT])}
    ps_tf = ctx.psum("p5_ps_tf", [128, 128], F32)
    pool = ProjPool(ctx, "p5_po", 2, NT)

    buf_ok = [[]]
    tf_free = [(None, 0)]
    for t in range(T // NT):
        w_guard = list(buf_ok[0])
        pr.on("sync", lambda e, t=t: e.dma_start(
            out=u16[:, :, :], in_=fm_tile(s_u, t * NT, NT)),
            wait=w_guard, inc=[(ctx.LD, 16)])
        pr.on("sync", lambda e, t=t: e.dma_start(
            out=qkf[:, :, :],
            in_=s_qkf[:, t * NT:(t + 1) * NT].rearrange("(c p) n -> p c n", p=128)),
            wait=w_guard, inc=[(ctx.LD, 16)])
        pr.on("sync", lambda e, t=t: e.dma_start(
            out=x1_32[:, :, :], in_=fm_tile(s_x1, t * NT, NT)),
            wait=w_guard, inc=[(ctx.LD, 16)])
        pr.on("sync", lambda e, t=t: e.dma_start(
            out=vtm[:, :, :],
            in_=s_vt[t * NT:(t + 1) * NT, :].rearrange("(s p) d -> p s d", p=64)),
            wait=w_guard, inc=[(ctx.LD, 16)])
        pr.on("gpsimd", lambda e, t=t: e.dma_start(
            out=cosF_sb[:, :], in_=cosF[:, t * NT:(t + 1) * NT]),
            wait=w_guard, inc=[(ctx.LG, 16)])
        pr.on("gpsimd", lambda e, t=t: e.dma_start(
            out=sinF_sb[:, :], in_=sinF[:, t * NT:(t + 1) * NT]),
            wait=w_guard, inc=[(ctx.LG, 16)])
        ld_t = ctx.LD.n
        lg_t = ctx.LG.n

        def mk_heads(e, t=t):
            i = None
            for head, dst in ((0, qqr), (2, qkkr), (1, lqr)):
                for ch in range(2):
                    e.tensor_scalar(qpre[:, ch, 0:NT], qkf[:, ch, 0:NT],
                                    ghc_sb[:, head * 2 + ch:head * 2 + ch + 1],
                                    bhc_sb[:, head * 2 + ch:head * 2 + ch + 1],
                                    ALU.mult, ALU.add)
                cs = cosF_sb[:, 0:NT]
                sn = sinF_sb[:, 0:NT]
                e.tensor_tensor(rta[:, 0:NT], qpre[:, 0, 0:NT], cs, ALU.mult)
                e.tensor_tensor(rtb[:, 0:NT], qpre[:, 1, 0:NT], sn, ALU.mult)
                e.tensor_tensor(dst[:, 0, 0:NT], rta[:, 0:NT], rtb[:, 0:NT], ALU.subtract)
                e.tensor_tensor(rta[:, 0:NT], qpre[:, 1, 0:NT], cs, ALU.mult)
                e.tensor_tensor(rtb[:, 0:NT], qpre[:, 0, 0:NT], sn, ALU.mult)
                i = e.tensor_tensor(dst[:, 1, 0:NT], rta[:, 0:NT], rtb[:, 0:NT], ALU.add)
            return i
        pr.on("vector", mk_heads, wait=[(ctx.LD, ld_t), (ctx.LG, lg_t),
                                        (ctx.TC, ctx.TC.n)], inc=[(ctx.VC, 1)])
        v_heads = ctx.VC.n

        for g in range(8):
            j, ps, free = pool.acquire()
            w = [(ctx.VC, v_heads)]
            if free[0] is not None:
                w.append(free)

            def mmq(e, g=g, ps=ps):
                for dc in range(2):
                    i = e.matmul(ps[0:64, 0:64], qkkr[:, dc, g * 64:(g + 1) * 64],
                                 qqr[:, dc, g * 64:(g + 1) * 64],
                                 start=(dc == 0), stop=(dc == 1))
                return i
            pr.on("tensor", mmq, wait=w, inc=[(ctx.TC, 1)])
            t_q = ctx.TC.n

            def ev_q(e, g=g, ps=ps):
                e.scalar_tensor_tensor(ag[0:64, 0:64], ps[0:64, 0:64], 1.0 / 64.0,
                                       b64_sb[0:64, 0:64], ALU.mult, ALU.add)
                e.drain()
                e.tensor_relu(rg[0:64, 0:64], ag[0:64, 0:64])
                e.drain()
                return e.tensor_tensor(a2_sb[0:64, g * 64:(g + 1) * 64],
                                       rg[0:64, 0:64], rg[0:64, 0:64], ALU.mult)
            pr.on("vector", ev_q, wait=[(ctx.TC, t_q), (ctx.LD, ld_w)], inc=[(ctx.VC, 1)])
            pool.release(j, ctx.VC, ctx.VC.n)
        v_a2 = ctx.VC.n

        gate_free = [(None, 0), (None, 0)]
        for ec in range(16):
            gp = ec % 2
            jg, psg, freeg = pool.acquire()
            wg = [(ctx.LD, ld_t), (ctx.LG, lg_w)]
            if freeg[0] is not None:
                wg.append(freeg)

            def mmg(e, ec=ec, psg=psg):
                for c in range(C):
                    i = e.matmul(psg[:, 0:NT], wg_sb[:, c, ec * 128:(ec + 1) * 128],
                                 u16[:, c, 0:NT], start=(c == 0), stop=(c == C - 1))
                return i
            pr.on("tensor", mmg, wait=wg, inc=[(ctx.TC, 1)])
            t_g = ctx.TC.n
            wgv = [(ctx.TC, t_g)]
            if gate_free[gp][0] is not None:
                wgv.append(gate_free[gp])
            pr.on("scalar", lambda e, ec=ec, psg=psg, gp=gp: e.activation(
                gate_t[gp][:, 0:NT], psg[:, 0:NT], AF.Silu, bias=bg_sb[:, ec:ec + 1]),
                wait=wgv, inc=[(ctx.SC, 1)])
            pool.release(jg, ctx.SC, ctx.SC.n)
            s_g = ctx.SC.n

            jo, pso, freeo = pool.acquire()
            w = [(ctx.VC, v_a2), (ctx.LG, lg_w)]
            if freeo[0] is not None:
                w.append(freeo)

            def mmo(e, ec=ec, ps=pso):
                for dc in range(2):
                    e.matmul(ps[:, 0:NT], lkv16[:, dc, ec * 128:(ec + 1) * 128],
                             lqr[:, dc, 0:NT], start=(dc == 0), stop=False)
                for g in range(8):
                    i = e.matmul(ps[:, g * 64:(g + 1) * 64],
                                 vtm[0:64, g, ec * 128:(ec + 1) * 128],
                                 a2_sb[0:64, g * 64:(g + 1) * 64],
                                 start=False, stop=(g == 7))
                return i
            pr.on("tensor", mmo, wait=w, inc=[(ctx.TC, 1)])
            t_o = ctx.TC.n
            pr.on("vector", lambda e, ec=ec, ps=pso, gp=gp: e.scalar_tensor_tensor(
                ob[:, ec, 0:NT], ps[:, 0:NT], 1.0, gate_t[gp][:, 0:NT],
                ALU.mult, ALU.mult),
                wait=[(ctx.TC, t_o), (ctx.SC, s_g)], inc=[(ctx.VC, 1)])
            gate_free[gp] = (ctx.VC, ctx.VC.n)
            pool.release(jo, ctx.VC, ctx.VC.n)
        v_ob = ctx.VC.n

        def ev_wo(e, mo, ps):
            return e.tensor_scalar(at32[:, mo, 0:NT], ps[:, 0:NT],
                                   bo_sb[:, mo:mo + 1], None, ALU.add)
        emit_proj_fm(ctx, wo_sb, ob[:, :, :], NT, pool, [(ctx.VC, v_ob)], ev_wo, C,
                     evac_eng="vector")
        s_at = ctx.VC.n

        v_x2 = emit_ln_fm(ctx, lnb, at32[:, :, :], None, at32[:, :, :],
                          NT, w_col=l2w_sb, b_col=l2b_sb, add_in=x1_32[:, :, :],
                          wait0=[(ctx.VC, s_at), (ctx.TC, ctx.TC.n)], stats_fp32=True)

        for sub in range(4):
            st_prev = ctx.ST.n
            for c in range(C):
                w = [(ctx.VC, v_x2)]
                if tf_free[0][0] is not None:
                    w.append(tf_free[0])
                pr.on("tensor", lambda e, sub=sub, c=c: e.transpose(
                    ps_tf[:, :], at32[:, c, sub * 128:(sub + 1) * 128],
                    cst["ident32"][:, :]), wait=w, inc=[(ctx.TC, 1)])
                t_t = ctx.TC.n
                pr.on("vector", lambda e, sub=sub, c=c: e.tensor_copy(
                    x2at[:, c * 128:(c + 1) * 128], ps_tf[:, :]),
                    wait=[(ctx.TC, t_t), (ctx.ST, st_prev)], inc=[(ctx.VC, 1)])
                tf_free[0] = (ctx.VC, ctx.VC.n)
            v_tr = ctx.VC.n
            pr.on("sync", lambda e, t=t, sub=sub: e.dma_start(
                out=s_x2a[t * NT + sub * 128:t * NT + (sub + 1) * 128, :],
                in_=x2at[:, :]), wait=[(ctx.VC, v_tr)], inc=[(ctx.ST, 16)])
        buf_ok[0] = [(ctx.TC, ctx.TC.n), (ctx.VC, ctx.VC.n), (ctx.SC, ctx.SC.n),
                     (ctx.ST, ctx.ST.n)]


def p6(ctx):
    pr, cst = ctx.pr, ctx.consts
    s_xn1 = ctx.hscratch("s_xn1")
    s_x2a = ctx.hscratch("s_x2a")
    s_gates = ctx.hscratch("s_gates")
    out = ctx.out
    w1T = ctx.hin("w1T", [8, D, 2048])
    w2T = ctx.hin("w2T", [8, 2048, D])
    w3T = ctx.hin("w3T", [8, D, 2048])
    a1T = ctx.hin("a1T", [8, D, 16])
    a2T = ctx.hin("a2T", [8, 2048, 16])
    a3T = ctx.hin("a3T", [8, D, 16])
    b1l = ctx.hin("b1l", [8, 16, 2048])
    b2l = ctx.hin("b2l", [8, 16, D])
    b3l = ctx.hin("b3l", [8, 16, 2048])
    bm1 = ctx.hin("bm1_c", [8, 128, 16])
    bm2r = ctx.hin("bm2_row", [8, D])
    bm3 = ctx.hin("bm3_c", [8, 128, 16])

    xn1_sb = ctx.sbuf("p6_xn1", [128, C, T], BF16)
    gates = ctx.sbuf("p6_gates", [128, 16, 8], F32)
    pr.on("gpsimd", lambda e: e.dma_start(
        out=xn1_sb[:, :, :], in_=s_xn1.rearrange("(c p) n -> p c n", p=128)),
        inc=[(ctx.LG, 16)])
    pr.on("sync", lambda e: e.dma_start(
        out=gates[:, :, :].rearrange("p a b -> p (a b)"), in_=s_gates[:, :]),
        inc=[(ctx.LD, 16)])
    lg_0 = ctx.LG.n
    ld_g = ctx.LD.n

    w1e = ctx.sbuf("p6_w1e", [128, C, 2048], BF16)
    w3e = ctx.sbuf("p6_w3e", [128, C, 2048], BF16)
    w2e = ctx.sbuf("p6_w2e", [128, 16, D], BF16)
    a1e = ctx.sbuf("p6_a1e", [128, C, 16], BF16)
    a3e = ctx.sbuf("p6_a3e", [128, C, 16], BF16)
    a2e = ctx.sbuf("p6_a2e", [128, 16, 16], BF16)
    b1le = ctx.sbuf("p6_b1le", [16, 2048], BF16)
    b3le = ctx.sbuf("p6_b3le", [16, 2048], BF16)
    b2le = ctx.sbuf("p6_b2le", [16, D], BF16)
    bm1e = ctx.sbuf("p6_bm1e", [128, 16], F32)
    bm3e = ctx.sbuf("p6_bm3e", [128, 16], F32)
    bm2b = ctx.sbuf("p6_bm2b", [128, D], BF16)
    al1 = ctx.sbuf("p6_al1", [16, 1024], BF16)
    al3 = ctx.sbuf("p6_al3", [16, 1024], BF16)
    al2 = ctx.sbuf("p6_al2", [16, 1024], BF16)
    glu = ctx.sbuf("p6_glu", [128, 16, 1024], BF16)
    s1t = [ctx.sbuf(f"p6_s1t{i}", [128, 512], BF16) for i in range(2)]
    h3t = [ctx.sbuf(f"p6_h3t{i}", [128, 512], BF16) for i in range(2)]
    ytmp = ctx.sbuf("p6_ytmp", [128, D], F32)
    x2t = [ctx.sbuf(f"p6_x2t{i}", [128, D], F32) for i in range(2)]
    yout = [ctx.sbuf(f"p6_yout{i}", [128, D], F32) for i in range(2)]
    ps_al = [ctx.psum(f"p6_ps_al{i}", [16, 512], F32) for i in range(2)]
    ps_y = [ctx.psum(f"p6_ps_y{i}", [128, D], F32) for i in range(2)]
    pool = ProjPool(ctx, "p6_po", 2, 512)

    lg13 = [0] * 8
    lga = [0] * 8
    lg2 = [0] * 8
    ldb13 = [0] * 8
    ldb2 = [0] * 8

    def load_w13(ex, wait):
        for dst, src in ((w1e, w1T), (w3e, w3T)):
            pr.on("gpsimd", lambda e, dst=dst, src=src, ex=ex: e.dma_start(
                out=dst[:, :, :], in_=src[ex].rearrange("(c p) o -> p c o", p=128)),
                wait=wait, inc=[(ctx.LG, 16)])
        for dst, src in ((b1le, b1l), (b3le, b3l)):
            pr.on("gpsimd", lambda e, dst=dst, src=src, ex=ex: e.dma_start(
                out=dst[:, :], in_=src[ex]), wait=wait, inc=[(ctx.LG, 16)])
        lg13[ex] = ctx.LG.n
        for dst, src in ((bm1e, bm1), (bm3e, bm3)):
            pr.on("sync", lambda e, dst=dst, src=src, ex=ex: e.dma_start(
                out=dst[:, :], in_=src[ex]), wait=wait, inc=[(ctx.LD, 16)])
        ldb13[ex] = ctx.LD.n

    def load_a(ex, wait):
        for dst, src in ((a1e, a1T), (a3e, a3T)):
            pr.on("gpsimd", lambda e, dst=dst, src=src, ex=ex: e.dma_start(
                out=dst[:, :, :], in_=src[ex].rearrange("(c p) o -> p c o", p=128)),
                wait=wait, inc=[(ctx.LG, 16)])
        lga[ex] = ctx.LG.n

    def load_w2(ex, wait):
        pr.on("gpsimd", lambda e, ex=ex: e.dma_start(
            out=w2e[:, :, :], in_=w2T[ex].rearrange("(c p) o -> p c o", p=128)),
            wait=wait, inc=[(ctx.LG, 16)])
        pr.on("gpsimd", lambda e, ex=ex: e.dma_start(
            out=a2e[:, :, :], in_=a2T[ex].rearrange("(c p) o -> p c o", p=128)),
            wait=wait, inc=[(ctx.LG, 16)])
        pr.on("gpsimd", lambda e, ex=ex: e.dma_start(
            out=b2le[:, :], in_=b2l[ex]), wait=wait, inc=[(ctx.LG, 16)])
        lg2[ex] = ctx.LG.n
        pr.on("gpsimd", lambda e, ex=ex: e.dma_start(
            out=bm2b[:, :], in_=bm2r[ex:ex + 1, :].partition_broadcast(128)),
            wait=wait, inc=[(ctx.LG, 16)])
        lg2[ex] = ctx.LG.n
        ldb2[ex] = ctx.LD.n

    load_w13(0, [])
    load_a(0, [])
    load_w2(0, [])

    x2_guard = [[]]
    al_free = [(None, 0), (None, 0)]
    for ex in range(8):
        for hf in range(2):
            base = hf * 1024
            for i4, (asrc, dst, tg) in enumerate(
                    [(a, d, tg) for (a, d) in ((a1e, al1), (a3e, al3)) for tg in range(2)]):
                jb = i4 % 2
                off = base + tg * 512

                def mma(e, asrc=asrc, off=off, jb=jb):
                    for c in range(C):
                        i = e.matmul(ps_al[jb][0:16, 0:512], asrc[:, c, 0:16],
                                     xn1_sb[:, c, off:off + 512],
                                     start=(c == 0), stop=(c == C - 1))
                    return i
                wA = [(ctx.LG, max(lga[ex], lg_0))]
                if al_free[jb][0] is not None:
                    wA.append(al_free[jb])
                if i4 == 0:
                    wA.append((ctx.TC, ctx.TC.n))
                pr.on("tensor", mma, wait=wA, inc=[(ctx.TC, 1)])
                t_a = ctx.TC.n
                pr.on("scalar", lambda e, dst=dst, tg=tg, jb=jb: e.activation(
                    dst[0:16, tg * 512:(tg + 1) * 512], ps_al[jb][0:16, 0:512], AF.Copy),
                    wait=[(ctx.TC, t_a)], inc=[(ctx.SC, 1)])
                al_free[jb] = (ctx.SC, ctx.SC.n)
            s_al = ctx.SC.n
            if hf == 1 and ex < 7:
                load_a(ex + 1, [(ctx.TC, ctx.TC.n)])

            glu_guard = (ctx.TC, ctx.TC.n)
            for hc in range(16):
                for tg in range(2):
                    off = base + tg * 512
                    p2 = (hc * 2 + tg) % 2
                    j1, ps1, free1 = pool.acquire()
                    w1_ = [(ctx.SC, s_al), (ctx.LG, lg13[ex]), (ctx.LD, ldb13[ex])]
                    if free1[0] is not None:
                        w1_.append(free1)

                    def mm1(e, hc=hc, off=off, tg=tg, ps=ps1):
                        for c in range(C):
                            e.matmul(ps[:, 0:512], w1e[:, c, hc * 128:(hc + 1) * 128],
                                     xn1_sb[:, c, off:off + 512], start=(c == 0), stop=False)
                        return e.matmul(ps[:, 0:512], b1le[0:16, hc * 128:(hc + 1) * 128],
                                        al1[0:16, tg * 512:(tg + 1) * 512],
                                        start=False, stop=True)
                    pr.on("tensor", mm1, wait=w1_, inc=[(ctx.TC, 1)])
                    t_1 = ctx.TC.n
                    pr.on("scalar", lambda e, hc=hc, p2=p2, ps=ps1: e.activation(
                        s1t[p2][:, :], ps[:, 0:512], AF.Silu, bias=bm1e[:, hc:hc + 1]),
                        wait=[(ctx.TC, t_1), (ctx.VC, ctx.VC.n)], inc=[(ctx.SC, 1)])
                    pool.release(j1, ctx.SC, ctx.SC.n)
                    s_1 = ctx.SC.n

                    j3, ps3, free3 = pool.acquire()
                    w3_ = [(ctx.SC, s_al), (ctx.LG, lg13[ex]), (ctx.LD, ldb13[ex])]
                    if free3[0] is not None:
                        w3_.append(free3)

                    def mm3(e, hc=hc, off=off, tg=tg, ps=ps3):
                        for c in range(C):
                            e.matmul(ps[:, 0:512], w3e[:, c, hc * 128:(hc + 1) * 128],
                                     xn1_sb[:, c, off:off + 512], start=(c == 0), stop=False)
                        return e.matmul(ps[:, 0:512], b3le[0:16, hc * 128:(hc + 1) * 128],
                                        al3[0:16, tg * 512:(tg + 1) * 512],
                                        start=False, stop=True)
                    pr.on("tensor", mm3, wait=w3_, inc=[(ctx.TC, 1)])
                    t_3 = ctx.TC.n
                    pr.on("vector", lambda e, hc=hc, p2=p2, ps=ps3: e.tensor_scalar(
                        h3t[p2][:, :], ps[:, 0:512], bm3e[:, hc:hc + 1], None, ALU.add),
                        wait=[(ctx.TC, t_3), (ctx.SC, ctx.SC.n)], inc=[(ctx.VC, 1)])
                    pool.release(j3, ctx.VC, ctx.VC.n)
                    v_3 = ctx.VC.n
                    pr.on("vector", lambda e, hc=hc, tg=tg, p2=p2: e.tensor_tensor(
                        glu[:, hc, tg * 512:(tg + 1) * 512], s1t[p2][:, :],
                        h3t[p2][:, :], ALU.mult),
                        wait=[(ctx.VC, v_3), (ctx.SC, s_1), glu_guard], inc=[(ctx.VC, 1)])
            v_glu = ctx.VC.n
            if hf == 1 and ex < 7:
                load_w13(ex + 1, [(ctx.TC, ctx.TC.n), (ctx.SC, ctx.SC.n),
                                  (ctx.VC, ctx.VC.n)])

            for tg in range(2):
                jb = tg % 2

                def mma2(e, tg=tg, jb=jb):
                    for hc in range(16):
                        i = e.matmul(ps_al[jb][0:16, 0:512], a2e[:, hc, 0:16],
                                     glu[:, hc, tg * 512:(tg + 1) * 512],
                                     start=(hc == 0), stop=(hc == 15))
                    return i
                wC = [(ctx.VC, v_glu), (ctx.LG, lg2[ex])]
                if al_free[jb][0] is not None:
                    wC.append(al_free[jb])
                pr.on("tensor", mma2, wait=wC, inc=[(ctx.TC, 1)])
                t_a2 = ctx.TC.n
                pr.on("scalar", lambda e, tg=tg, jb=jb: e.activation(
                    al2[0:16, tg * 512:(tg + 1) * 512], ps_al[jb][0:16, 0:512], AF.Copy),
                    wait=[(ctx.TC, t_a2)], inc=[(ctx.SC, 1)])
                al_free[jb] = (ctx.SC, ctx.SC.n)
            s_al2 = ctx.SC.n

            x2t_guard = [(None, 0), (None, 0)]
            yps_free = [(None, 0), (None, 0)]
            yout_free = [0, 0]
            for tc in range(8):
                gch = hf * 8 + tc
                p2 = tc % 2
                gbase = base + tc * 128
                wx = [(ctx.ST, None)] + x2_guard[0]
                if x2t_guard[p2][0] is not None:
                    wx.append(x2t_guard[p2])
                pr.on("sync", lambda e, gbase=gbase, p2=p2: e.dma_start(
                    out=x2t[p2][:, :], in_=s_x2a[gbase:gbase + 128, :]),
                    wait=wx, inc=[(ctx.LD, 16)])
                ld_x2 = ctx.LD.n

                def mmy(e, tc=tc, p2=p2):
                    i = None
                    for ng in range(2):
                        for hc in range(16):
                            e.matmul(ps_y[p2][:, ng * 512:(ng + 1) * 512],
                                     glu[:, hc, tc * 128:(tc + 1) * 128],
                                     w2e[:, hc, ng * 512:(ng + 1) * 512],
                                     start=(hc == 0), stop=False)
                        i = e.matmul(ps_y[p2][:, ng * 512:(ng + 1) * 512],
                                     al2[0:16, tc * 128:(tc + 1) * 128],
                                     b2le[0:16, ng * 512:(ng + 1) * 512],
                                     start=False, stop=True)
                    return i
                wy = [(ctx.SC, s_al2), (ctx.LG, lg2[ex]), (ctx.LD, ldb2[ex])]
                if yps_free[p2][0] is not None:
                    wy.append(yps_free[p2])
                if tc == 0:
                    wy.append((ctx.VC, ctx.VC.n))
                pr.on("tensor", mmy, wait=wy, inc=[(ctx.TC, 1)])
                t_y = ctx.TC.n

                def ev_y(e, gch=gch, p2=p2, ex=ex):
                    e.tensor_tensor(ytmp[:, :], ps_y[p2][:, 0:D], bm2b[:, :], ALU.add)
                    return e.scalar_tensor_tensor(
                        yout[p2][:, :], ytmp[:, :], gates[:, gch, ex:ex + 1],
                        x2t[p2][:, :], ALU.mult, ALU.add)
                pr.on("vector", ev_y, wait=[(ctx.TC, t_y), (ctx.LD, max(ld_x2, ld_g)),
                                            (ctx.ST, yout_free[p2])],
                      inc=[(ctx.VC, 1)])
                v_y = ctx.VC.n
                x2t_guard[p2] = (ctx.VC, v_y)
                yps_free[p2] = (ctx.VC, v_y)
                dst = out if ex == 7 else s_x2a
                pr.on("sync", lambda e, gbase=gbase, p2=p2, dst=dst: e.dma_start(
                    out=dst[gbase:gbase + 128, :], in_=yout[p2][:, :]),
                    wait=[(ctx.VC, v_y)], inc=[(ctx.ST, 16)])
                yout_free[p2] = ctx.ST.n
            x2_guard[0] = [(ctx.VC, ctx.VC.n)]
            if hf == 1 and ex < 7:
                load_w2(ex + 1, [(ctx.TC, ctx.TC.n), (ctx.VC, ctx.VC.n)])


PHASE_EMITTERS = {"p0": p0, "p1": p1, "p2": p2, "p3": p3, "p4": p4, "p5": p5, "p6": p6}



LAUNCH_GROUPS = [["p0", "p1", "p2", "p3", "p4", "p5", "p6"]]
NCORES = 8


def kernel(**inputs):
    per_core = prep_inputs(inputs)
    env = [dict(d) for d in per_core]
    for grp in LAUNCH_GROUPS:
        nc, in_names = build_program(grp)
        in_maps = [{k: np.ascontiguousarray(env[c][k]) for k in in_names}
                   for c in range(NCORES)]
        res = run_bass_kernel_spmd(nc, in_maps, core_ids=list(range(NCORES)))
        for c in range(NCORES):
            for k, v in res.results[c].items():
                env[c][k] = v
    out = np.zeros((B, N, D), np.float32)
    for c in range(NCORES):
        b, h = c // 2, c % 2
        out[b, own_token_idx(h), :] = env[c]["out_x2"]
    return out


# version 06
# speedup vs baseline: 1.1032x; 1.0510x over previous
import contextlib
import sys
if "/opt/trn_rl_repo" not in sys.path:
    sys.path.insert(0, "/opt/trn_rl_repo")
import numpy as np
import ml_dtypes
import concourse.bass as bass
import concourse.mybir as mybir
from concourse.bass_utils import run_bass_kernel_spmd

F32 = mybir.dt.float32
BF16 = mybir.dt.bfloat16
AF = mybir.ActivationFunctionType
ALU = mybir.AluOpType
AX = mybir.AxisListType

B, N, D = 4, 4096, 1024
T = 2048
NT = 512
C = D // 128
NT3 = 256
NEG = -1.0e9
EPS = 1e-5
BF = ml_dtypes.bfloat16


def _cols(v):
    v = np.asarray(v, np.float32)
    return np.ascontiguousarray(v.reshape(-1, 128).T)


def own_token_idx(h):
    return np.concatenate([np.arange(128) + 128 * (2 * i + h) for i in range(16)])


def prep_inputs(inp):
    f = {k: np.asarray(v, np.float32) for k, v in inp.items()}
    x = f["L0_attn_x"]

    sh = {}
    sh["wq"] = f["L0_attn_Wq"]
    sh["wk"] = f["L0_attn_Wk"]
    sh["wv"] = f["L0_attn_Wv"]
    sh["wfc"] = f["L0_attn_Wfc"]
    sh["bfc_c"] = _cols(f["L0_attn_bfc"])
    sh["g0_row"] = np.ascontiguousarray(f["L0_attn_gamma"][None, :])
    sh["b0_row"] = np.ascontiguousarray(f["L0_attn_beta"][None, :])
    sh["w1"] = f["L0_ffn_W1"]
    sh["b1_c"] = _cols(f["L0_ffn_b1"])
    sh["w2"] = f["L0_ffn_W2"]
    sh["b2_c"] = _cols(f["L0_ffn_b2"])
    sh["ln1w_c"] = _cols(f["L1_attn_ln1_w"])
    sh["ln1b_c"] = _cols(f["L1_attn_ln1_b"])
    sh["wqkT"] = np.ascontiguousarray(f["L1_attn_Wqk"].T)
    sh["bqk_c"] = _cols(f["L1_attn_bqk"])
    sh["wvT1"] = np.ascontiguousarray(f["L1_attn_Wv"].T)
    sh["bv_row"] = np.ascontiguousarray(f["L1_attn_bv"][None, :])
    sh["wgT"] = np.ascontiguousarray(f["L1_attn_Wg"].T)
    sh["bg_row"] = np.ascontiguousarray(f["L1_attn_bg"][None, :])
    sh["bg_c"] = _cols(f["L1_attn_bg"])
    sh["woT"] = np.ascontiguousarray(f["L1_attn_Wo"].T)
    sh["bo_c"] = _cols(f["L1_attn_bo"])
    sh["ln2w_c"] = _cols(f["L1_attn_ln2_w"])
    sh["ln2b_c"] = _cols(f["L1_attn_ln2_b"])
    gam, bet = f["L1_attn_gamma"], f["L1_attn_beta"]
    sh["gh_c"] = np.ascontiguousarray(gam.reshape(4, 2, 128).transpose(2, 0, 1).reshape(128, 8))
    sh["bh_c"] = np.ascontiguousarray(bet.reshape(4, 2, 128).transpose(2, 0, 1).reshape(128, 8))
    sh["g3_row"] = np.ascontiguousarray(gam[3][None, :])
    sh["b3_row"] = np.ascontiguousarray(bet[3][None, :])
    idx = np.arange(64)
    dist = np.clip(np.abs(idx[:, None] - idx[None, :]), 0, 32)
    sh["bias64"] = np.ascontiguousarray(f["L1_attn_pos_emb"][dist]).astype(np.float32)
    sh["wgmT"] = np.ascontiguousarray(f["L1_ffn_Wg"].T)
    sh["w1T"] = np.ascontiguousarray(f["L1_ffn_W1"].transpose(0, 2, 1))
    sh["w2T"] = np.ascontiguousarray(f["L1_ffn_W2"].transpose(0, 2, 1))
    sh["w3T"] = np.ascontiguousarray(f["L1_ffn_W3"].transpose(0, 2, 1))
    sh["a1T"] = np.ascontiguousarray(f["L1_ffn_A1"].transpose(0, 2, 1))
    sh["a2T"] = np.ascontiguousarray(f["L1_ffn_A2"].transpose(0, 2, 1))
    sh["a3T"] = np.ascontiguousarray(f["L1_ffn_A3"].transpose(0, 2, 1))
    sh["b1l"] = np.ascontiguousarray(f["L1_ffn_B1"].transpose(0, 2, 1))
    sh["b2l"] = np.ascontiguousarray(f["L1_ffn_B2"].transpose(0, 2, 1))
    sh["b3l"] = np.ascontiguousarray(f["L1_ffn_B3"].transpose(0, 2, 1))
    sh["bm1_c"] = np.ascontiguousarray(np.stack([_cols(v) for v in f["L1_ffn_b1"]]))
    sh["bm2_c"] = np.ascontiguousarray(np.stack([_cols(v) for v in f["L1_ffn_b2"]]))
    sh["bm3_c"] = np.ascontiguousarray(np.stack([_cols(v) for v in f["L1_ffn_b3"]]))
    sh["bm2_row"] = np.ascontiguousarray(f["L1_ffn_b2"])
    sh["onesf"] = np.ones((1, 128), np.float32)
    sh["ident"] = np.eye(128, dtype=np.float32)

    half = 128
    inv_freq = (10000.0 ** (-np.arange(half, dtype=np.float32) / half)).astype(np.float32)

    ins = []
    for c in range(8):
        b, h = c // 2, c % 2
        d = dict(sh)
        oi = own_token_idx(h)
        xt = np.ascontiguousarray(x[b].T)
        d["xT_all"] = xt
        d["xT_own"] = np.ascontiguousarray(xt[:, oi])
        masks = np.zeros((16, 2, 128, 128), np.float32)
        vsel = np.zeros((16, 2, 128, 128), np.float32)
        for i in range(16):
            g = 2 * i + h
            for jj, j in enumerate((2 * i, 2 * i + 1)):
                kk = np.arange(128)[None, :] + 128 * j
                qq = np.arange(128)[:, None] + 128 * g
                masks[i, jj] = np.where(kk <= qq, 0.0, NEG)
                if j == g:
                    vsel[i, jj] = np.eye(128, dtype=np.float32)
        d["masks"] = masks
        d["vsel"] = vsel.astype(BF)
        pos = oi.astype(np.float32)
        sinu = pos[:, None] * inv_freq[None, :]
        d["cosF"] = np.ascontiguousarray(np.cos(sinu).T.astype(np.float32))
        d["sinF"] = np.ascontiguousarray(np.sin(sinu).T.astype(np.float32))
        d["cosT"] = np.ascontiguousarray(np.cos(sinu).astype(np.float32))
        d["sinT"] = np.ascontiguousarray(np.sin(sinu).astype(np.float32))
        ins.append(d)
    return ins



ENGS = ("sync", "tensor", "vector", "scalar", "gpsimd")


class SemC:
    __slots__ = ("h", "n", "name")
    def __init__(self, h, name):
        self.h, self.n, self.name = h, 0, name


class Prog:
    def __init__(self, nc):
        self.nc = nc
        self.ops = {e: [] for e in ENGS}

    def sem(self, name):
        return SemC(self.nc.alloc_semaphore(name), name)

    def on(self, eng, fn, wait=(), inc=()):
        w = [(s, (s.n if v is None else v)) for (s, v) in wait]
        self.ops[eng].append((w, fn, list(inc)))
        for s, k in inc:
            s.n += k

    def emit(self, block):
        for ename in ENGS:
            ops = self.ops[ename]
            def section(eng, ops=ops):
                last_wait = {}
                for (waits, fn, incs) in ops:
                    for s, v in waits:
                        if v > 0 and last_wait.get(id(s), 0) < v:
                            eng.wait_ge(s.h, v)
                            last_wait[id(s)] = v
                    instr = fn(eng) if fn is not None else None
                    if incs:
                        if instr is None:
                            instr = eng.nop()
                        for s, k in incs:
                            instr.then_inc(s.h, k)
            getattr(block, ename)(section)


class Ctx:
    def __init__(self, nc, pr, hin, hscratch, out):
        self.nc, self.pr, self.hin, self.hscratch, self.out = nc, pr, hin, hscratch, out
        self.stack = contextlib.ExitStack()
        self.pstack = None
        self.LD = pr.sem("LD")
        self.LG = pr.sem("LG")
        self.ST = pr.sem("ST")
        self.TC = pr.sem("TC")
        self.VC = pr.sem("VC")
        self.SC = pr.sem("SC")
        self.CC = pr.sem("CC")
        self.consts = {}

    def sbuf(self, name, shape, dtype):
        st = self.pstack if self.pstack is not None else self.stack
        return st.enter_context(self.nc.sbuf_tensor(name, list(shape), dtype))

    def psum(self, name, shape, dtype=F32):
        st = self.pstack if self.pstack is not None else self.stack
        return st.enter_context(self.nc.psum_tensor(name, list(shape), dtype))

    def barrier(self):
        sems = [self.LD, self.LG, self.ST, self.TC, self.VC, self.SC, self.CC]
        w = [(s, None) for s in sems]
        for e in ENGS:
            self.pr.on(e, None, wait=w)


def fm_tile(hbm_ap, t0, nt):
    return hbm_ap[:, t0:t0 + nt].rearrange("(c p) n -> p c n", p=128)


def tm_tile(hbm_ap, t0, nt):
    return hbm_ap[t0:t0 + nt, :].rearrange("(s p) d -> p s d", p=128)



def emit_ln_fm(ctx, bufs, x32v, x16v, xn16v, nt, w_col=None, b_col=None, t2_out=None,
               wait0=(), add_in=None, stats_fp32=False):
    pr, cst = ctx.pr, ctx.consts
    nC = x32v.shape[1]
    inv = 1.0 / (128 * nC)
    sq, rowb, t1 = bufs.get("sq"), bufs["row"], bufs["t1"]
    t1b = bufs.get("t1b", t1)
    row = lambda k: rowb[0:1, k * nt:(k + 1) * nt]
    ps_m, ps_v, ps_mb, ps_rb = bufs["st0"], bufs["st1"], bufs["bc0"], bufs["bc1"]

    if stats_fp32:
        base_w = list(wait0) + [(ctx.LD, None), (ctx.LG, None)]
        def mm_x(e):
            for c in range(nC):
                i = e.matmul(ps_m[0:1, 0:nt], cst["ones32c"][:, 0:1], x32v[:, c, :],
                             start=(c == 0), stop=(c == nC - 1))
            return i
        pr.on("tensor", mm_x, wait=base_w, inc=[(ctx.TC, 1)])
        for c in range(nC):
            pr.on("vector", lambda e, c=c: e.tensor_tensor(
                t1b[:, 0:nt], x32v[:, c, :], x32v[:, c, :], ALU.mult),
                wait=base_w + [(ctx.TC, ctx.TC.n)], inc=[(ctx.VC, 1)])
            v_c = ctx.VC.n
            pr.on("tensor", lambda e, c=c: e.matmul(
                ps_v[0:1, 0:nt], cst["ones32c"][:, 0:1], t1b[:, 0:nt],
                start=(c == 0), stop=(c == nC - 1)), wait=[(ctx.VC, v_c)],
                inc=[(ctx.TC, 1)])
        t_st = ctx.TC.n
    else:
        pr.on("vector", lambda e: e.tensor_tensor(sq[:, 0:nC, 0:nt], x32v, x32v, ALU.mult),
              wait=list(wait0) + [(ctx.LD, None), (ctx.LG, None)], inc=[(ctx.VC, 1)])
        v_sq = ctx.VC.n

        def mm_stats(e):
            for c in range(nC):
                e.matmul(ps_m[0:1, 0:nt], cst["ones16c"][:, 0:1], x16v[:, c, :],
                         start=(c == 0), stop=(c == nC - 1))
            for c in range(nC):
                i = e.matmul(ps_v[0:1, 0:nt], cst["ones16c"][:, 0:1], sq[:, c, 0:nt],
                             start=(c == 0), stop=(c == nC - 1))
            return i
        pr.on("tensor", mm_stats, wait=[(ctx.VC, v_sq), (ctx.LD, None), (ctx.LG, None)],
              inc=[(ctx.TC, 1)])
        t_st = ctx.TC.n

    def row_math(e):
        e.tensor_scalar(row(0), ps_m[0:1, 0:nt], inv, None, ALU.mult)
        e.tensor_tensor(row(3), row(0), row(0), ALU.mult)
        e.scalar_tensor_tensor(row(1), ps_v[0:1, 0:nt], inv,
                               row(3), ALU.mult, ALU.subtract)
        return e.tensor_scalar(row(3), row(1), EPS, None, ALU.add)
    pr.on("vector", row_math, wait=[(ctx.TC, t_st)], inc=[(ctx.VC, 1)])
    v_row = ctx.VC.n

    def row_rstd(e):
        e.activation(row(1), row(3), AF.Ln)
        return e.activation(row(2), row(1), AF.Exp, scale=-0.5)
    pr.on("scalar", row_rstd, wait=[(ctx.VC, v_row)], inc=[(ctx.SC, 1)])
    s_r = ctx.SC.n

    def mm_bcast(e):
        e.matmul(ps_mb[:, 0:nt], cst["ones32"][0:1, :], row(0), start=True, stop=True)
        return e.matmul(ps_rb[:, 0:nt], cst["ones32"][0:1, :], row(2), start=True, stop=True)
    pr.on("tensor", mm_bcast, wait=[(ctx.SC, s_r), (ctx.VC, v_row)], inc=[(ctx.TC, 1)])
    t_bc = ctx.TC.n

    def apply(e):
        for c in range(nC):
            e.tensor_tensor(t1[:, 0:nt], x32v[:, c, :], ps_mb[:, 0:nt], ALU.subtract)
            if w_col is not None and add_in is not None:
                e.tensor_tensor(t1b[:, 0:nt], t1[:, 0:nt], ps_rb[:, 0:nt], ALU.mult)
                e.tensor_scalar(t1[:, 0:nt], t1b[:, 0:nt],
                                w_col[:, c:c + 1], b_col[:, c:c + 1], ALU.mult, ALU.add)
                i = e.tensor_tensor(xn16v[:, c, :], t1[:, 0:nt], add_in[:, c, :], ALU.add)
            elif w_col is not None:
                e.tensor_tensor(t1b[:, 0:nt], t1[:, 0:nt], ps_rb[:, 0:nt], ALU.mult)
                i = e.tensor_scalar(xn16v[:, c, :], t1b[:, 0:nt],
                                    w_col[:, c:c + 1], b_col[:, c:c + 1], ALU.mult, ALU.add)
            elif t2_out is not None:
                e.tensor_tensor(t2_out[:, c, :], t1[:, 0:nt], ps_rb[:, 0:nt], ALU.mult)
                i = e.tensor_copy(xn16v[:, c, :], t2_out[:, c, :])
            else:
                i = e.tensor_tensor(xn16v[:, c, :], t1[:, 0:nt], ps_rb[:, 0:nt], ALU.mult)
        return i
    pr.on("vector", apply, wait=[(ctx.TC, t_bc)], inc=[(ctx.VC, 1)])
    return ctx.VC.n


class ProjPool:
    def __init__(self, ctx, name, n, nt, dtype=F32):
        self.ctx = ctx
        self.tiles = [ctx.psum(f"{name}{i}", [128, nt], dtype) for i in range(n)]
        self.free_at = [(None, 0)] * n
        self.i = 0

    def acquire(self):
        j = self.i % len(self.tiles)
        self.i += 1
        return j, self.tiles[j], self.free_at[j]

    def release(self, j, sem, count):
        self.free_at[j] = (sem, count)


def emit_proj_fm(ctx, w_sb, act16, nt, pool, ready, evac, n_out_chunks, extra_wait=(),
                 evac_eng="scalar", evac_wait=()):
    pr = ctx.pr
    nC = act16.shape[1]
    for mo in range(n_out_chunks):
        j, ps, free = pool.acquire()
        w = list(ready) + list(extra_wait)
        if free[0] is not None:
            w.append((free[0], free[1]))

        def mm(e, mo=mo, ps=ps):
            for c in range(nC):
                i = e.matmul(ps[:, 0:nt], w_sb[:, c, mo * 128:(mo + 1) * 128],
                             act16[:, c, 0:nt], start=(c == 0), stop=(c == nC - 1))
            return i
        pr.on("tensor", mm, wait=w, inc=[(ctx.TC, 1)])
        t_mm = ctx.TC.n
        esem = ctx.SC if evac_eng == "scalar" else ctx.VC
        pr.on(evac_eng, lambda e, mo=mo, ps=ps: evac(e, mo, ps),
              wait=[(ctx.TC, t_mm)] + list(evac_wait), inc=[(esem, 1)])
        pool.release(j, esem, esem.n)
    return esem.n



SCRATCH = {
    "s_xn": ([D, T], BF16, "p0", ["p3"]),
    "s_q": ([D, T], BF16, "p0", ["p1"]),
    "s_k": ([D, N], BF16, "p0", ["p1"]),
    "s_v": ([N, D], BF16, "p0", ["p1"]),
    "s_af": ([T, D], BF16, "p1", ["p2"]),
    "s_x1a": ([D, T], F32, "p2", ["p3"]),
    "s_x1": ([D, T], F32, "p3", ["p4", "p5"]),
    "s_xn1": ([D, T], BF16, "p4", ["p6"]),
    "s_u": ([D, T], BF16, "p4", ["p5"]),
    "s_qkf": ([256, T], BF16, "p4", ["p5"]),
    "s_vt": ([T, 2048], BF16, "p4", ["p5"]),
    "s_lkv": ([256, 2048], F32, "p4", ["p5"]),
    "s_x2a": ([T, D], F32, "p5", ["p6"]),
    "s_gates": ([128, 128], F32, "p4", ["p6"]),
}


def build_program(phases):
    nc = bass.Bass("TRN2", target_bir_lowering=False)
    pr = Prog(nc)
    hbm = {}
    in_names = []

    def hin(name, shape, dtype=F32):
        if name not in hbm:
            hbm[name] = nc.dram_tensor(name, list(shape), dtype, kind="ExternalInput").ap()
            in_names.append(name)
        return hbm[name]

    def hscratch(name):
        if name not in hbm:
            shape, dt, prod, cons = SCRATCH[name]
            if prod in phases:
                kind = "Internal" if all(cp in phases for cp in cons) else "ExternalOutput"
            else:
                kind = "ExternalInput"
                in_names.append(name)
            hbm[name] = nc.dram_tensor(name, list(shape), dt, kind=kind).ap()
        return hbm[name]

    out = None
    if "p6" in phases:
        out = nc.dram_tensor("out_x2", [T, D], F32, kind="ExternalOutput").ap()

    ctx = Ctx(nc, pr, hin, hscratch, out)
    with ctx.stack:
        _emit_prolog(ctx)
        for ph in phases:
            ctx.barrier()
            with contextlib.ExitStack() as ps:
                ctx.pstack = ps
                PHASE_EMITTERS[ph](ctx)
                ctx.pstack = None
        _emit_epilog(ctx)
        with nc.Block() as block:
            pr.emit(block)
    return nc, in_names


def _emit_prolog(ctx):
    pr, hin = ctx.pr, ctx.hin
    cst = ctx.consts
    cst["ones32"] = ctx.sbuf("c_ones32", [1, 128], F32)
    cst["ones16"] = ctx.sbuf("c_ones16", [1, 128], BF16)
    cst["ones16c"] = ctx.sbuf("c_ones16c", [128, 1], BF16)
    cst["ones32c"] = ctx.sbuf("c_ones32c", [128, 1], F32)
    cst["ident32"] = ctx.sbuf("c_ident32", [128, 128], F32)
    cst["ident16"] = ctx.sbuf("c_ident16", [128, 128], BF16)
    onesf = hin("onesf", [1, 128])
    ident = hin("ident", [128, 128])
    pr.on("sync", lambda e: e.dma_start(out=cst["ones32"][:, :], in_=onesf), inc=[(ctx.LD, 16)])
    pr.on("sync", lambda e: e.dma_start(out=cst["ident32"][:, :], in_=ident), inc=[(ctx.LD, 16)])
    pr.on("gpsimd", lambda e: e.dma_start(out=cst["ones16"][:, :], in_=onesf), inc=[(ctx.LG, 16)])
    pr.on("gpsimd", lambda e: e.dma_start(
        out=cst["ones16c"][:, :], in_=onesf[0:1, 0:1].partition_broadcast(128)),
        inc=[(ctx.LG, 16)])
    pr.on("sync", lambda e: e.dma_start(
        out=cst["ones32c"][:, :], in_=onesf[0:1, 0:1].partition_broadcast(128)),
        inc=[(ctx.LD, 16)])
    pr.on("gpsimd", lambda e: e.dma_start(out=cst["ident16"][:, :], in_=ident), inc=[(ctx.LG, 16)])


def _emit_epilog(ctx):
    pr = ctx.pr
    pr.on("sync", None, wait=[(ctx.ST, None), (ctx.LD, None)])
    pr.on("gpsimd", None, wait=[(ctx.ST, None), (ctx.LG, None)])



def p0(ctx):
    pr = ctx.pr
    xa = ctx.hin("xT_all", [D, N])
    xo = ctx.hin("xT_own", [D, T])
    s_k, s_v = ctx.hscratch("s_k"), ctx.hscratch("s_v")
    s_q, s_xn = ctx.hscratch("s_q"), ctx.hscratch("s_xn")

    w_sb = {}
    for nm in ("wq", "wk", "wv"):
        src = ctx.hin(nm, [D, D])
        w_sb[nm] = ctx.sbuf(f"p0_{nm}", [128, C, D], BF16)
        pr.on("gpsimd", lambda e, dst=w_sb[nm], src=src: e.dma_start(
            out=dst[:, :, :], in_=src.rearrange("(c p) o -> p c o", p=128)),
            inc=[(ctx.LG, 16)])

    xa32 = [ctx.sbuf(f"p0_xa32_{i}", [128, C, NT], F32) for i in range(2)]
    xa16 = [ctx.sbuf(f"p0_xa16_{i}", [128, C, NT], BF16) for i in range(2)]
    ln_ps = {"st0": ctx.psum("p0_ps_s0", [128, NT]),
             "st1": ctx.psum("p0_ps_s1", [128, NT]),
             "bc0": ctx.psum("p0_ps_b0", [128, NT]),
             "bc1": ctx.psum("p0_ps_b1", [128, NT])}
    lnb = [dict(ln_ps,
                sq=ctx.sbuf(f"p0_sq_{i}", [128, C, NT], BF16),
                row=ctx.sbuf(f"p0_row_{i}", [1, 4 * NT], F32),
                t1=ctx.sbuf(f"p0_t1_{i}", [128, NT], F32)) for i in range(2)]
    xn16 = [ctx.sbuf(f"p0_xn_{i}", [128, C, NT], BF16) for i in range(2)]
    kbuf = [ctx.sbuf(f"p0_kb_{i}", [128, C, NT], BF16) for i in range(2)]
    vbuf = [ctx.sbuf(f"p0_vb_{i}", [128, NT // 128, D], BF16) for i in range(2)]
    pool = ProjPool(ctx, "p0_po", 2, NT)

    buf_ok = [[], []]

    def do_tile(src, t, own):
        p = t % 2
        w_guard = list(buf_ok[p])
        pr.on("sync", lambda e, t=t, p=p, src=src: e.dma_start(
            out=xa32[p][:, :, :], in_=fm_tile(src, t * NT, NT)),
            wait=w_guard, inc=[(ctx.LD, 16)])
        ld_p0 = ctx.LD.n
        pr.on("vector", lambda e, p=p: e.tensor_copy(xa16[p][:, :, :], xa32[p][:, :, :]),
              wait=w_guard + [(ctx.LD, ld_p0)], inc=[(ctx.VC, 1)])
        v_xn = emit_ln_fm(ctx, lnb[p], xa32[p][:, :, :], xa16[p][:, :, :],
                          xn16[p][:, :, :], NT, wait0=w_guard)
        ready = [(ctx.VC, v_xn)]
        st_marks = []
        if own:
            pr.on("gpsimd", lambda e, t=t, p=p: e.dma_start(
                out=fm_tile(s_xn, t * NT, NT), in_=xn16[p][:, :, :]),
                wait=ready, inc=[(ctx.ST, 16)])
            st_marks.append(ctx.ST.n)

            def ev_q(e, mo, ps, p=p):
                return e.activation(kbuf[p][:, mo, 0:NT], ps[:, 0:NT], AF.Copy,
                                    scale=1.0 / 32.0)
            emit_proj_fm(ctx, w_sb["wq"], xn16[p][:, :, :], NT, pool, ready, ev_q, C)
            s_done = ctx.SC.n
            pr.on("gpsimd", lambda e, t=t, p=p: e.dma_start(
                out=fm_tile(s_q, t * NT, NT), in_=kbuf[p][:, :, :]),
                wait=[(ctx.SC, s_done)], inc=[(ctx.ST, 16)])
            st_marks.append(ctx.ST.n)
        else:
            def ev_k(e, mo, ps, p=p):
                return e.activation(kbuf[p][:, mo, 0:NT], ps[:, 0:NT], AF.Copy)
            emit_proj_fm(ctx, w_sb["wk"], xn16[p][:, :, :], NT, pool, ready, ev_k, C)
            s_done = ctx.SC.n
            pr.on("gpsimd", lambda e, t=t, p=p: e.dma_start(
                out=fm_tile(s_k, t * NT, NT), in_=kbuf[p][:, :, :]),
                wait=[(ctx.SC, s_done)], inc=[(ctx.ST, 16)])
            st_marks.append(ctx.ST.n)
            for sub in range(NT // 128):
                for half in range(2):
                    j, ps, free = pool.acquire()
                    w = list(ready) + ([free] if free[0] is not None else [])

                    def mmv(e, sub=sub, half=half, ps=ps, p=p):
                        for c in range(C):
                            i = e.matmul(ps[:, 0:NT],
                                         xn16[p][:, c, sub * 128:(sub + 1) * 128],
                                         w_sb["wv"][:, c, half * NT:(half + 1) * NT],
                                         start=(c == 0), stop=(c == C - 1))
                        return i
                    pr.on("tensor", mmv, wait=w, inc=[(ctx.TC, 1)])
                    t_mm = ctx.TC.n
                    pr.on("scalar", lambda e, sub=sub, half=half, ps=ps, p=p: e.activation(
                        vbuf[p][:, sub, half * NT:(half + 1) * NT], ps[:, 0:NT], AF.Copy),
                        wait=[(ctx.TC, t_mm)], inc=[(ctx.SC, 1)])
                    pool.release(j, ctx.SC, ctx.SC.n)
            s_done = ctx.SC.n
            pr.on("gpsimd", lambda e, t=t, p=p: e.dma_start(
                out=tm_tile(s_v, t * NT, NT), in_=vbuf[p][:, :, :]),
                wait=[(ctx.SC, s_done)], inc=[(ctx.ST, 16)])
            st_marks.append(ctx.ST.n)
        buf_ok[p] = [(ctx.TC, ctx.TC.n), (ctx.VC, ctx.VC.n), (ctx.SC, ctx.SC.n),
                     (ctx.ST, max(st_marks))]

    for t in range(N // NT):
        do_tile(xa, t, own=False)
    for t in range(T // NT):
        do_tile(xo, t, own=True)


P1_DEBUG = False


def p1(ctx):
    pr = ctx.pr
    cst = ctx.consts
    s_k, s_v = ctx.hscratch("s_k"), ctx.hscratch("s_v")
    s_q, s_af = ctx.hscratch("s_q"), ctx.hscratch("s_af")
    masks = ctx.hin("masks", [16, 2, 128, 128])
    vsel = ctx.hin("vsel", [16, 2, 128, 128], BF16)
    g0r = ctx.hin("g0_row", [1, D])
    b0r = ctx.hin("b0_row", [1, D])

    k_sb = ctx.sbuf("p1_k", [128, C, N], BF16)
    v_sb = ctx.sbuf("p1_v", [128, N // 128, D], BF16)
    q_sb = [ctx.sbuf(f"p1_q{i}", [128, C, 128], BF16) for i in range(2)]
    mk_sb = [ctx.sbuf(f"p1_mk{i}", [128, 2, 128], F32) for i in range(2)]
    vs_sb = ctx.sbuf("p1_vs", [128, 2, 128], BF16)
    vs_guard = [0]
    srow_b = [ctx.sbuf(f"p1_srow{i}", [128, N], F32) for i in range(2)]
    prow_b = [ctx.sbuf(f"p1_prow{i}", [128, N], BF16) for i in range(2)]
    prow2_b = [ctx.sbuf(f"p1_prow2{i}", [128, N], BF16) for i in range(2)]
    mch_b = [ctx.sbuf(f"p1_mch{i}", [128, 16], F32) for i in range(2)]
    mneg_b = [ctx.sbuf(f"p1_mneg{i}", [128, 1], F32) for i in range(2)]
    lcol_b = [ctx.sbuf(f"p1_l{i}", [128, 1], F32) for i in range(2)]
    rcol_b = [ctx.sbuf(f"p1_r{i}", [128, 1], F32) for i in range(2)]
    pt_sb = [ctx.sbuf(f"p1_pt{i}", [128, 128], BF16) for i in range(2)]
    afb = ctx.sbuf("p1_af", [128, D], BF16)
    g0b = ctx.sbuf("p1_g0b", [128, D], BF16)
    b0b = ctx.sbuf("p1_b0b", [128, D], BF16)
    m2c = ctx.sbuf("p1_m2c", [128, 1], F32)
    v2c = ctx.sbuf("p1_v2c", [128, 1], F32)
    m2a = ctx.sbuf("p1_m2a", [128, 1], F32)
    v2a = ctx.sbuf("p1_v2a", [128, 1], F32)
    v2b = ctx.sbuf("p1_v2b", [128, 1], F32)
    ps_s = [ctx.psum(f"p1_ps_s{i}", [128, 512]) for i in range(2)]
    ps_o = ctx.psum("p1_ps_o", [128, D])
    ps_t = [ctx.psum(f"p1_ps_t{i}", [128, 128], BF16) for i in range(2)]

    pr.on("gpsimd", lambda e: e.dma_start(
        out=k_sb[:, :, :], in_=s_k.rearrange("(c p) n -> p c n", p=128)), inc=[(ctx.LG, 16)])
    pr.on("gpsimd", lambda e: e.dma_start(
        out=v_sb[:, :, :], in_=s_v.rearrange("(j p) d -> p j d", p=128)), inc=[(ctx.LG, 16)])
    pr.on("gpsimd", lambda e: e.dma_start(
        out=g0b[:, :], in_=g0r.partition_broadcast(128)), inc=[(ctx.LG, 16)])
    pr.on("gpsimd", lambda e: e.dma_start(
        out=b0b[:, :], in_=b0r.partition_broadcast(128)), inc=[(ctx.LG, 16)])

    qload_guard = [[], []]
    ps_s_free = [(None, 0), (None, 0)]
    pt_ps_free = [(None, 0), (None, 0)]
    pt_sb_free = [(None, 0), (None, 0)]
    srow_guard = [[], []]
    st_af_prev = [0]

    for i in range(16):
        nk = 2 * i + 2
        ncol = nk * 128
        nkc = (ncol + 511) // 512
        pq = i % 2
        srow, prow, prow2 = srow_b[pq], prow_b[pq], prow2_b[pq]
        mch, mneg, lcol, rcol = mch_b[pq], mneg_b[pq], lcol_b[pq], rcol_b[pq]
        fin32 = srow[:, 0:D]
        fin32b = srow[:, D:2 * D]
        pr.on("sync", lambda e, i=i, pq=pq: e.dma_start(
            out=q_sb[pq][:, :, :],
            in_=s_q[:, i * 128:(i + 1) * 128].rearrange("(c p) n -> p c n", p=128)),
            wait=qload_guard[pq], inc=[(ctx.LD, 16)])
        pr.on("sync", lambda e, i=i, pq=pq: e.dma_start(
            out=mk_sb[pq][:, :, :], in_=masks[i].rearrange("j q k -> q j k")),
            wait=qload_guard[pq], inc=[(ctx.LD, 16)])
        pr.on("sync", lambda e, i=i: e.dma_start(
            out=vs_sb[:, :, :], in_=vsel[i].rearrange("j q k -> q j k")),
            wait=qload_guard[pq] + [(ctx.VC, vs_guard[0])], inc=[(ctx.LD, 16)])
        ld_q = ctx.LD.n
        lg_q = ctx.LG.n

        for kc in range(nkc):
            c0 = kc * 512
            cw = min(512, ncol - c0)
            jp, ps, free = kc % 2, ps_s[kc % 2], ps_s_free[kc % 2]
            w = [(ctx.LD, ld_q), (ctx.LG, None)]
            if free[0] is not None:
                w.append(free)

            def mm_s(e, i=i, pq=pq, ps=ps, c0=c0, cw=cw):
                for c in range(C):
                    r = e.matmul(ps[:, 0:cw], q_sb[pq][:, c, :], k_sb[:, c, c0:c0 + cw],
                                 start=(c == 0), stop=(c == C - 1))
                return r
            pr.on("tensor", mm_s, wait=w, inc=[(ctx.TC, 1)])
            t_s = ctx.TC.n

            last = (kc == nkc - 1)

            def ev_s(e, ps=ps, c0=c0, cw=cw, last=last, pq=pq, i=i, ncol=ncol, kc=kc):
                e.tensor_copy(srow[:, c0:c0 + cw], ps[:, 0:cw])
                if last:
                    e.tensor_tensor(srow[:, ncol - 256:ncol], srow[:, ncol - 256:ncol],
                                    mk_sb[pq][:, :, :].rearrange("p j k -> p (j k)"), ALU.add)
                return e.reduce_max(mch[:, kc:kc + 1], srow[:, c0:c0 + cw], axis=AX.X)
            pr.on("vector", ev_s, wait=[(ctx.TC, t_s)] + srow_guard[pq],
                  inc=[(ctx.VC, 1)])
            ps_s_free[jp] = (ctx.VC, ctx.VC.n)

        def red_m(e, nkc=nkc):
            if nkc < 16:
                e.memset(mch[:, nkc:16], -1.0e30)
            e.drain()
            return e.reduce_max(mneg[:, 0:1], mch[:, 0:16], axis=AX.X, negate=True)
        pr.on("vector", red_m, inc=[(ctx.VC, 1)])
        v_m = ctx.VC.n
        pr.on("scalar", lambda e, ncol=ncol: e.activation(
            prow[:, 0:ncol], srow[:, 0:ncol], AF.Exp, bias=mneg[:, 0:1]),
            wait=[(ctx.VC, v_m), (ctx.TC, ctx.TC.n)],
            inc=[(ctx.SC, 1)])
        s_exp = ctx.SC.n

        def p_fix(e, ncol=ncol, pq=pq):
            e.reduce_sum(lcol[:, 0:1], prow[:, 0:ncol], axis=AX.X)
            e.drain()
            e.reciprocal(rcol[:, 0:1], lcol[:, 0:1])
            e.drain()
            if ncol > 256:
                e.tensor_scalar(prow2[:, 0:ncol - 256], prow[:, 0:ncol - 256],
                                rcol[:, 0:1], None, ALU.mult)
            return e.scalar_tensor_tensor(
                prow2[:, ncol - 256:ncol], prow[:, ncol - 256:ncol], rcol[:, 0:1],
                vs_sb[:, :, :].rearrange("p j k -> p (j k)"), ALU.mult, ALU.add)
        pr.on("vector", p_fix, wait=[(ctx.SC, s_exp), (ctx.LD, ld_q),
                                     (ctx.TC, ctx.TC.n)], inc=[(ctx.VC, 1)])
        v_p = ctx.VC.n
        vs_guard[0] = v_p

        mm_done_at = {}
        for j in range(nk):
            jp = j % 2
            w = [(ctx.VC, v_p)]
            if pt_ps_free[jp][0] is not None:
                w.append(pt_ps_free[jp])
            pr.on("tensor", lambda e, j=j, jp=jp: e.transpose(
                ps_t[jp][:, :], prow2[:, j * 128:(j + 1) * 128], cst["ident16"][:, :]),
                wait=w, inc=[(ctx.TC, 1)])
            t_t = ctx.TC.n
            wv = [(ctx.TC, t_t)]
            if pt_sb_free[jp][0] is not None:
                wv.append(pt_sb_free[jp])
            pr.on("vector", lambda e, jp=jp: e.tensor_copy(pt_sb[jp][:, :], ps_t[jp][:, :]),
                  wait=wv, inc=[(ctx.VC, 1)])
            pt_ps_free[jp] = (ctx.VC, ctx.VC.n)
            mm_done_at[j] = (ctx.VC.n, jp)
            if j >= 1:
                _emit_pv_mm(ctx, i, j - 1, nk, mm_done_at, v_sb, pt_sb, ps_o, pt_sb_free)
        _emit_pv_mm(ctx, i, nk - 1, nk, mm_done_at, v_sb, pt_sb, ps_o, pt_sb_free)
        t_pv = ctx.TC.n

        def ln2(e):
            e.tensor_copy(fin32, ps_o[:, :])
            e.reduce_sum(m2a[:, 0:1], fin32, axis=AX.X)
            e.drain()
            e.tensor_scalar(m2c[:, 0:1], m2a[:, 0:1], 1.0 / D, None, ALU.mult)
            e.drain()
            e.tensor_scalar(fin32b, fin32, m2c[:, 0:1], None, ALU.subtract)
            e.tensor_tensor(fin32, fin32b, fin32b, ALU.mult)
            e.reduce_sum(v2a[:, 0:1], fin32, axis=AX.X)
            e.drain()
            return e.tensor_scalar(v2c[:, 0:1], v2a[:, 0:1], 1.0 / D, EPS, ALU.mult, ALU.add)
        pr.on("vector", ln2, wait=[(ctx.TC, t_pv), (ctx.LD, None)], inc=[(ctx.VC, 1)])
        v_ln = ctx.VC.n

        def rstd2(e):
            e.activation(v2b[:, 0:1], v2c[:, 0:1], AF.Ln)
            e.drain()
            return e.activation(v2c[:, 0:1], v2b[:, 0:1], AF.Exp, scale=-0.5)
        pr.on("scalar", rstd2, wait=[(ctx.VC, v_ln)], inc=[(ctx.SC, 1)])
        s_r2 = ctx.SC.n

        def fin_aff(e):
            e.scalar_tensor_tensor(fin32, fin32b, v2c[:, 0:1], g0b[:, :],
                                   ALU.mult, ALU.mult)
            return e.tensor_tensor(afb[:, :], fin32, b0b[:, :], ALU.add)
        pr.on("vector", fin_aff, wait=[(ctx.SC, s_r2), (ctx.ST, st_af_prev[0])],
              inc=[(ctx.VC, 1)])
        v_fin = ctx.VC.n

        pr.on("gpsimd", lambda e, i=i: e.dma_start(
            out=s_af[i * 128:(i + 1) * 128, :], in_=afb[:, :]),
            wait=[(ctx.VC, v_fin)], inc=[(ctx.ST, 16)])
        st_af_prev[0] = ctx.ST.n
        if P1_DEBUG and i == 1:
            nc = ctx.nc
            d_s = nc.dram_tensor("dbg_srow", [128, 512], F32, kind="ExternalOutput").ap()
            d_p = nc.dram_tensor("dbg_prow", [128, 512], BF16, kind="ExternalOutput").ap()
            d_m = nc.dram_tensor("dbg_m", [128, 4], F32, kind="ExternalOutput").ap()
            d_o = nc.dram_tensor("dbg_o", [128, D], F32, kind="ExternalOutput").ap()
            d_l = nc.dram_tensor("dbg_l", [3, 128, 1], F32, kind="ExternalOutput").ap()
            def dbg(e):
                e.dma_start(out=d_s, in_=srow[:, 0:512]).then_inc(ctx.ST.h, 16)
                e.dma_start(out=d_p, in_=prow[:, 0:512]).then_inc(ctx.ST.h, 16)
                e.dma_start(out=d_m, in_=mch[:, 0:4]).then_inc(ctx.ST.h, 16)
                e.dma_start(out=d_l[0], in_=lcol[:, 0:1]).then_inc(ctx.ST.h, 16)
                e.dma_start(out=d_l[1], in_=rcol[:, 0:1]).then_inc(ctx.ST.h, 16)
                e.dma_start(out=d_l[2], in_=mneg[:, 0:1]).then_inc(ctx.ST.h, 16)
                return e.dma_start(out=d_o, in_=fin32[:, :])
            pr.on("sync", dbg, wait=[(ctx.VC, ctx.VC.n)], inc=[(ctx.ST, 16)])
            ctx.ST.n += 80
            stall = ctx.ST.n
            pr.on("vector", None, wait=[(ctx.ST, stall)], inc=[(ctx.VC, 1)])
            pr.on("scalar", None, wait=[(ctx.ST, stall)], inc=[(ctx.SC, 1)])
            pr.on("tensor", None, wait=[(ctx.ST, stall)], inc=[(ctx.TC, 1)])
        qload_guard[pq] = [(ctx.TC, ctx.TC.n), (ctx.VC, ctx.VC.n), (ctx.SC, ctx.SC.n)]


def _emit_pv_mm(ctx, i, j, nk, mm_done_at, v_sb, pt_sb, ps_o, pt_sb_free):
    pr = ctx.pr
    v_c, jp = mm_done_at[j]

    def mm(e, j=j, jp=jp, nk=nk):
        e.matmul(ps_o[:, 0:512], pt_sb[jp][:, :], v_sb[:, j, 0:512],
                 start=(j == 0), stop=(j == nk - 1))
        return e.matmul(ps_o[:, 512:1024], pt_sb[jp][:, :], v_sb[:, j, 512:1024],
                        start=(j == 0), stop=(j == nk - 1))
    pr.on("tensor", mm, wait=[(ctx.VC, v_c)], inc=[(ctx.TC, 1)])
    pt_sb_free[jp] = (ctx.TC, ctx.TC.n)


def p2(ctx):
    pr, cst = ctx.pr, ctx.consts
    s_af = ctx.hscratch("s_af")
    s_x1a = ctx.hscratch("s_x1a")
    xo = ctx.hin("xT_own", [D, T])
    wfc = ctx.hin("wfc", [D, D])
    bfc = ctx.hin("bfc_c", [128, 8])

    w_sb = ctx.sbuf("p2_wfc", [128, C, D], BF16)
    bfc_sb = ctx.sbuf("p2_bfc", [128, 8], F32)
    pr.on("gpsimd", lambda e: e.dma_start(
        out=w_sb[:, :, :], in_=wfc.rearrange("(c p) o -> p c o", p=128)), inc=[(ctx.LG, 16)])
    lg_w2 = ctx.LG.n
    pr.on("sync", lambda e: e.dma_start(out=bfc_sb[:, :], in_=bfc), inc=[(ctx.LD, 16)])

    af_tm = [ctx.sbuf(f"p2_af{i}", [128, 4, D], BF16) for i in range(2)]
    af_fm = [ctx.sbuf(f"p2_aff{i}", [128, C, NT], BF16) for i in range(2)]
    x32 = [ctx.sbuf(f"p2_x{i}", [128, C, NT], F32) for i in range(2)]
    x1a = [ctx.sbuf(f"p2_x1a{i}", [128, C, NT], F32) for i in range(2)]
    ps_t = [ctx.psum(f"p2_pt{i}", [128, 128], BF16) for i in range(2)]
    pool = ProjPool(ctx, "p2_po", 2, NT)

    buf_ok = [[], []]
    pt_free = [(None, 0), (None, 0)]
    for t in range(T // NT):
        p = t % 2
        w_guard = list(buf_ok[p])
        pr.on("sync", lambda e, t=t, p=p: e.dma_start(
            out=af_tm[p][:, :, :], in_=tm_tile(s_af, t * NT, NT)),
            wait=w_guard, inc=[(ctx.LD, 16)])
        pr.on("sync", lambda e, t=t, p=p: e.dma_start(
            out=x32[p][:, :, :], in_=fm_tile(xo, t * NT, NT)),
            wait=w_guard, inc=[(ctx.LD, 16)])
        ld_t = ctx.LD.n
        for sub in range(4):
            for c in range(C):
                jp = (sub * C + c) % 2
                w = [(ctx.LD, ld_t)]
                if pt_free[jp][0] is not None:
                    w.append(pt_free[jp])
                pr.on("tensor", lambda e, p=p, sub=sub, c=c, jp=jp: e.transpose(
                    ps_t[jp][:, :], af_tm[p][:, sub, c * 128:(c + 1) * 128],
                    cst["ident16"][:, :]), wait=w, inc=[(ctx.TC, 1)])
                t_t = ctx.TC.n
                pr.on("vector", lambda e, p=p, sub=sub, c=c, jp=jp: e.tensor_copy(
                    af_fm[p][:, c, sub * 128:(sub + 1) * 128], ps_t[jp][:, :]),
                    wait=[(ctx.TC, t_t)], inc=[(ctx.VC, 1)])
                pt_free[jp] = (ctx.VC, ctx.VC.n)
        v_tr = ctx.VC.n

        def ev_fc(e, mo, ps, p=p):
            return e.scalar_tensor_tensor(x1a[p][:, mo, 0:NT], ps[:, 0:NT],
                                          bfc_sb[:, mo:mo + 1], x32[p][:, mo, 0:NT],
                                          ALU.add, ALU.add)
        emit_proj_fm(ctx, w_sb, af_fm[p][:, :, :], NT, pool,
                     [(ctx.VC, v_tr), (ctx.LG, lg_w2)], ev_fc, C,
                     evac_eng="vector", evac_wait=[(ctx.LD, ld_t)])
        v_add = ctx.VC.n
        pr.on("gpsimd", lambda e, t=t, p=p: e.dma_start(
            out=fm_tile(s_x1a, t * NT, NT), in_=x1a[p][:, :, :]),
            wait=[(ctx.VC, v_add)], inc=[(ctx.ST, 16)])
        buf_ok[p] = [(ctx.TC, ctx.TC.n), (ctx.VC, ctx.VC.n), (ctx.SC, ctx.SC.n),
                     (ctx.ST, ctx.ST.n)]


def p3(ctx):
    pr = ctx.pr
    s_xn = ctx.hscratch("s_xn")
    s_x1a = ctx.hscratch("s_x1a")
    s_x1 = ctx.hscratch("s_x1")
    w1 = ctx.hin("w1", [D, 4096])
    w2 = ctx.hin("w2", [4096, D])
    b1 = ctx.hin("b1_c", [128, 32])
    b2 = ctx.hin("b2_c", [128, 8])

    w1_sb = ctx.sbuf("p3_w1", [128, C, 4096], BF16)
    w2_sb = ctx.sbuf("p3_w2", [128, 32, D], BF16)
    b1_sb = ctx.sbuf("p3_b1", [128, 32], F32)
    b2_sb = ctx.sbuf("p3_b2", [128, 8], F32)
    pr.on("gpsimd", lambda e: e.dma_start(
        out=w1_sb[:, :, :], in_=w1.rearrange("(c p) o -> p c o", p=128)), inc=[(ctx.LG, 16)])
    pr.on("gpsimd", lambda e: e.dma_start(
        out=w2_sb[:, :, :], in_=w2.rearrange("(c p) o -> p c o", p=128)), inc=[(ctx.LG, 16)])
    lg_w3 = ctx.LG.n
    pr.on("sync", lambda e: e.dma_start(out=b1_sb[:, :], in_=b1), inc=[(ctx.LD, 16)])
    pr.on("sync", lambda e: e.dma_start(out=b2_sb[:, :], in_=b2), inc=[(ctx.LD, 16)])

    xn_sb = [ctx.sbuf(f"p3_xn{i}", [128, C, NT3], BF16) for i in range(2)]
    h_sb = ctx.sbuf("p3_h", [128, 32, NT3], BF16)
    x1a32 = ctx.sbuf("p3_x1a", [128, C, NT3], F32)
    x1b = ctx.sbuf("p3_x1b", [128, C, NT3], F32)
    pool = ProjPool(ctx, "p3_po", 2, NT3)
    h_guard = [0]
    x1a_guard = [[]]
    x1b_guard = [[]]

    buf_ok = [[], []]
    for t in range(T // NT3):
        p = t % 2
        w_guard = list(buf_ok[p])
        pr.on("sync", lambda e, t=t, p=p: e.dma_start(
            out=xn_sb[p][:, :, :], in_=fm_tile(s_xn, t * NT3, NT3)),
            wait=w_guard, inc=[(ctx.LD, 16)])
        pr.on("sync", lambda e, t=t, p=p: e.dma_start(
            out=x1a32[:, :, :], in_=fm_tile(s_x1a, t * NT3, NT3)),
            wait=w_guard + x1a_guard[0], inc=[(ctx.LD, 16)])
        ld_t = ctx.LD.n

        def ev_h(e, mo, ps, p=p):
            return e.activation(h_sb[:, mo, 0:NT3], ps[:, 0:NT3], AF.Relu,
                                bias=b1_sb[:, mo:mo + 1])
        emit_proj_fm(ctx, w1_sb, xn_sb[p][:, :, :], NT3, pool,
                     [(ctx.LD, ld_t), (ctx.TC, h_guard[0]), (ctx.LG, lg_w3)], ev_h, 32)
        s_h = ctx.SC.n

        def ev_y(e, mo, ps, p=p):
            return e.scalar_tensor_tensor(x1b[:, mo, 0:NT3], ps[:, 0:NT3],
                                          b2_sb[:, mo:mo + 1], x1a32[:, mo, 0:NT3],
                                          ALU.add, ALU.add)
        emit_proj_fm(ctx, w2_sb, h_sb[:, :, :], NT3, pool, [(ctx.SC, s_h)], ev_y, C,
                     evac_eng="vector", evac_wait=[(ctx.LD, ld_t)] + x1b_guard[0])
        v_add = ctx.VC.n
        h_guard[0] = ctx.TC.n
        x1a_guard[0] = [(ctx.VC, v_add)]
        pr.on("gpsimd", lambda e, t=t, p=p: e.dma_start(
            out=fm_tile(s_x1, t * NT3, NT3), in_=x1b[:, :, :]),
            wait=[(ctx.VC, v_add)], inc=[(ctx.ST, 16)])
        x1b_guard[0] = [(ctx.ST, ctx.ST.n)]
        buf_ok[p] = [(ctx.TC, ctx.TC.n), (ctx.VC, ctx.VC.n), (ctx.SC, ctx.SC.n),
                     (ctx.ST, ctx.ST.n)]


def p4(ctx):
    pr, cst = ctx.pr, ctx.consts
    nc = ctx.nc
    s_x1 = ctx.hscratch("s_x1")
    s_xn1 = ctx.hscratch("s_xn1")
    s_u = ctx.hscratch("s_u")
    s_qkf = ctx.hscratch("s_qkf")
    s_vt = ctx.hscratch("s_vt")
    s_lkv = ctx.hscratch("s_lkv")
    wqkT = ctx.hin("wqkT", [D, 256])
    wvT1 = ctx.hin("wvT1", [D, 2048])
    bqk = ctx.hin("bqk_c", [128, 2])
    ln1w = ctx.hin("ln1w_c", [128, 8])
    ln1b = ctx.hin("ln1b_c", [128, 8])
    g3r = ctx.hin("g3_row", [1, 256])
    b3r = ctx.hin("b3_row", [1, 256])
    bvr = ctx.hin("bv_row", [1, 2048])
    cosT = ctx.hin("cosT", [T, 128])
    sinT = ctx.hin("sinT", [T, 128])
    cc_in = nc.dram_tensor("cc_in", [256, 2048], F32, kind="Internal").ap()
    cc_out = nc.dram_tensor("cc_out", [256, 2048], F32, kind="Internal").ap()

    s_gates = ctx.hscratch("s_gates")
    wgmT = ctx.hin("wgmT", [D, 8])
    wgm_sb = ctx.sbuf("p4_wgm", [128, C, 8], BF16)
    gt_sc = ctx.sbuf("p4_gt_sc", [128, 8], F32)
    gt_m1 = ctx.sbuf("p4_gt_m1", [128, 1], F32)
    gt_m1n = ctx.sbuf("p4_gt_m1n", [128, 1], F32)
    gt_m2 = ctx.sbuf("p4_gt_m2", [128, 1], F32)
    gt_tk = ctx.sbuf("p4_gt_tk", [128, 8], F32)
    gt_s2 = ctx.sbuf("p4_gt_s2", [128, 8], F32)
    gt_kp = ctx.sbuf("p4_gt_kp", [128, 8], F32)
    gt_eu = ctx.sbuf("p4_gt_eu", [128, 8], F32)
    gt_gk = ctx.sbuf("p4_gt_gk", [128, 8], F32)
    gt_dn = ctx.sbuf("p4_gt_dn", [128, 1], F32)
    gt_rd = ctx.sbuf("p4_gt_rd", [128, 1], F32)
    gt_out = ctx.sbuf("p4_gt_out", [128, 16, 8], F32)
    wqk_sb = ctx.sbuf("p4_wqk", [128, C, 256], BF16)
    wv_sb = ctx.sbuf("p4_wv", [128, C, 2048], BF16)
    bqk_sb = ctx.sbuf("p4_bqk", [128, 2], F32)
    ln1w_sb = ctx.sbuf("p4_l1w", [128, 8], F32)
    ln1b_sb = ctx.sbuf("p4_l1b", [128, 8], F32)
    g3b = ctx.sbuf("p4_g3b", [128, 256], F32)
    b3b = ctx.sbuf("p4_b3b", [128, 256], F32)
    bvb = ctx.sbuf("p4_bvb", [128, 2048], F32)
    cosT_sb = ctx.sbuf("p4_cosT", [128, 16, 128], BF16)
    sinT_sb = ctx.sbuf("p4_sinT", [128, 16, 128], BF16)
    pr.on("gpsimd", lambda e: e.dma_start(
        out=wgm_sb[:, :, :], in_=wgmT.rearrange("(c p) o -> p c o", p=128)), inc=[(ctx.LG, 16)])
    pr.on("gpsimd", lambda e: e.dma_start(
        out=wqk_sb[:, :, :], in_=wqkT.rearrange("(c p) o -> p c o", p=128)), inc=[(ctx.LG, 16)])
    pr.on("gpsimd", lambda e: e.dma_start(
        out=wv_sb[:, :, :], in_=wvT1.rearrange("(c p) o -> p c o", p=128)), inc=[(ctx.LG, 16)])
    pr.on("gpsimd", lambda e: e.dma_start(
        out=cosT_sb[:, :, :], in_=cosT.rearrange("(s p) f -> p s f", p=128)), inc=[(ctx.LG, 16)])
    pr.on("gpsimd", lambda e: e.dma_start(
        out=sinT_sb[:, :, :], in_=sinT.rearrange("(s p) f -> p s f", p=128)), inc=[(ctx.LG, 16)])
    pr.on("sync", lambda e: e.dma_start(out=bqk_sb[:, :], in_=bqk), inc=[(ctx.LD, 16)])
    pr.on("sync", lambda e: e.dma_start(out=ln1w_sb[:, :], in_=ln1w), inc=[(ctx.LD, 16)])
    pr.on("sync", lambda e: e.dma_start(out=ln1b_sb[:, :], in_=ln1b), inc=[(ctx.LD, 16)])
    pr.on("sync", lambda e: e.dma_start(out=g3b[:, :], in_=g3r.partition_broadcast(128)), inc=[(ctx.LD, 16)])
    pr.on("sync", lambda e: e.dma_start(out=b3b[:, :], in_=b3r.partition_broadcast(128)), inc=[(ctx.LD, 16)])
    pr.on("sync", lambda e: e.dma_start(out=bvb[:, :], in_=bvr.partition_broadcast(128)), inc=[(ctx.LD, 16)])
    lg_w = ctx.LG.n
    ld_w = ctx.LD.n

    x32 = ctx.sbuf("p4_x32", [128, C, NT], F32)
    x16 = ctx.sbuf("p4_x16", [128, C, NT], BF16)
    t2 = ctx.sbuf("p4_t2", [128, C, NT], F32)
    xn1_16 = ctx.sbuf("p4_xn1", [128, C, NT], BF16)
    u16 = ctx.sbuf("p4_u", [128, C, NT], BF16)
    qkf = ctx.sbuf("p4_qkf", [128, 2, NT], BF16)
    qkt = ctx.sbuf("p4_qkt", [128, 4, 256], BF16)
    qg = ctx.sbuf("p4_qg", [128, 256], BF16)
    qh = ctx.sbuf("p4_qh", [128, 256], BF16)
    lkr = ctx.sbuf("p4_lkr", [128, 4, 256], BF16)
    rta = ctx.sbuf("p4_rta", [128, 128], BF16)
    rtb = ctx.sbuf("p4_rtb", [128, 128], BF16)
    vtm = ctx.sbuf("p4_vtm", [128, 4, 2048], BF16)
    vtmp = ctx.sbuf("p4_vtmp", [128, 512], F32)
    lkv = ctx.sbuf("p4_lkv", [128, 2, 2048], F32)
    lkv2 = ctx.sbuf("p4_lkv2", [128, 2, 2048], F32)
    lnb = {"sq": ctx.sbuf("p4_sq", [128, C, NT], BF16),
           "row": ctx.sbuf("p4_row", [1, 4 * NT], F32),
           "t1": ctx.sbuf("p4_t1", [128, NT], F32),
           "t1b": ctx.sbuf("p4_t1b", [128, NT], F32),
           "st0": ctx.psum("p4_ps_s0", [128, NT]),
           "st1": ctx.psum("p4_ps_s1", [128, NT]),
           "bc0": ctx.psum("p4_ps_b0", [128, NT]),
           "bc1": ctx.psum("p4_ps_b1", [128, NT])}
    ps_t = [ctx.psum(f"p4_pt{i}", [128, 128], BF16) for i in range(2)]
    pool = ProjPool(ctx, "p4_po", 2, NT)

    buf_ok = [[]]
    pt_free = [(None, 0), (None, 0)]
    for t in range(T // NT):
        w_guard = list(buf_ok[0])
        pr.on("sync", lambda e, t=t: e.dma_start(
            out=x32[:, :, :], in_=fm_tile(s_x1, t * NT, NT)),
            wait=w_guard, inc=[(ctx.LD, 16)])
        ld_t, lg_t = ctx.LD.n, ctx.LG.n
        pr.on("vector", lambda e: e.tensor_copy(x16[:, :, :], x32[:, :, :]),
              wait=w_guard + [(ctx.LD, ld_t)], inc=[(ctx.VC, 1)])

        v_xn1 = emit_ln_fm(ctx, lnb, x32[:, :, :], x16[:, :, :], xn1_16[:, :, :],
                           NT, t2_out=t2[:, :, :], wait0=w_guard)
        pr.on("gpsimd", lambda e, t=t: e.dma_start(
            out=fm_tile(s_xn1, t * NT, NT), in_=xn1_16[:, :, :]),
            wait=[(ctx.VC, v_xn1)], inc=[(ctx.ST, 16)])
        gt_guard = [0]
        for q in range(4):
            tch = t * 4 + q
            jg, ps_gt, freeg = pool.acquire()

            def mmgt(e, q=q, ps_gt=ps_gt):
                for c in range(C):
                    i = e.matmul(ps_gt[:, 0:8], xn1_16[:, c, q * 128:(q + 1) * 128],
                                 wgm_sb[:, c, 0:8], start=(c == 0), stop=(c == C - 1))
                return i
            wgt = [(ctx.VC, v_xn1), (ctx.LG, lg_w), (ctx.VC, gt_guard[0])]
            if freeg[0] is not None:
                wgt.append(freeg)
            pr.on("tensor", mmgt, wait=wgt, inc=[(ctx.TC, 1)])
            t_gt = ctx.TC.n

            def top2(e, tch=tch, ps_gt=ps_gt):
                e.tensor_copy(gt_sc[:, :], ps_gt[:, 0:8])
                e.drain()
                e.reduce_max(gt_m1[:, 0:1], gt_sc[:, :], axis=AX.X)
                e.drain()
                e.tensor_scalar(gt_m1n[:, 0:1], gt_m1[:, 0:1], -1.0, None, ALU.mult)
                e.tensor_scalar(gt_tk[:, :], gt_sc[:, :], gt_m1[:, 0:1], None, ALU.is_ge)
                e.drain()
                e.scalar_tensor_tensor(gt_s2[:, :], gt_tk[:, :], -1.0e9, gt_sc[:, :],
                                       ALU.mult, ALU.add)
                e.drain()
                e.reduce_max(gt_m2[:, 0:1], gt_s2[:, :], axis=AX.X)
                e.drain()
                return e.tensor_scalar(gt_kp[:, :], gt_sc[:, :], gt_m2[:, 0:1], None, ALU.is_ge)
            pr.on("vector", top2, wait=[(ctx.TC, t_gt)], inc=[(ctx.VC, 1)])
            v_t2 = ctx.VC.n
            pool.release(jg, ctx.VC, v_t2)
            pr.on("scalar", lambda e: e.activation(gt_eu[:, :], gt_sc[:, :], AF.Exp,
                                                   bias=gt_m1n[:, 0:1]),
                  wait=[(ctx.VC, v_t2)], inc=[(ctx.SC, 1)])
            s_eu = ctx.SC.n

            def gfin(e, tch=tch):
                e.tensor_tensor(gt_gk[:, :], gt_eu[:, :], gt_kp[:, :], ALU.mult)
                e.drain()
                e.reduce_sum(gt_dn[:, 0:1], gt_gk[:, :], axis=AX.X)
                e.drain()
                e.reciprocal(gt_rd[:, 0:1], gt_dn[:, 0:1])
                e.drain()
                return e.tensor_scalar(gt_out[:, tch % 4 + (tch // 4) * 4, :],
                                       gt_gk[:, :], gt_rd[:, 0:1], None, ALU.mult)
            pr.on("vector", gfin, wait=[(ctx.SC, s_eu)], inc=[(ctx.VC, 1)])
            gt_guard[0] = ctx.VC.n
        if t == T // NT - 1:
            pass
        v_u = emit_ln_fm(ctx, lnb, t2[:, :, :], xn1_16[:, :, :], u16[:, :, :],
                         NT, w_col=ln1w_sb, b_col=ln1b_sb,
                         wait0=[(ctx.VC, v_xn1), (ctx.TC, ctx.TC.n)])
        pr.on("gpsimd", lambda e, t=t: e.dma_start(
            out=fm_tile(s_u, t * NT, NT), in_=u16[:, :, :]),
            wait=[(ctx.VC, v_u)], inc=[(ctx.ST, 16)])

        def ev_qk(e, mo, ps):
            return e.activation(qkf[:, mo, 0:NT], ps[:, 0:NT], AF.Silu,
                                bias=bqk_sb[:, mo:mo + 1])
        emit_proj_fm(ctx, wqk_sb, u16[:, :, :], NT, pool,
                     [(ctx.VC, v_u), (ctx.LG, lg_w), (ctx.LD, ld_w)], ev_qk, 2)
        s_qk = ctx.SC.n
        pr.on("gpsimd", lambda e, t=t: e.dma_start(
            out=s_qkf[:, t * NT:(t + 1) * NT].rearrange("(c p) n -> p c n", p=128),
            in_=qkf[:, :, :]), wait=[(ctx.SC, s_qk)], inc=[(ctx.ST, 16)])

        for sub in range(4):
            for ch in range(2):
                jp = (sub * 2 + ch) % 2
                w = [(ctx.SC, s_qk)]
                if pt_free[jp][0] is not None:
                    w.append(pt_free[jp])
                pr.on("tensor", lambda e, sub=sub, ch=ch, jp=jp: e.transpose(
                    ps_t[jp][:, :], qkf[:, ch, sub * 128:(sub + 1) * 128],
                    cst["ident16"][:, :]), wait=w, inc=[(ctx.TC, 1)])
                t_t = ctx.TC.n
                pr.on("vector", lambda e, sub=sub, ch=ch, jp=jp: e.tensor_copy(
                    qkt[:, sub, ch * 128:(ch + 1) * 128], ps_t[jp][:, :]),
                    wait=[(ctx.TC, t_t)], inc=[(ctx.VC, 1)])
                pt_free[jp] = (ctx.VC, ctx.VC.n)
        v_qkt = ctx.VC.n

        def mk_lk(e, t=t):
            i = None
            for sub in range(4):
                ts_ = t * 4 + sub
                e.tensor_tensor(qg[:, 0:256], qkt[:, sub, :], g3b[:, :], ALU.mult)
                e.tensor_tensor(qh[:, 0:256], qg[:, 0:256], b3b[:, :], ALU.add)
                e.tensor_tensor(rta[:, :], qh[:, 0:128], cosT_sb[:, ts_, :], ALU.mult)
                e.tensor_tensor(rtb[:, :], qh[:, 128:256], sinT_sb[:, ts_, :], ALU.mult)
                e.tensor_tensor(lkr[:, sub, 0:128], rta[:, :], rtb[:, :], ALU.subtract)
                e.tensor_tensor(rta[:, :], qh[:, 128:256], cosT_sb[:, ts_, :], ALU.mult)
                e.tensor_tensor(rtb[:, :], qh[:, 0:128], sinT_sb[:, ts_, :], ALU.mult)
                i = e.tensor_tensor(lkr[:, sub, 128:256], rta[:, :], rtb[:, :], ALU.add)
            return i
        pr.on("vector", mk_lk, wait=[(ctx.VC, v_qkt), (ctx.LG, lg_w)], inc=[(ctx.VC, 1)])
        v_lk = ctx.VC.n

        for sub in range(4):
            for g in range(4):
                j, ps, free = pool.acquire()
                w = [(ctx.VC, v_u), (ctx.LG, lg_w)]
                if free[0] is not None:
                    w.append(free)

                def mmv(e, sub=sub, g=g, ps=ps):
                    for c in range(C):
                        i = e.matmul(ps[:, 0:512], u16[:, c, sub * 128:(sub + 1) * 128],
                                     wv_sb[:, c, g * 512:(g + 1) * 512],
                                     start=(c == 0), stop=(c == C - 1))
                    return i
                pr.on("tensor", mmv, wait=w, inc=[(ctx.TC, 1)])
                t_mm = ctx.TC.n
                pr.on("vector", lambda e, sub=sub, g=g, ps=ps: e.tensor_tensor(
                    vtmp[:, 0:512], ps[:, 0:512], bvb[:, g * 512:(g + 1) * 512], ALU.add),
                    wait=[(ctx.TC, t_mm), (ctx.LD, ld_w), (ctx.SC, ctx.SC.n)],
                    inc=[(ctx.VC, 1)])
                v_vt = ctx.VC.n
                pr.on("scalar", lambda e, sub=sub, g=g: e.activation(
                    vtm[:, sub, g * 512:(g + 1) * 512], vtmp[:, 0:512], AF.Silu),
                    wait=[(ctx.VC, v_vt)], inc=[(ctx.SC, 1)])
                pool.release(j, ctx.SC, ctx.SC.n)
        s_v = ctx.SC.n
        pr.on("gpsimd", lambda e, t=t: e.dma_start(
            out=tm_tile(s_vt, t * NT, NT), in_=vtm[:, :, :]),
            wait=[(ctx.SC, s_v)], inc=[(ctx.ST, 16)])

        for dc in range(2):
            for g in range(4):
                j, ps, free = pool.acquire()
                w = [(ctx.VC, v_lk), (ctx.SC, s_v)]
                if free[0] is not None:
                    w.append(free)

                def mml(e, dc=dc, g=g, ps=ps):
                    for sub in range(4):
                        i = e.matmul(ps[:, 0:512], lkr[:, sub, dc * 128:(dc + 1) * 128],
                                     vtm[:, sub, g * 512:(g + 1) * 512],
                                     start=(sub == 0), stop=(sub == 3))
                    return i
                pr.on("tensor", mml, wait=w, inc=[(ctx.TC, 1)])
                t_mm = ctx.TC.n
                if t == 0:
                    pr.on("vector", lambda e, dc=dc, g=g, ps=ps: e.tensor_copy(
                        lkv[:, dc, g * 512:(g + 1) * 512], ps[:, 0:512]),
                        wait=[(ctx.TC, t_mm)], inc=[(ctx.VC, 1)])
                else:
                    pr.on("vector", lambda e, dc=dc, g=g, ps=ps: e.tensor_tensor(
                        lkv[:, dc, g * 512:(g + 1) * 512],
                        lkv[:, dc, g * 512:(g + 1) * 512], ps[:, 0:512], ALU.add),
                        wait=[(ctx.TC, t_mm)], inc=[(ctx.VC, 1)])
                pool.release(j, ctx.VC, ctx.VC.n)
        buf_ok[0] = [(ctx.TC, ctx.TC.n), (ctx.VC, ctx.VC.n), (ctx.SC, ctx.SC.n),
                     (ctx.ST, ctx.ST.n)]

    pr.on("gpsimd", lambda e: e.dma_start(
        out=s_gates[:, :], in_=gt_out[:, :, :].rearrange("p a b -> p (a b)")),
        wait=[(ctx.VC, None)], inc=[(ctx.ST, 16)])
    pr.on("vector", lambda e: e.tensor_scalar(
        lkv2[:, :, :], lkv[:, :, :], 1.0 / N, None, ALU.mult), inc=[(ctx.VC, 1)])
    v_sc = ctx.VC.n
    pr.on("gpsimd", lambda e: e.dma_start(
        out=cc_in.rearrange("(c p) o -> p c o", p=128), in_=lkv2[:, :, :]),
        wait=[(ctx.VC, v_sc)], inc=[(ctx.LG, 16)])
    lg_cc0 = ctx.LG.n
    pr.on("gpsimd", lambda e: e.collective_compute(
        "AllReduce", ALU.add, [[0, 1], [2, 3], [4, 5], [6, 7]],
        [cc_in.opt()], [cc_out.opt()]),
        wait=[(ctx.LG, lg_cc0)], inc=[(ctx.CC, 1)])
    pr.on("gpsimd", lambda e: e.dma_start(
        out=lkv2[:, :, :], in_=cc_out.rearrange("(c p) o -> p c o", p=128)),
        wait=[(ctx.CC, 1), (ctx.VC, v_sc)], inc=[(ctx.LG, 16)])
    lg_cc = ctx.LG.n
    pr.on("gpsimd", lambda e: e.dma_start(
        out=s_lkv.rearrange("(c p) o -> p c o", p=128), in_=lkv2[:, :, :]),
        wait=[(ctx.LG, lg_cc)], inc=[(ctx.ST, 16)])


def p5(ctx):
    pr, cst = ctx.pr, ctx.consts
    s_x1 = ctx.hscratch("s_x1")
    s_u = ctx.hscratch("s_u")
    s_qkf = ctx.hscratch("s_qkf")
    s_vt = ctx.hscratch("s_vt")
    s_lkv = ctx.hscratch("s_lkv")
    s_x2a = ctx.hscratch("s_x2a")
    wgT = ctx.hin("wgT", [D, 2048])
    woT = ctx.hin("woT", [2048, D])
    bg = ctx.hin("bg_c", [128, 16])
    bo = ctx.hin("bo_c", [128, 8])
    ln2w = ctx.hin("ln2w_c", [128, 8])
    ln2b = ctx.hin("ln2b_c", [128, 8])
    ghc = ctx.hin("gh_c", [128, 8])
    bhc = ctx.hin("bh_c", [128, 8])
    b64 = ctx.hin("bias64", [64, 64])
    cosF = ctx.hin("cosF", [128, T])
    sinF = ctx.hin("sinF", [128, T])

    wg_sb = ctx.sbuf("p5_wg", [128, C, 2048], BF16)
    wo_sb = ctx.sbuf("p5_wo", [128, 16, D], BF16)
    lkv16 = ctx.sbuf("p5_lkv", [128, 2, 2048], BF16)
    cosF_sb = ctx.sbuf("p5_cosF", [128, T], BF16)
    sinF_sb = ctx.sbuf("p5_sinF", [128, T], BF16)
    bg_sb = ctx.sbuf("p5_bg", [128, 16], F32)
    bo_sb = ctx.sbuf("p5_bo", [128, 8], F32)
    l2w_sb = ctx.sbuf("p5_l2w", [128, 8], F32)
    l2b_sb = ctx.sbuf("p5_l2b", [128, 8], F32)
    ghc_sb = ctx.sbuf("p5_ghc", [128, 8], F32)
    bhc_sb = ctx.sbuf("p5_bhc", [128, 8], F32)
    b64_sb = ctx.sbuf("p5_b64", [64, 64], F32)
    pr.on("gpsimd", lambda e: e.dma_start(
        out=wg_sb[:, :, :], in_=wgT.rearrange("(c p) o -> p c o", p=128)), inc=[(ctx.LG, 16)])
    pr.on("gpsimd", lambda e: e.dma_start(
        out=wo_sb[:, :, :], in_=woT.rearrange("(c p) o -> p c o", p=128)), inc=[(ctx.LG, 16)])
    pr.on("gpsimd", lambda e: e.dma_start(
        out=lkv16[:, :, :], in_=s_lkv.rearrange("(c p) o -> p c o", p=128)), inc=[(ctx.LG, 16)])
    pr.on("gpsimd", lambda e: e.dma_start(out=cosF_sb[:, :], in_=cosF), inc=[(ctx.LG, 16)])
    pr.on("gpsimd", lambda e: e.dma_start(out=sinF_sb[:, :], in_=sinF), inc=[(ctx.LG, 16)])

    for nm, src, dst in (("bg", bg, bg_sb), ("bo", bo, bo_sb), ("l2w", ln2w, l2w_sb),
                         ("l2b", ln2b, l2b_sb), ("ghc", ghc, ghc_sb), ("bhc", bhc, bhc_sb),
                         ("b64", b64, b64_sb)):
        pr.on("sync", lambda e, src=src, dst=dst: e.dma_start(out=dst[:, :], in_=src),
              inc=[(ctx.LD, 16)])
    lg_w = ctx.LG.n
    ld_w = ctx.LD.n

    u16 = ctx.sbuf("p5_u", [128, C, NT], BF16)
    qkf = ctx.sbuf("p5_qkf", [128, 2, NT], BF16)
    x1_32 = ctx.sbuf("p5_x1", [128, C, NT], F32)
    vtm = ctx.sbuf("p5_vtm", [64, 8, 2048], BF16)
    qpre = ctx.sbuf("p5_qpre", [128, 2, NT], BF16)
    qqr = ctx.sbuf("p5_qqr", [128, 2, NT], BF16)
    qkkr = ctx.sbuf("p5_qkkr", [128, 2, NT], BF16)
    lqr = ctx.sbuf("p5_lqr", [128, 2, NT], BF16)
    rta = ctx.sbuf("p5_rta", [128, NT], BF16)
    rtb = ctx.sbuf("p5_rtb", [128, NT], BF16)
    gate_t = [ctx.sbuf(f"p5_gate{i}", [128, NT], BF16) for i in range(2)]
    ag = ctx.sbuf("p5_ag", [64, 64], F32)
    rg = ctx.sbuf("p5_rg", [64, 64], F32)
    a2_sb = ctx.sbuf("p5_a2", [64, 8 * 64], BF16)
    ob = ctx.sbuf("p5_ob", [128, 16, NT], BF16)
    at32 = ctx.sbuf("p5_at32", [128, C, NT], F32)
    x2at = ctx.sbuf("p5_x2at", [128, D], F32)
    lnb = {"row": ctx.sbuf("p5_row", [1, 4 * NT], F32),
           "t1": ctx.sbuf("p5_t1", [128, NT], F32),
           "t1b": ctx.sbuf("p5_t1b", [128, NT], F32),
           "st0": ctx.psum("p5_ps_s0", [128, NT]),
           "st1": ctx.psum("p5_ps_s1", [128, NT]),
           "bc0": ctx.psum("p5_ps_b0", [128, NT]),
           "bc1": ctx.psum("p5_ps_b1", [128, NT])}
    ps_tf = ctx.psum("p5_ps_tf", [128, 128], F32)
    pool = ProjPool(ctx, "p5_po", 2, NT)

    buf_ok = [[]]
    tf_free = [(None, 0)]
    for t in range(T // NT):
        w_guard = list(buf_ok[0])
        pr.on("sync", lambda e, t=t: e.dma_start(
            out=u16[:, :, :], in_=fm_tile(s_u, t * NT, NT)),
            wait=w_guard, inc=[(ctx.LD, 16)])
        pr.on("sync", lambda e, t=t: e.dma_start(
            out=qkf[:, :, :],
            in_=s_qkf[:, t * NT:(t + 1) * NT].rearrange("(c p) n -> p c n", p=128)),
            wait=w_guard, inc=[(ctx.LD, 16)])
        pr.on("sync", lambda e, t=t: e.dma_start(
            out=x1_32[:, :, :], in_=fm_tile(s_x1, t * NT, NT)),
            wait=w_guard, inc=[(ctx.LD, 16)])
        pr.on("sync", lambda e, t=t: e.dma_start(
            out=vtm[:, :, :],
            in_=s_vt[t * NT:(t + 1) * NT, :].rearrange("(s p) d -> p s d", p=64)),
            wait=w_guard, inc=[(ctx.LD, 16)])
        ld_t = ctx.LD.n
        lg_t = lg_w

        def mk_heads(e, t=t):
            i = None
            for head, dst in ((0, qqr), (2, qkkr), (1, lqr)):
                for ch in range(2):
                    e.tensor_scalar(qpre[:, ch, 0:NT], qkf[:, ch, 0:NT],
                                    ghc_sb[:, head * 2 + ch:head * 2 + ch + 1],
                                    bhc_sb[:, head * 2 + ch:head * 2 + ch + 1],
                                    ALU.mult, ALU.add)
                cs = cosF_sb[:, t * NT:(t + 1) * NT]
                sn = sinF_sb[:, t * NT:(t + 1) * NT]
                e.tensor_tensor(rta[:, 0:NT], qpre[:, 0, 0:NT], cs, ALU.mult)
                e.tensor_tensor(rtb[:, 0:NT], qpre[:, 1, 0:NT], sn, ALU.mult)
                e.tensor_tensor(dst[:, 0, 0:NT], rta[:, 0:NT], rtb[:, 0:NT], ALU.subtract)
                e.tensor_tensor(rta[:, 0:NT], qpre[:, 1, 0:NT], cs, ALU.mult)
                e.tensor_tensor(rtb[:, 0:NT], qpre[:, 0, 0:NT], sn, ALU.mult)
                i = e.tensor_tensor(dst[:, 1, 0:NT], rta[:, 0:NT], rtb[:, 0:NT], ALU.add)
            return i
        pr.on("vector", mk_heads, wait=[(ctx.LD, ld_t), (ctx.LG, lg_t),
                                        (ctx.TC, ctx.TC.n)], inc=[(ctx.VC, 1)])
        v_heads = ctx.VC.n

        for g in range(8):
            j, ps, free = pool.acquire()
            w = [(ctx.VC, v_heads)]
            if free[0] is not None:
                w.append(free)

            def mmq(e, g=g, ps=ps):
                for dc in range(2):
                    i = e.matmul(ps[0:64, 0:64], qkkr[:, dc, g * 64:(g + 1) * 64],
                                 qqr[:, dc, g * 64:(g + 1) * 64],
                                 start=(dc == 0), stop=(dc == 1))
                return i
            pr.on("tensor", mmq, wait=w, inc=[(ctx.TC, 1)])
            t_q = ctx.TC.n

            def ev_q(e, g=g, ps=ps):
                e.scalar_tensor_tensor(ag[0:64, 0:64], ps[0:64, 0:64], 1.0 / 64.0,
                                       b64_sb[0:64, 0:64], ALU.mult, ALU.add)
                e.drain()
                e.tensor_relu(rg[0:64, 0:64], ag[0:64, 0:64])
                e.drain()
                return e.tensor_tensor(a2_sb[0:64, g * 64:(g + 1) * 64],
                                       rg[0:64, 0:64], rg[0:64, 0:64], ALU.mult)
            pr.on("vector", ev_q, wait=[(ctx.TC, t_q), (ctx.LD, ld_w)], inc=[(ctx.VC, 1)])
            pool.release(j, ctx.VC, ctx.VC.n)
        v_a2 = ctx.VC.n

        gate_free = [(None, 0), (None, 0)]
        for ec in range(16):
            gp = ec % 2
            jg, psg, freeg = pool.acquire()
            wg = [(ctx.LD, ld_t), (ctx.LG, lg_w)]
            if freeg[0] is not None:
                wg.append(freeg)

            def mmg(e, ec=ec, psg=psg):
                for c in range(C):
                    i = e.matmul(psg[:, 0:NT], wg_sb[:, c, ec * 128:(ec + 1) * 128],
                                 u16[:, c, 0:NT], start=(c == 0), stop=(c == C - 1))
                return i
            pr.on("tensor", mmg, wait=wg, inc=[(ctx.TC, 1)])
            t_g = ctx.TC.n
            wgv = [(ctx.TC, t_g)]
            if gate_free[gp][0] is not None:
                wgv.append(gate_free[gp])
            pr.on("scalar", lambda e, ec=ec, psg=psg, gp=gp: e.activation(
                gate_t[gp][:, 0:NT], psg[:, 0:NT], AF.Silu, bias=bg_sb[:, ec:ec + 1]),
                wait=wgv, inc=[(ctx.SC, 1)])
            pool.release(jg, ctx.SC, ctx.SC.n)
            s_g = ctx.SC.n

            jo, pso, freeo = pool.acquire()
            w = [(ctx.VC, v_a2), (ctx.LG, lg_w)]
            if freeo[0] is not None:
                w.append(freeo)

            def mmo(e, ec=ec, ps=pso):
                for dc in range(2):
                    e.matmul(ps[:, 0:NT], lkv16[:, dc, ec * 128:(ec + 1) * 128],
                             lqr[:, dc, 0:NT], start=(dc == 0), stop=False)
                for g in range(8):
                    i = e.matmul(ps[:, g * 64:(g + 1) * 64],
                                 vtm[0:64, g, ec * 128:(ec + 1) * 128],
                                 a2_sb[0:64, g * 64:(g + 1) * 64],
                                 start=False, stop=(g == 7))
                return i
            pr.on("tensor", mmo, wait=w, inc=[(ctx.TC, 1)])
            t_o = ctx.TC.n
            pr.on("vector", lambda e, ec=ec, ps=pso, gp=gp: e.scalar_tensor_tensor(
                ob[:, ec, 0:NT], ps[:, 0:NT], 1.0, gate_t[gp][:, 0:NT],
                ALU.mult, ALU.mult),
                wait=[(ctx.TC, t_o), (ctx.SC, s_g)], inc=[(ctx.VC, 1)])
            gate_free[gp] = (ctx.VC, ctx.VC.n)
            pool.release(jo, ctx.VC, ctx.VC.n)
        v_ob = ctx.VC.n

        def ev_wo(e, mo, ps):
            return e.tensor_scalar(at32[:, mo, 0:NT], ps[:, 0:NT],
                                   bo_sb[:, mo:mo + 1], None, ALU.add)
        emit_proj_fm(ctx, wo_sb, ob[:, :, :], NT, pool, [(ctx.VC, v_ob)], ev_wo, C,
                     evac_eng="vector")
        s_at = ctx.VC.n

        v_x2 = emit_ln_fm(ctx, lnb, at32[:, :, :], None, at32[:, :, :],
                          NT, w_col=l2w_sb, b_col=l2b_sb, add_in=x1_32[:, :, :],
                          wait0=[(ctx.VC, s_at), (ctx.TC, ctx.TC.n)], stats_fp32=True)

        for sub in range(4):
            st_prev = ctx.ST.n
            for c in range(C):
                w = [(ctx.VC, v_x2)]
                if tf_free[0][0] is not None:
                    w.append(tf_free[0])
                pr.on("tensor", lambda e, sub=sub, c=c: e.transpose(
                    ps_tf[:, :], at32[:, c, sub * 128:(sub + 1) * 128],
                    cst["ident32"][:, :]), wait=w, inc=[(ctx.TC, 1)])
                t_t = ctx.TC.n
                pr.on("vector", lambda e, sub=sub, c=c: e.tensor_copy(
                    x2at[:, c * 128:(c + 1) * 128], ps_tf[:, :]),
                    wait=[(ctx.TC, t_t), (ctx.ST, st_prev)], inc=[(ctx.VC, 1)])
                tf_free[0] = (ctx.VC, ctx.VC.n)
            v_tr = ctx.VC.n
            pr.on("gpsimd", lambda e, t=t, sub=sub: e.dma_start(
                out=s_x2a[t * NT + sub * 128:t * NT + (sub + 1) * 128, :],
                in_=x2at[:, :]), wait=[(ctx.VC, v_tr)], inc=[(ctx.ST, 16)])
        buf_ok[0] = [(ctx.TC, ctx.TC.n), (ctx.VC, ctx.VC.n), (ctx.SC, ctx.SC.n),
                     (ctx.ST, ctx.ST.n)]


def p6(ctx):
    pr, cst = ctx.pr, ctx.consts
    s_xn1 = ctx.hscratch("s_xn1")
    s_x2a = ctx.hscratch("s_x2a")
    s_gates = ctx.hscratch("s_gates")
    out = ctx.out
    w1T = ctx.hin("w1T", [8, D, 2048])
    w2T = ctx.hin("w2T", [8, 2048, D])
    w3T = ctx.hin("w3T", [8, D, 2048])
    a1T = ctx.hin("a1T", [8, D, 16])
    a2T = ctx.hin("a2T", [8, 2048, 16])
    a3T = ctx.hin("a3T", [8, D, 16])
    b1l = ctx.hin("b1l", [8, 16, 2048])
    b2l = ctx.hin("b2l", [8, 16, D])
    b3l = ctx.hin("b3l", [8, 16, 2048])
    bm1 = ctx.hin("bm1_c", [8, 128, 16])
    bm2r = ctx.hin("bm2_row", [8, D])
    bm3 = ctx.hin("bm3_c", [8, 128, 16])

    xn1_sb = ctx.sbuf("p6_xn1", [128, C, T], BF16)
    gates = ctx.sbuf("p6_gates", [128, 16, 8], F32)
    pr.on("gpsimd", lambda e: e.dma_start(
        out=xn1_sb[:, :, :], in_=s_xn1.rearrange("(c p) n -> p c n", p=128)),
        inc=[(ctx.LG, 16)])
    pr.on("sync", lambda e: e.dma_start(
        out=gates[:, :, :].rearrange("p a b -> p (a b)"), in_=s_gates[:, :]),
        inc=[(ctx.LD, 16)])
    lg_0 = ctx.LG.n
    ld_g = ctx.LD.n

    w1e = ctx.sbuf("p6_w1e", [128, C, 2048], BF16)
    w3e = ctx.sbuf("p6_w3e", [128, C, 2048], BF16)
    w2e = ctx.sbuf("p6_w2e", [128, 16, D], BF16)
    a1e = ctx.sbuf("p6_a1e", [128, C, 16], BF16)
    a3e = ctx.sbuf("p6_a3e", [128, C, 16], BF16)
    a2e = ctx.sbuf("p6_a2e", [128, 16, 16], BF16)
    b1le = ctx.sbuf("p6_b1le", [16, 2048], BF16)
    b3le = ctx.sbuf("p6_b3le", [16, 2048], BF16)
    b2le = ctx.sbuf("p6_b2le", [16, D], BF16)
    bm1e = ctx.sbuf("p6_bm1e", [128, 16], F32)
    bm3e = ctx.sbuf("p6_bm3e", [128, 16], F32)
    bm2b = ctx.sbuf("p6_bm2b", [128, D], BF16)
    al1 = ctx.sbuf("p6_al1", [16, 1024], BF16)
    al3 = ctx.sbuf("p6_al3", [16, 1024], BF16)
    al2 = ctx.sbuf("p6_al2", [16, 1024], BF16)
    glu = ctx.sbuf("p6_glu", [128, 16, 1024], BF16)
    s1t = [ctx.sbuf(f"p6_s1t{i}", [128, 512], BF16) for i in range(2)]
    h3t = [ctx.sbuf(f"p6_h3t{i}", [128, 512], BF16) for i in range(2)]
    ytmp = ctx.sbuf("p6_ytmp", [128, D], F32)
    x2t = [ctx.sbuf(f"p6_x2t{i}", [128, D], F32) for i in range(2)]
    yout = [ctx.sbuf(f"p6_yout{i}", [128, D], F32) for i in range(2)]
    ps_al = [ctx.psum(f"p6_ps_al{i}", [16, 512], F32) for i in range(2)]
    ps_y = [ctx.psum(f"p6_ps_y{i}", [128, D], F32) for i in range(2)]
    pool = ProjPool(ctx, "p6_po", 2, 512)

    lg13 = [0] * 8
    lga = [0] * 8
    lg2 = [0] * 8
    ldb13 = [0] * 8
    ldb2 = [0] * 8

    def load_w13(ex, wait):
        for dst, src in ((w1e, w1T), (w3e, w3T)):
            pr.on("gpsimd", lambda e, dst=dst, src=src, ex=ex: e.dma_start(
                out=dst[:, :, :], in_=src[ex].rearrange("(c p) o -> p c o", p=128)),
                wait=wait, inc=[(ctx.LG, 16)])
        for dst, src in ((b1le, b1l), (b3le, b3l)):
            pr.on("gpsimd", lambda e, dst=dst, src=src, ex=ex: e.dma_start(
                out=dst[:, :], in_=src[ex]), wait=wait, inc=[(ctx.LG, 16)])
        lg13[ex] = ctx.LG.n
        for dst, src in ((bm1e, bm1), (bm3e, bm3)):
            pr.on("sync", lambda e, dst=dst, src=src, ex=ex: e.dma_start(
                out=dst[:, :], in_=src[ex]), wait=wait, inc=[(ctx.LD, 16)])
        ldb13[ex] = ctx.LD.n

    def load_a(ex, wait):
        for dst, src in ((a1e, a1T), (a3e, a3T)):
            pr.on("gpsimd", lambda e, dst=dst, src=src, ex=ex: e.dma_start(
                out=dst[:, :, :], in_=src[ex].rearrange("(c p) o -> p c o", p=128)),
                wait=wait, inc=[(ctx.LG, 16)])
        lga[ex] = ctx.LG.n

    def load_w2(ex, wait):
        pr.on("gpsimd", lambda e, ex=ex: e.dma_start(
            out=w2e[:, :, :], in_=w2T[ex].rearrange("(c p) o -> p c o", p=128)),
            wait=wait, inc=[(ctx.LG, 16)])
        pr.on("gpsimd", lambda e, ex=ex: e.dma_start(
            out=a2e[:, :, :], in_=a2T[ex].rearrange("(c p) o -> p c o", p=128)),
            wait=wait, inc=[(ctx.LG, 16)])
        pr.on("gpsimd", lambda e, ex=ex: e.dma_start(
            out=b2le[:, :], in_=b2l[ex]), wait=wait, inc=[(ctx.LG, 16)])
        lg2[ex] = ctx.LG.n
        pr.on("gpsimd", lambda e, ex=ex: e.dma_start(
            out=bm2b[:, :], in_=bm2r[ex:ex + 1, :].partition_broadcast(128)),
            wait=wait, inc=[(ctx.LG, 16)])
        lg2[ex] = ctx.LG.n
        ldb2[ex] = ctx.LD.n

    load_w13(0, [])
    load_a(0, [])
    load_w2(0, [])

    x2_guard = [[]]
    al_free = [(None, 0), (None, 0)]
    for ex in range(8):
        for hf in range(2):
            base = hf * 1024
            for i4, (asrc, dst, tg) in enumerate(
                    [(a, d, tg) for (a, d) in ((a1e, al1), (a3e, al3)) for tg in range(2)]):
                jb = i4 % 2
                off = base + tg * 512

                def mma(e, asrc=asrc, off=off, jb=jb):
                    for c in range(C):
                        i = e.matmul(ps_al[jb][0:16, 0:512], asrc[:, c, 0:16],
                                     xn1_sb[:, c, off:off + 512],
                                     start=(c == 0), stop=(c == C - 1))
                    return i
                wA = [(ctx.LG, max(lga[ex], lg_0))]
                if al_free[jb][0] is not None:
                    wA.append(al_free[jb])
                if i4 == 0:
                    wA.append((ctx.TC, ctx.TC.n))
                pr.on("tensor", mma, wait=wA, inc=[(ctx.TC, 1)])
                t_a = ctx.TC.n
                pr.on("scalar", lambda e, dst=dst, tg=tg, jb=jb: e.activation(
                    dst[0:16, tg * 512:(tg + 1) * 512], ps_al[jb][0:16, 0:512], AF.Copy),
                    wait=[(ctx.TC, t_a)], inc=[(ctx.SC, 1)])
                al_free[jb] = (ctx.SC, ctx.SC.n)
            s_al = ctx.SC.n
            if hf == 1 and ex < 7:
                load_a(ex + 1, [(ctx.TC, ctx.TC.n)])

            glu_guard = (ctx.TC, ctx.TC.n)
            for hc in range(16):
                for tg in range(2):
                    off = base + tg * 512
                    p2 = (hc * 2 + tg) % 2
                    j1, ps1, free1 = pool.acquire()
                    w1_ = [(ctx.SC, s_al), (ctx.LG, lg13[ex]), (ctx.LD, ldb13[ex])]
                    if free1[0] is not None:
                        w1_.append(free1)

                    def mm1(e, hc=hc, off=off, tg=tg, ps=ps1):
                        for c in range(C):
                            e.matmul(ps[:, 0:512], w1e[:, c, hc * 128:(hc + 1) * 128],
                                     xn1_sb[:, c, off:off + 512], start=(c == 0), stop=False)
                        return e.matmul(ps[:, 0:512], b1le[0:16, hc * 128:(hc + 1) * 128],
                                        al1[0:16, tg * 512:(tg + 1) * 512],
                                        start=False, stop=True)
                    pr.on("tensor", mm1, wait=w1_, inc=[(ctx.TC, 1)])
                    t_1 = ctx.TC.n
                    pr.on("scalar", lambda e, hc=hc, p2=p2, ps=ps1: e.activation(
                        s1t[p2][:, :], ps[:, 0:512], AF.Silu, bias=bm1e[:, hc:hc + 1]),
                        wait=[(ctx.TC, t_1), (ctx.VC, ctx.VC.n)], inc=[(ctx.SC, 1)])
                    pool.release(j1, ctx.SC, ctx.SC.n)
                    s_1 = ctx.SC.n

                    j3, ps3, free3 = pool.acquire()
                    w3_ = [(ctx.SC, s_al), (ctx.LG, lg13[ex]), (ctx.LD, ldb13[ex])]
                    if free3[0] is not None:
                        w3_.append(free3)

                    def mm3(e, hc=hc, off=off, tg=tg, ps=ps3):
                        for c in range(C):
                            e.matmul(ps[:, 0:512], w3e[:, c, hc * 128:(hc + 1) * 128],
                                     xn1_sb[:, c, off:off + 512], start=(c == 0), stop=False)
                        return e.matmul(ps[:, 0:512], b3le[0:16, hc * 128:(hc + 1) * 128],
                                        al3[0:16, tg * 512:(tg + 1) * 512],
                                        start=False, stop=True)
                    pr.on("tensor", mm3, wait=w3_, inc=[(ctx.TC, 1)])
                    t_3 = ctx.TC.n
                    pr.on("vector", lambda e, hc=hc, p2=p2, ps=ps3: e.tensor_scalar(
                        h3t[p2][:, :], ps[:, 0:512], bm3e[:, hc:hc + 1], None, ALU.add),
                        wait=[(ctx.TC, t_3), (ctx.SC, ctx.SC.n)], inc=[(ctx.VC, 1)])
                    pool.release(j3, ctx.VC, ctx.VC.n)
                    v_3 = ctx.VC.n
                    pr.on("vector", lambda e, hc=hc, tg=tg, p2=p2: e.tensor_tensor(
                        glu[:, hc, tg * 512:(tg + 1) * 512], s1t[p2][:, :],
                        h3t[p2][:, :], ALU.mult),
                        wait=[(ctx.VC, v_3), (ctx.SC, s_1), glu_guard], inc=[(ctx.VC, 1)])
            v_glu = ctx.VC.n
            if hf == 1 and ex < 7:
                load_w13(ex + 1, [(ctx.TC, ctx.TC.n), (ctx.SC, ctx.SC.n),
                                  (ctx.VC, ctx.VC.n)])

            for tg in range(2):
                jb = tg % 2

                def mma2(e, tg=tg, jb=jb):
                    for hc in range(16):
                        i = e.matmul(ps_al[jb][0:16, 0:512], a2e[:, hc, 0:16],
                                     glu[:, hc, tg * 512:(tg + 1) * 512],
                                     start=(hc == 0), stop=(hc == 15))
                    return i
                wC = [(ctx.VC, v_glu), (ctx.LG, lg2[ex])]
                if al_free[jb][0] is not None:
                    wC.append(al_free[jb])
                pr.on("tensor", mma2, wait=wC, inc=[(ctx.TC, 1)])
                t_a2 = ctx.TC.n
                pr.on("scalar", lambda e, tg=tg, jb=jb: e.activation(
                    al2[0:16, tg * 512:(tg + 1) * 512], ps_al[jb][0:16, 0:512], AF.Copy),
                    wait=[(ctx.TC, t_a2)], inc=[(ctx.SC, 1)])
                al_free[jb] = (ctx.SC, ctx.SC.n)
            s_al2 = ctx.SC.n

            x2t_guard = [(None, 0), (None, 0)]
            yps_free = [(None, 0), (None, 0)]
            yout_free = [0, 0]
            for tc in range(8):
                gch = hf * 8 + tc
                p2 = tc % 2
                gbase = base + tc * 128
                wx = [(ctx.ST, None)] + x2_guard[0]
                if x2t_guard[p2][0] is not None:
                    wx.append(x2t_guard[p2])
                pr.on("sync", lambda e, gbase=gbase, p2=p2: e.dma_start(
                    out=x2t[p2][:, :], in_=s_x2a[gbase:gbase + 128, :]),
                    wait=wx, inc=[(ctx.LD, 16)])
                ld_x2 = ctx.LD.n

                def mmy(e, tc=tc, p2=p2):
                    i = None
                    for ng in range(2):
                        for hc in range(16):
                            e.matmul(ps_y[p2][:, ng * 512:(ng + 1) * 512],
                                     glu[:, hc, tc * 128:(tc + 1) * 128],
                                     w2e[:, hc, ng * 512:(ng + 1) * 512],
                                     start=(hc == 0), stop=False)
                        i = e.matmul(ps_y[p2][:, ng * 512:(ng + 1) * 512],
                                     al2[0:16, tc * 128:(tc + 1) * 128],
                                     b2le[0:16, ng * 512:(ng + 1) * 512],
                                     start=False, stop=True)
                    return i
                wy = [(ctx.SC, s_al2), (ctx.LG, lg2[ex]), (ctx.LD, ldb2[ex])]
                if yps_free[p2][0] is not None:
                    wy.append(yps_free[p2])
                if tc == 0:
                    wy.append((ctx.VC, ctx.VC.n))
                pr.on("tensor", mmy, wait=wy, inc=[(ctx.TC, 1)])
                t_y = ctx.TC.n

                def ev_y(e, gch=gch, p2=p2, ex=ex):
                    e.tensor_tensor(ytmp[:, :], ps_y[p2][:, 0:D], bm2b[:, :], ALU.add)
                    return e.scalar_tensor_tensor(
                        yout[p2][:, :], ytmp[:, :], gates[:, gch, ex:ex + 1],
                        x2t[p2][:, :], ALU.mult, ALU.add)
                pr.on("vector", ev_y, wait=[(ctx.TC, t_y), (ctx.LD, max(ld_x2, ld_g)),
                                            (ctx.ST, yout_free[p2])],
                      inc=[(ctx.VC, 1)])
                v_y = ctx.VC.n
                x2t_guard[p2] = (ctx.VC, v_y)
                yps_free[p2] = (ctx.VC, v_y)
                dst = out if ex == 7 else s_x2a
                pr.on("gpsimd", lambda e, gbase=gbase, p2=p2, dst=dst: e.dma_start(
                    out=dst[gbase:gbase + 128, :], in_=yout[p2][:, :]),
                    wait=[(ctx.VC, v_y)], inc=[(ctx.ST, 16)])
                yout_free[p2] = ctx.ST.n
            x2_guard[0] = [(ctx.VC, ctx.VC.n)]
            if hf == 1 and ex < 7:
                load_w2(ex + 1, [(ctx.TC, ctx.TC.n), (ctx.VC, ctx.VC.n)])


PHASE_EMITTERS = {"p0": p0, "p1": p1, "p2": p2, "p3": p3, "p4": p4, "p5": p5, "p6": p6}



LAUNCH_GROUPS = [["p0", "p1", "p2", "p3", "p4", "p5", "p6"]]
NCORES = 8


def kernel(**inputs):
    per_core = prep_inputs(inputs)
    env = [dict(d) for d in per_core]
    for grp in LAUNCH_GROUPS:
        nc, in_names = build_program(grp)
        in_maps = [{k: np.ascontiguousarray(env[c][k]) for k in in_names}
                   for c in range(NCORES)]
        res = run_bass_kernel_spmd(nc, in_maps, core_ids=list(range(NCORES)))
        for c in range(NCORES):
            for k, v in res.results[c].items():
                env[c][k] = v
    out = np.zeros((B, N, D), np.float32)
    for c in range(NCORES):
        b, h = c // 2, c % 2
        out[b, own_token_idx(h), :] = env[c]["out_x2"]
    return out


# version 07 (final)
# speedup vs baseline: 1.1671x; 1.0580x over previous
import contextlib
import sys
if "/opt/trn_rl_repo" not in sys.path:
    sys.path.insert(0, "/opt/trn_rl_repo")
import numpy as np
import ml_dtypes
import concourse.bass as bass
import concourse.mybir as mybir
from concourse.bass_utils import run_bass_kernel_spmd

F32 = mybir.dt.float32
BF16 = mybir.dt.bfloat16
AF = mybir.ActivationFunctionType
ALU = mybir.AluOpType
AX = mybir.AxisListType

B, N, D = 4, 4096, 1024
T = 2048
NT = 512
C = D // 128
NT3 = 256
NEG = -1.0e9
EPS = 1e-5
BF = ml_dtypes.bfloat16


def _cols(v):
    v = np.asarray(v, np.float32)
    return np.ascontiguousarray(v.reshape(-1, 128).T)


def own_token_idx(h):
    return np.concatenate([np.arange(128) + 128 * (2 * i + h) for i in range(16)])


def prep_inputs(inp):
    f = {k: np.asarray(v, np.float32) for k, v in inp.items()}
    x = f["L0_attn_x"]

    sh = {}
    sh["wq"] = f["L0_attn_Wq"]
    sh["wk"] = f["L0_attn_Wk"]
    sh["wv"] = f["L0_attn_Wv"]
    sh["wfc"] = f["L0_attn_Wfc"]
    sh["bfc_c"] = _cols(f["L0_attn_bfc"])
    sh["g0_row"] = np.ascontiguousarray(f["L0_attn_gamma"][None, :])
    sh["b0_row"] = np.ascontiguousarray(f["L0_attn_beta"][None, :])
    sh["w1"] = f["L0_ffn_W1"]
    sh["b1_c"] = _cols(f["L0_ffn_b1"])
    sh["w2"] = f["L0_ffn_W2"]
    sh["b2_c"] = _cols(f["L0_ffn_b2"])
    sh["ln1w_c"] = _cols(f["L1_attn_ln1_w"])
    sh["ln1b_c"] = _cols(f["L1_attn_ln1_b"])
    sh["wqkT"] = np.ascontiguousarray(f["L1_attn_Wqk"].T)
    sh["bqk_c"] = _cols(f["L1_attn_bqk"])
    sh["wvT1"] = np.ascontiguousarray(f["L1_attn_Wv"].T)
    sh["bv_row"] = np.ascontiguousarray(f["L1_attn_bv"][None, :])
    sh["wgT"] = np.ascontiguousarray(f["L1_attn_Wg"].T)
    sh["bg_row"] = np.ascontiguousarray(f["L1_attn_bg"][None, :])
    sh["bg_c"] = _cols(f["L1_attn_bg"])
    sh["woT"] = np.ascontiguousarray(f["L1_attn_Wo"].T)
    sh["bo_c"] = _cols(f["L1_attn_bo"])
    sh["ln2w_c"] = _cols(f["L1_attn_ln2_w"])
    sh["ln2b_c"] = _cols(f["L1_attn_ln2_b"])
    gam, bet = f["L1_attn_gamma"], f["L1_attn_beta"]
    sh["gh_c"] = np.ascontiguousarray(gam.reshape(4, 2, 128).transpose(2, 0, 1).reshape(128, 8))
    sh["bh_c"] = np.ascontiguousarray(bet.reshape(4, 2, 128).transpose(2, 0, 1).reshape(128, 8))
    sh["g3_row"] = np.ascontiguousarray(gam[3][None, :])
    sh["b3_row"] = np.ascontiguousarray(bet[3][None, :])
    idx = np.arange(64)
    dist = np.clip(np.abs(idx[:, None] - idx[None, :]), 0, 32)
    sh["bias64"] = np.ascontiguousarray(f["L1_attn_pos_emb"][dist]).astype(np.float32)
    sh["wgmT"] = np.ascontiguousarray(f["L1_ffn_Wg"].T)
    sh["w1T"] = np.ascontiguousarray(f["L1_ffn_W1"].transpose(0, 2, 1))
    sh["w2T"] = np.ascontiguousarray(f["L1_ffn_W2"].transpose(0, 2, 1))
    sh["w3T"] = np.ascontiguousarray(f["L1_ffn_W3"].transpose(0, 2, 1))
    sh["a1T"] = np.ascontiguousarray(f["L1_ffn_A1"].transpose(0, 2, 1))
    sh["a2T"] = np.ascontiguousarray(f["L1_ffn_A2"].transpose(0, 2, 1))
    sh["a3T"] = np.ascontiguousarray(f["L1_ffn_A3"].transpose(0, 2, 1))
    sh["b1l"] = np.ascontiguousarray(f["L1_ffn_B1"].transpose(0, 2, 1))
    sh["b2l"] = np.ascontiguousarray(f["L1_ffn_B2"].transpose(0, 2, 1))
    sh["b3l"] = np.ascontiguousarray(f["L1_ffn_B3"].transpose(0, 2, 1))
    sh["bm1_c"] = np.ascontiguousarray(np.stack([_cols(v) for v in f["L1_ffn_b1"]]))
    sh["bm2_c"] = np.ascontiguousarray(np.stack([_cols(v) for v in f["L1_ffn_b2"]]))
    sh["bm3_c"] = np.ascontiguousarray(np.stack([_cols(v) for v in f["L1_ffn_b3"]]))
    sh["bm2_row"] = np.ascontiguousarray(f["L1_ffn_b2"])
    sh["onesf"] = np.ones((1, 128), np.float32)
    sh["ident"] = np.eye(128, dtype=np.float32)

    half = 128
    inv_freq = (10000.0 ** (-np.arange(half, dtype=np.float32) / half)).astype(np.float32)

    ins = []
    for c in range(8):
        b, h = c // 2, c % 2
        d = dict(sh)
        oi = own_token_idx(h)
        xt = np.ascontiguousarray(x[b].T)
        d["xT_all"] = xt
        d["xT_own"] = np.ascontiguousarray(xt[:, oi])
        masks = np.zeros((16, 2, 128, 128), np.float32)
        vsel = np.zeros((16, 2, 128, 128), np.float32)
        for i in range(16):
            g = 2 * i + h
            for jj, j in enumerate((2 * i, 2 * i + 1)):
                kk = np.arange(128)[None, :] + 128 * j
                qq = np.arange(128)[:, None] + 128 * g
                masks[i, jj] = np.where(kk <= qq, 0.0, NEG)
                if j == g:
                    vsel[i, jj] = np.eye(128, dtype=np.float32)
        d["masks"] = masks
        d["vsel"] = vsel.astype(BF)
        pos = oi.astype(np.float32)
        sinu = pos[:, None] * inv_freq[None, :]
        d["cosF"] = np.ascontiguousarray(np.cos(sinu).T.astype(np.float32))
        d["sinF"] = np.ascontiguousarray(np.sin(sinu).T.astype(np.float32))
        d["cosT"] = np.ascontiguousarray(np.cos(sinu).astype(np.float32))
        d["sinT"] = np.ascontiguousarray(np.sin(sinu).astype(np.float32))
        ins.append(d)
    return ins



ENGS = ("sync", "tensor", "vector", "scalar", "gpsimd")


class SemC:
    __slots__ = ("h", "n", "name")
    def __init__(self, h, name):
        self.h, self.n, self.name = h, 0, name


class Prog:
    def __init__(self, nc):
        self.nc = nc
        self.ops = {e: [] for e in ENGS}

    def sem(self, name):
        return SemC(self.nc.alloc_semaphore(name), name)

    def on(self, eng, fn, wait=(), inc=()):
        w = [(s, (s.n if v is None else v)) for (s, v) in wait]
        self.ops[eng].append((w, fn, list(inc)))
        for s, k in inc:
            s.n += k

    def emit(self, block):
        for ename in ENGS:
            ops = self.ops[ename]
            def section(eng, ops=ops):
                last_wait = {}
                for (waits, fn, incs) in ops:
                    for s, v in waits:
                        if v > 0 and last_wait.get(id(s), 0) < v:
                            eng.wait_ge(s.h, v)
                            last_wait[id(s)] = v
                    instr = fn(eng) if fn is not None else None
                    if incs:
                        if instr is None:
                            instr = eng.nop()
                        for s, k in incs:
                            instr.then_inc(s.h, k)
            getattr(block, ename)(section)


class Ctx:
    def __init__(self, nc, pr, hin, hscratch, out):
        self.nc, self.pr, self.hin, self.hscratch, self.out = nc, pr, hin, hscratch, out
        self.stack = contextlib.ExitStack()
        self.pstack = None
        self.LD = pr.sem("LD")
        self.LG = pr.sem("LG")
        self.ST = pr.sem("ST")
        self.TC = pr.sem("TC")
        self.VC = pr.sem("VC")
        self.SC = pr.sem("SC")
        self.CC = pr.sem("CC")
        self.consts = {}

    def sbuf(self, name, shape, dtype):
        st = self.pstack if self.pstack is not None else self.stack
        return st.enter_context(self.nc.sbuf_tensor(name, list(shape), dtype))

    def psum(self, name, shape, dtype=F32):
        st = self.pstack if self.pstack is not None else self.stack
        return st.enter_context(self.nc.psum_tensor(name, list(shape), dtype))

    def barrier(self):
        sems = [self.LD, self.LG, self.ST, self.TC, self.VC, self.SC, self.CC]
        w = [(s, None) for s in sems]
        for e in ENGS:
            self.pr.on(e, None, wait=w)


def fm_tile(hbm_ap, t0, nt):
    return hbm_ap[:, t0:t0 + nt].rearrange("(c p) n -> p c n", p=128)


def tm_tile(hbm_ap, t0, nt):
    return hbm_ap[t0:t0 + nt, :].rearrange("(s p) d -> p s d", p=128)



def emit_ln_fm(ctx, bufs, x32v, x16v, xn16v, nt, w_col=None, b_col=None, t2_out=None,
               wait0=(), add_in=None, stats_fp32=False, pool=None):
    pr, cst = ctx.pr, ctx.consts
    nC = x32v.shape[1]
    inv = 1.0 / (128 * nC)
    sq, rowb, t1 = bufs.get("sq"), bufs["row"], bufs["t1"]
    t1b = bufs.get("t1b", t1)
    row = lambda k: rowb[0:1, k * nt:(k + 1) * nt]
    if pool is not None:
        jm, ps_m, free_m = pool.acquire()
        jv, ps_v, free_v = pool.acquire()
        jmb, ps_mb, free_mb = pool.acquire()
        jrb, ps_rb, free_rb = pool.acquire()
        pool_frees = [f for f in (free_m, free_v, free_mb, free_rb) if f[0] is not None]
    else:
        ps_m, ps_v, ps_mb, ps_rb = bufs["st0"], bufs["st1"], bufs["bc0"], bufs["bc1"]
        pool_frees = []

    if stats_fp32:
        base_w = list(wait0) + [(ctx.LD, None), (ctx.LG, None)]
        def mm_x(e):
            for c in range(nC):
                i = e.matmul(ps_m[0:1, 0:nt], cst["ones32c"][:, 0:1], x32v[:, c, :],
                             start=(c == 0), stop=(c == nC - 1))
            return i
        pr.on("tensor", mm_x, wait=base_w + pool_frees, inc=[(ctx.TC, 1)])
        for c in range(nC):
            pr.on("vector", lambda e, c=c: e.tensor_tensor(
                t1b[:, 0:nt], x32v[:, c, :], x32v[:, c, :], ALU.mult),
                wait=base_w + [(ctx.TC, ctx.TC.n)], inc=[(ctx.VC, 1)])
            v_c = ctx.VC.n
            pr.on("tensor", lambda e, c=c: e.matmul(
                ps_v[0:1, 0:nt], cst["ones32c"][:, 0:1], t1b[:, 0:nt],
                start=(c == 0), stop=(c == nC - 1)), wait=[(ctx.VC, v_c)],
                inc=[(ctx.TC, 1)])
        t_st = ctx.TC.n
    else:
        pr.on("vector", lambda e: e.tensor_tensor(sq[:, 0:nC, 0:nt], x32v, x32v, ALU.mult),
              wait=list(wait0) + [(ctx.LD, None), (ctx.LG, None)], inc=[(ctx.VC, 1)])
        v_sq = ctx.VC.n

        def mm_stats(e):
            for c in range(nC):
                e.matmul(ps_m[0:1, 0:nt], cst["ones16c"][:, 0:1], x16v[:, c, :],
                         start=(c == 0), stop=(c == nC - 1))
            for c in range(nC):
                i = e.matmul(ps_v[0:1, 0:nt], cst["ones16c"][:, 0:1], sq[:, c, 0:nt],
                             start=(c == 0), stop=(c == nC - 1))
            return i
        pr.on("tensor", mm_stats, wait=[(ctx.VC, v_sq), (ctx.LD, None), (ctx.LG, None)]
              + pool_frees, inc=[(ctx.TC, 1)])
        t_st = ctx.TC.n

    def row_math(e):
        e.tensor_scalar(row(0), ps_m[0:1, 0:nt], inv, None, ALU.mult)
        e.tensor_tensor(row(3), row(0), row(0), ALU.mult)
        e.scalar_tensor_tensor(row(1), ps_v[0:1, 0:nt], inv,
                               row(3), ALU.mult, ALU.subtract)
        return e.tensor_scalar(row(3), row(1), EPS, None, ALU.add)
    pr.on("vector", row_math, wait=[(ctx.TC, t_st)], inc=[(ctx.VC, 1)])
    v_row = ctx.VC.n
    if pool is not None:
        pool.release(jm, ctx.VC, v_row)
        pool.release(jv, ctx.VC, v_row)

    def row_rstd(e):
        e.activation(row(1), row(3), AF.Ln)
        return e.activation(row(2), row(1), AF.Exp, scale=-0.5)
    pr.on("scalar", row_rstd, wait=[(ctx.VC, v_row)], inc=[(ctx.SC, 1)])
    s_r = ctx.SC.n

    def mm_bcast(e):
        e.matmul(ps_mb[:, 0:nt], cst["ones32"][0:1, :], row(0), start=True, stop=True)
        return e.matmul(ps_rb[:, 0:nt], cst["ones32"][0:1, :], row(2), start=True, stop=True)
    pr.on("tensor", mm_bcast, wait=[(ctx.SC, s_r), (ctx.VC, v_row)], inc=[(ctx.TC, 1)])
    t_bc = ctx.TC.n

    def apply(e):
        for c in range(nC):
            e.tensor_tensor(t1[:, 0:nt], x32v[:, c, :], ps_mb[:, 0:nt], ALU.subtract)
            if w_col is not None and add_in is not None:
                e.tensor_tensor(t1b[:, 0:nt], t1[:, 0:nt], ps_rb[:, 0:nt], ALU.mult)
                e.tensor_scalar(t1[:, 0:nt], t1b[:, 0:nt],
                                w_col[:, c:c + 1], b_col[:, c:c + 1], ALU.mult, ALU.add)
                i = e.tensor_tensor(xn16v[:, c, :], t1[:, 0:nt], add_in[:, c, :], ALU.add)
            elif w_col is not None:
                e.tensor_tensor(t1b[:, 0:nt], t1[:, 0:nt], ps_rb[:, 0:nt], ALU.mult)
                i = e.tensor_scalar(xn16v[:, c, :], t1b[:, 0:nt],
                                    w_col[:, c:c + 1], b_col[:, c:c + 1], ALU.mult, ALU.add)
            elif t2_out is not None:
                e.tensor_tensor(t2_out[:, c, :], t1[:, 0:nt], ps_rb[:, 0:nt], ALU.mult)
                i = e.tensor_copy(xn16v[:, c, :], t2_out[:, c, :])
            else:
                i = e.tensor_tensor(xn16v[:, c, :], t1[:, 0:nt], ps_rb[:, 0:nt], ALU.mult)
        return i
    pr.on("vector", apply, wait=[(ctx.TC, t_bc)], inc=[(ctx.VC, 1)])
    if pool is not None:
        pool.release(jmb, ctx.VC, ctx.VC.n)
        pool.release(jrb, ctx.VC, ctx.VC.n)
    return ctx.VC.n


class ProjPool:
    def __init__(self, ctx, name, n, nt, dtype=F32):
        self.ctx = ctx
        self.tiles = [ctx.psum(f"{name}{i}", [128, nt], dtype) for i in range(n)]
        self.free_at = [(None, 0)] * n
        self.i = 0

    def acquire(self):
        j = self.i % len(self.tiles)
        self.i += 1
        return j, self.tiles[j], self.free_at[j]

    def release(self, j, sem, count):
        self.free_at[j] = (sem, count)


def emit_proj_fm(ctx, w_sb, act16, nt, pool, ready, evac, n_out_chunks, extra_wait=(),
                 evac_eng="scalar", evac_wait=()):
    pr = ctx.pr
    nC = act16.shape[1]
    for mo in range(n_out_chunks):
        j, ps, free = pool.acquire()
        w = list(ready) + list(extra_wait)
        if free[0] is not None:
            w.append((free[0], free[1]))

        def mm(e, mo=mo, ps=ps):
            for c in range(nC):
                i = e.matmul(ps[:, 0:nt], w_sb[:, c, mo * 128:(mo + 1) * 128],
                             act16[:, c, 0:nt], start=(c == 0), stop=(c == nC - 1))
            return i
        pr.on("tensor", mm, wait=w, inc=[(ctx.TC, 1)])
        t_mm = ctx.TC.n
        esem = ctx.SC if evac_eng == "scalar" else ctx.VC
        pr.on(evac_eng, lambda e, mo=mo, ps=ps: evac(e, mo, ps),
              wait=[(ctx.TC, t_mm)] + list(evac_wait), inc=[(esem, 1)])
        pool.release(j, esem, esem.n)
    return esem.n



SCRATCH = {
    "s_xn": ([D, T], BF16, "p0", ["p3"]),
    "s_q": ([D, T], BF16, "p0", ["p1"]),
    "s_k": ([D, N], BF16, "p0", ["p1"]),
    "s_v": ([N, D], BF16, "p0", ["p1"]),
    "s_af": ([T, D], BF16, "p1", ["p2"]),
    "s_x1a": ([D, T], F32, "p2", ["p3"]),
    "s_x1": ([D, T], F32, "p3", ["p4", "p5"]),
    "s_xn1": ([D, T], BF16, "p4", ["p6"]),
    "s_u": ([D, T], BF16, "p4", ["p5"]),
    "s_qkf": ([256, T], BF16, "p4", ["p5"]),
    "s_vt": ([T, 2048], BF16, "p4", ["p5"]),
    "s_lkv": ([256, 2048], F32, "p4", ["p5"]),
    "s_x2a": ([T, D], F32, "p5", ["p6"]),
    "s_gates": ([128, 128], F32, "p4", ["p6"]),
}


def build_program(phases):
    nc = bass.Bass("TRN2", target_bir_lowering=False)
    pr = Prog(nc)
    hbm = {}
    in_names = []

    def hin(name, shape, dtype=F32):
        if name not in hbm:
            hbm[name] = nc.dram_tensor(name, list(shape), dtype, kind="ExternalInput").ap()
            in_names.append(name)
        return hbm[name]

    def hscratch(name):
        if name not in hbm:
            shape, dt, prod, cons = SCRATCH[name]
            if prod in phases:
                kind = "Internal" if all(cp in phases for cp in cons) else "ExternalOutput"
            else:
                kind = "ExternalInput"
                in_names.append(name)
            hbm[name] = nc.dram_tensor(name, list(shape), dt, kind=kind).ap()
        return hbm[name]

    out = None
    if "p6" in phases:
        out = nc.dram_tensor("out_x2", [T, D], F32, kind="ExternalOutput").ap()

    ctx = Ctx(nc, pr, hin, hscratch, out)
    with ctx.stack:
        _emit_prolog(ctx)
        for ph in phases:
            ctx.barrier()
            with contextlib.ExitStack() as ps:
                ctx.pstack = ps
                PHASE_EMITTERS[ph](ctx)
                ctx.pstack = None
        _emit_epilog(ctx)
        with nc.Block() as block:
            pr.emit(block)
    return nc, in_names


def _emit_prolog(ctx):
    pr, hin = ctx.pr, ctx.hin
    cst = ctx.consts
    cst["ones32"] = ctx.sbuf("c_ones32", [1, 128], F32)
    cst["ones16"] = ctx.sbuf("c_ones16", [1, 128], BF16)
    cst["ones16c"] = ctx.sbuf("c_ones16c", [128, 1], BF16)
    cst["ones32c"] = ctx.sbuf("c_ones32c", [128, 1], F32)
    cst["ident32"] = ctx.sbuf("c_ident32", [128, 128], F32)
    cst["ident16"] = ctx.sbuf("c_ident16", [128, 128], BF16)
    onesf = hin("onesf", [1, 128])
    ident = hin("ident", [128, 128])
    pr.on("sync", lambda e: e.dma_start(out=cst["ones32"][:, :], in_=onesf), inc=[(ctx.LD, 16)])
    pr.on("sync", lambda e: e.dma_start(out=cst["ident32"][:, :], in_=ident), inc=[(ctx.LD, 16)])
    pr.on("gpsimd", lambda e: e.dma_start(out=cst["ones16"][:, :], in_=onesf), inc=[(ctx.LG, 16)])
    pr.on("gpsimd", lambda e: e.dma_start(
        out=cst["ones16c"][:, :], in_=onesf[0:1, 0:1].partition_broadcast(128)),
        inc=[(ctx.LG, 16)])
    pr.on("sync", lambda e: e.dma_start(
        out=cst["ones32c"][:, :], in_=onesf[0:1, 0:1].partition_broadcast(128)),
        inc=[(ctx.LD, 16)])
    pr.on("gpsimd", lambda e: e.dma_start(out=cst["ident16"][:, :], in_=ident), inc=[(ctx.LG, 16)])


def _emit_epilog(ctx):
    pr = ctx.pr
    pr.on("sync", None, wait=[(ctx.ST, None), (ctx.LD, None)])
    pr.on("gpsimd", None, wait=[(ctx.ST, None), (ctx.LG, None)])



def p0(ctx):
    pr = ctx.pr
    xa = ctx.hin("xT_all", [D, N])
    xo = ctx.hin("xT_own", [D, T])
    s_k, s_v = ctx.hscratch("s_k"), ctx.hscratch("s_v")
    s_q, s_xn = ctx.hscratch("s_q"), ctx.hscratch("s_xn")

    w_sb = {}
    for nm in ("wq", "wk", "wv"):
        src = ctx.hin(nm, [D, D])
        w_sb[nm] = ctx.sbuf(f"p0_{nm}", [128, C, D], BF16)
        pr.on("gpsimd", lambda e, dst=w_sb[nm], src=src: e.dma_start(
            out=dst[:, :, :], in_=src.rearrange("(c p) o -> p c o", p=128)),
            inc=[(ctx.LG, 16)])

    xa32 = [ctx.sbuf(f"p0_xa32_{i}", [128, C, NT], F32) for i in range(2)]
    xa16 = [ctx.sbuf(f"p0_xa16_{i}", [128, C, NT], BF16) for i in range(2)]
    lnb = [dict(sq=ctx.sbuf(f"p0_sq_{i}", [128, C, NT], BF16),
                row=ctx.sbuf(f"p0_row_{i}", [1, 4 * NT], F32),
                t1=ctx.sbuf(f"p0_t1_{i}", [128, NT], F32)) for i in range(2)]
    xn16 = [ctx.sbuf(f"p0_xn_{i}", [128, C, NT], BF16) for i in range(2)]
    kbuf = [ctx.sbuf(f"p0_kb_{i}", [128, C, NT], BF16) for i in range(2)]
    vbuf = [ctx.sbuf(f"p0_vb_{i}", [128, NT // 128, D], BF16) for i in range(2)]
    pool = ProjPool(ctx, "p0_po", 6, NT)

    buf_ok = [[], []]

    def do_tile(src, t, own):
        p = t % 2
        w_guard = list(buf_ok[p])
        pr.on("sync", lambda e, t=t, p=p, src=src: e.dma_start(
            out=xa32[p][:, :, :], in_=fm_tile(src, t * NT, NT)),
            wait=w_guard, inc=[(ctx.LD, 16)])
        ld_p0 = ctx.LD.n
        pr.on("vector", lambda e, p=p: e.tensor_copy(xa16[p][:, :, :], xa32[p][:, :, :]),
              wait=w_guard + [(ctx.LD, ld_p0)], inc=[(ctx.VC, 1)])
        v_xn = emit_ln_fm(ctx, lnb[p], xa32[p][:, :, :], xa16[p][:, :, :],
                          xn16[p][:, :, :], NT, wait0=w_guard, pool=pool)
        ready = [(ctx.VC, v_xn)]
        st_marks = []
        if own:
            pr.on("gpsimd", lambda e, t=t, p=p: e.dma_start(
                out=fm_tile(s_xn, t * NT, NT), in_=xn16[p][:, :, :]),
                wait=ready, inc=[(ctx.ST, 16)])
            st_marks.append(ctx.ST.n)

            def ev_q(e, mo, ps, p=p):
                return e.activation(kbuf[p][:, mo, 0:NT], ps[:, 0:NT], AF.Copy,
                                    scale=1.0 / 32.0)
            emit_proj_fm(ctx, w_sb["wq"], xn16[p][:, :, :], NT, pool, ready, ev_q, C)
            s_done = ctx.SC.n
            pr.on("gpsimd", lambda e, t=t, p=p: e.dma_start(
                out=fm_tile(s_q, t * NT, NT), in_=kbuf[p][:, :, :]),
                wait=[(ctx.SC, s_done)], inc=[(ctx.ST, 16)])
            st_marks.append(ctx.ST.n)
        else:
            def ev_k(e, mo, ps, p=p):
                return e.activation(kbuf[p][:, mo, 0:NT], ps[:, 0:NT], AF.Copy)
            emit_proj_fm(ctx, w_sb["wk"], xn16[p][:, :, :], NT, pool, ready, ev_k, C)
            s_done = ctx.SC.n
            pr.on("gpsimd", lambda e, t=t, p=p: e.dma_start(
                out=fm_tile(s_k, t * NT, NT), in_=kbuf[p][:, :, :]),
                wait=[(ctx.SC, s_done)], inc=[(ctx.ST, 16)])
            st_marks.append(ctx.ST.n)
            for sub in range(NT // 128):
                for half in range(2):
                    j, ps, free = pool.acquire()
                    w = list(ready) + ([free] if free[0] is not None else [])

                    def mmv(e, sub=sub, half=half, ps=ps, p=p):
                        for c in range(C):
                            i = e.matmul(ps[:, 0:NT],
                                         xn16[p][:, c, sub * 128:(sub + 1) * 128],
                                         w_sb["wv"][:, c, half * NT:(half + 1) * NT],
                                         start=(c == 0), stop=(c == C - 1))
                        return i
                    pr.on("tensor", mmv, wait=w, inc=[(ctx.TC, 1)])
                    t_mm = ctx.TC.n
                    pr.on("scalar", lambda e, sub=sub, half=half, ps=ps, p=p: e.activation(
                        vbuf[p][:, sub, half * NT:(half + 1) * NT], ps[:, 0:NT], AF.Copy),
                        wait=[(ctx.TC, t_mm)], inc=[(ctx.SC, 1)])
                    pool.release(j, ctx.SC, ctx.SC.n)
            s_done = ctx.SC.n
            pr.on("gpsimd", lambda e, t=t, p=p: e.dma_start(
                out=tm_tile(s_v, t * NT, NT), in_=vbuf[p][:, :, :]),
                wait=[(ctx.SC, s_done)], inc=[(ctx.ST, 16)])
            st_marks.append(ctx.ST.n)
        buf_ok[p] = [(ctx.TC, ctx.TC.n), (ctx.VC, ctx.VC.n), (ctx.SC, ctx.SC.n),
                     (ctx.ST, max(st_marks))]

    for t in range(N // NT):
        do_tile(xa, t, own=False)
    for t in range(T // NT):
        do_tile(xo, t, own=True)


P1_DEBUG = False


def p1(ctx):
    pr = ctx.pr
    cst = ctx.consts
    s_k, s_v = ctx.hscratch("s_k"), ctx.hscratch("s_v")
    s_q, s_af = ctx.hscratch("s_q"), ctx.hscratch("s_af")
    masks = ctx.hin("masks", [16, 2, 128, 128])
    vsel = ctx.hin("vsel", [16, 2, 128, 128], BF16)
    g0r = ctx.hin("g0_row", [1, D])
    b0r = ctx.hin("b0_row", [1, D])

    k_sb = ctx.sbuf("p1_k", [128, C, N], BF16)
    v_sb = ctx.sbuf("p1_v", [128, N // 128, D], BF16)
    q_sb = [ctx.sbuf(f"p1_q{i}", [128, C, 128], BF16) for i in range(2)]
    mk_sb = [ctx.sbuf(f"p1_mk{i}", [128, 2, 128], F32) for i in range(2)]
    vs_sb = ctx.sbuf("p1_vs", [128, 2, 128], BF16)
    vs_guard = [0]
    srow_b = [ctx.sbuf(f"p1_srow{i}", [128, N], F32) for i in range(2)]
    prow_b = [ctx.sbuf(f"p1_prow{i}", [128, N], BF16) for i in range(2)]
    prow2_b = [ctx.sbuf(f"p1_prow2{i}", [128, N], BF16) for i in range(2)]
    mch_b = [ctx.sbuf(f"p1_mch{i}", [128, 16], F32) for i in range(2)]
    mneg_b = [ctx.sbuf(f"p1_mneg{i}", [128, 1], F32) for i in range(2)]
    lcol_b = [ctx.sbuf(f"p1_l{i}", [128, 1], F32) for i in range(2)]
    rcol_b = [ctx.sbuf(f"p1_r{i}", [128, 1], F32) for i in range(2)]
    pt_sb = [ctx.sbuf(f"p1_pt{i}", [128, 128], BF16) for i in range(2)]
    afb = ctx.sbuf("p1_af", [128, D], BF16)
    g0b = ctx.sbuf("p1_g0b", [128, D], BF16)
    b0b = ctx.sbuf("p1_b0b", [128, D], BF16)
    m2c = ctx.sbuf("p1_m2c", [128, 1], F32)
    v2c = ctx.sbuf("p1_v2c", [128, 1], F32)
    m2a = ctx.sbuf("p1_m2a", [128, 1], F32)
    v2a = ctx.sbuf("p1_v2a", [128, 1], F32)
    v2b = ctx.sbuf("p1_v2b", [128, 1], F32)
    ps_s = [ctx.psum(f"p1_ps_s{i}", [128, 512]) for i in range(4)]
    ps_o = ctx.psum("p1_ps_o", [128, D])
    ps_t = [ctx.psum(f"p1_ps_t{i}", [128, 128], BF16) for i in range(2)]

    pr.on("gpsimd", lambda e: e.dma_start(
        out=k_sb[:, :, :], in_=s_k.rearrange("(c p) n -> p c n", p=128)), inc=[(ctx.LG, 16)])
    pr.on("gpsimd", lambda e: e.dma_start(
        out=v_sb[:, :, :], in_=s_v.rearrange("(j p) d -> p j d", p=128)), inc=[(ctx.LG, 16)])
    pr.on("gpsimd", lambda e: e.dma_start(
        out=g0b[:, :], in_=g0r.partition_broadcast(128)), inc=[(ctx.LG, 16)])
    pr.on("gpsimd", lambda e: e.dma_start(
        out=b0b[:, :], in_=b0r.partition_broadcast(128)), inc=[(ctx.LG, 16)])

    qload_guard = [[], []]
    ps_s_free = [(None, 0)] * 4
    pt_ps_free = [(None, 0), (None, 0)]
    pt_sb_free = [(None, 0), (None, 0)]
    srow_guard = [[], []]
    st_af_prev = [0]

    for i in range(16):
        nk = 2 * i + 2
        ncol = nk * 128
        nkc = (ncol + 511) // 512
        pq = i % 2
        srow, prow, prow2 = srow_b[pq], prow_b[pq], prow2_b[pq]
        mch, mneg, lcol, rcol = mch_b[pq], mneg_b[pq], lcol_b[pq], rcol_b[pq]
        fin32 = srow[:, 0:D]
        fin32b = srow[:, D:2 * D]
        pr.on("sync", lambda e, i=i, pq=pq: e.dma_start(
            out=q_sb[pq][:, :, :],
            in_=s_q[:, i * 128:(i + 1) * 128].rearrange("(c p) n -> p c n", p=128)),
            wait=qload_guard[pq], inc=[(ctx.LD, 16)])
        pr.on("sync", lambda e, i=i, pq=pq: e.dma_start(
            out=mk_sb[pq][:, :, :], in_=masks[i].rearrange("j q k -> q j k")),
            wait=qload_guard[pq], inc=[(ctx.LD, 16)])
        pr.on("sync", lambda e, i=i: e.dma_start(
            out=vs_sb[:, :, :], in_=vsel[i].rearrange("j q k -> q j k")),
            wait=qload_guard[pq] + [(ctx.VC, vs_guard[0])], inc=[(ctx.LD, 16)])
        ld_q = ctx.LD.n
        lg_q = ctx.LG.n

        for kc in range(nkc):
            c0 = kc * 512
            cw = min(512, ncol - c0)
            jp, ps, free = kc % 4, ps_s[kc % 4], ps_s_free[kc % 4]
            w = [(ctx.LD, ld_q), (ctx.LG, None)]
            if free[0] is not None:
                w.append(free)

            def mm_s(e, i=i, pq=pq, ps=ps, c0=c0, cw=cw):
                for c in range(C):
                    r = e.matmul(ps[:, 0:cw], q_sb[pq][:, c, :], k_sb[:, c, c0:c0 + cw],
                                 start=(c == 0), stop=(c == C - 1))
                return r
            pr.on("tensor", mm_s, wait=w, inc=[(ctx.TC, 1)])
            t_s = ctx.TC.n

            last = (kc == nkc - 1)

            def ev_s(e, ps=ps, c0=c0, cw=cw, last=last, pq=pq, i=i, ncol=ncol, kc=kc):
                e.tensor_copy(srow[:, c0:c0 + cw], ps[:, 0:cw])
                if last:
                    e.tensor_tensor(srow[:, ncol - 256:ncol], srow[:, ncol - 256:ncol],
                                    mk_sb[pq][:, :, :].rearrange("p j k -> p (j k)"), ALU.add)
                return e.reduce_max(mch[:, kc:kc + 1], srow[:, c0:c0 + cw], axis=AX.X)
            pr.on("vector", ev_s, wait=[(ctx.TC, t_s)] + srow_guard[pq],
                  inc=[(ctx.VC, 1)])
            ps_s_free[jp] = (ctx.VC, ctx.VC.n)

        def red_m(e, nkc=nkc):
            if nkc < 16:
                e.memset(mch[:, nkc:16], -1.0e30)
            e.drain()
            return e.reduce_max(mneg[:, 0:1], mch[:, 0:16], axis=AX.X, negate=True)
        pr.on("vector", red_m, inc=[(ctx.VC, 1)])
        v_m = ctx.VC.n
        pr.on("scalar", lambda e, ncol=ncol: e.activation(
            prow[:, 0:ncol], srow[:, 0:ncol], AF.Exp, bias=mneg[:, 0:1]),
            wait=[(ctx.VC, v_m), (ctx.TC, ctx.TC.n)],
            inc=[(ctx.SC, 1)])
        s_exp = ctx.SC.n

        def p_fix(e, ncol=ncol, pq=pq):
            e.reduce_sum(lcol[:, 0:1], prow[:, 0:ncol], axis=AX.X)
            e.drain()
            e.reciprocal(rcol[:, 0:1], lcol[:, 0:1])
            e.drain()
            if ncol > 256:
                e.tensor_scalar(prow2[:, 0:ncol - 256], prow[:, 0:ncol - 256],
                                rcol[:, 0:1], None, ALU.mult)
            return e.scalar_tensor_tensor(
                prow2[:, ncol - 256:ncol], prow[:, ncol - 256:ncol], rcol[:, 0:1],
                vs_sb[:, :, :].rearrange("p j k -> p (j k)"), ALU.mult, ALU.add)
        pr.on("vector", p_fix, wait=[(ctx.SC, s_exp), (ctx.LD, ld_q),
                                     (ctx.TC, ctx.TC.n)], inc=[(ctx.VC, 1)])
        v_p = ctx.VC.n
        vs_guard[0] = v_p

        mm_done_at = {}
        for j in range(nk):
            jp = j % 2
            w = [(ctx.VC, v_p)]
            if pt_ps_free[jp][0] is not None:
                w.append(pt_ps_free[jp])
            pr.on("tensor", lambda e, j=j, jp=jp: e.transpose(
                ps_t[jp][:, :], prow2[:, j * 128:(j + 1) * 128], cst["ident16"][:, :]),
                wait=w, inc=[(ctx.TC, 1)])
            t_t = ctx.TC.n
            wv = [(ctx.TC, t_t)]
            if pt_sb_free[jp][0] is not None:
                wv.append(pt_sb_free[jp])
            pr.on("scalar", lambda e, jp=jp: e.activation(
                pt_sb[jp][:, :], ps_t[jp][:, :], AF.Copy), wait=wv, inc=[(ctx.SC, 1)])
            pt_ps_free[jp] = (ctx.SC, ctx.SC.n)
            mm_done_at[j] = (ctx.SC.n, jp)
            if j >= 1:
                _emit_pv_mm(ctx, i, j - 1, nk, mm_done_at, v_sb, pt_sb, ps_o, pt_sb_free)
        _emit_pv_mm(ctx, i, nk - 1, nk, mm_done_at, v_sb, pt_sb, ps_o, pt_sb_free)
        t_pv = ctx.TC.n

        def ln2(e):
            e.tensor_copy(fin32, ps_o[:, :])
            e.reduce_sum(m2a[:, 0:1], fin32, axis=AX.X)
            e.drain()
            e.tensor_scalar(m2c[:, 0:1], m2a[:, 0:1], 1.0 / D, None, ALU.mult)
            e.drain()
            e.tensor_scalar(fin32b, fin32, m2c[:, 0:1], None, ALU.subtract)
            e.tensor_tensor(fin32, fin32b, fin32b, ALU.mult)
            e.reduce_sum(v2a[:, 0:1], fin32, axis=AX.X)
            e.drain()
            return e.tensor_scalar(v2c[:, 0:1], v2a[:, 0:1], 1.0 / D, EPS, ALU.mult, ALU.add)
        pr.on("vector", ln2, wait=[(ctx.TC, t_pv), (ctx.LD, None)], inc=[(ctx.VC, 1)])
        v_ln = ctx.VC.n

        def rstd2(e):
            e.activation(v2b[:, 0:1], v2c[:, 0:1], AF.Ln)
            e.drain()
            return e.activation(v2c[:, 0:1], v2b[:, 0:1], AF.Exp, scale=-0.5)
        pr.on("scalar", rstd2, wait=[(ctx.VC, v_ln)], inc=[(ctx.SC, 1)])
        s_r2 = ctx.SC.n

        def fin_aff(e):
            e.scalar_tensor_tensor(fin32, fin32b, v2c[:, 0:1], g0b[:, :],
                                   ALU.mult, ALU.mult)
            return e.tensor_tensor(afb[:, :], fin32, b0b[:, :], ALU.add)
        pr.on("vector", fin_aff, wait=[(ctx.SC, s_r2), (ctx.ST, st_af_prev[0])],
              inc=[(ctx.VC, 1)])
        v_fin = ctx.VC.n

        pr.on("gpsimd", lambda e, i=i: e.dma_start(
            out=s_af[i * 128:(i + 1) * 128, :], in_=afb[:, :]),
            wait=[(ctx.VC, v_fin)], inc=[(ctx.ST, 16)])
        st_af_prev[0] = ctx.ST.n
        if P1_DEBUG and i == 1:
            nc = ctx.nc
            d_s = nc.dram_tensor("dbg_srow", [128, 512], F32, kind="ExternalOutput").ap()
            d_p = nc.dram_tensor("dbg_prow", [128, 512], BF16, kind="ExternalOutput").ap()
            d_m = nc.dram_tensor("dbg_m", [128, 4], F32, kind="ExternalOutput").ap()
            d_o = nc.dram_tensor("dbg_o", [128, D], F32, kind="ExternalOutput").ap()
            d_l = nc.dram_tensor("dbg_l", [3, 128, 1], F32, kind="ExternalOutput").ap()
            def dbg(e):
                e.dma_start(out=d_s, in_=srow[:, 0:512]).then_inc(ctx.ST.h, 16)
                e.dma_start(out=d_p, in_=prow[:, 0:512]).then_inc(ctx.ST.h, 16)
                e.dma_start(out=d_m, in_=mch[:, 0:4]).then_inc(ctx.ST.h, 16)
                e.dma_start(out=d_l[0], in_=lcol[:, 0:1]).then_inc(ctx.ST.h, 16)
                e.dma_start(out=d_l[1], in_=rcol[:, 0:1]).then_inc(ctx.ST.h, 16)
                e.dma_start(out=d_l[2], in_=mneg[:, 0:1]).then_inc(ctx.ST.h, 16)
                return e.dma_start(out=d_o, in_=fin32[:, :])
            pr.on("sync", dbg, wait=[(ctx.VC, ctx.VC.n)], inc=[(ctx.ST, 16)])
            ctx.ST.n += 80
            stall = ctx.ST.n
            pr.on("vector", None, wait=[(ctx.ST, stall)], inc=[(ctx.VC, 1)])
            pr.on("scalar", None, wait=[(ctx.ST, stall)], inc=[(ctx.SC, 1)])
            pr.on("tensor", None, wait=[(ctx.ST, stall)], inc=[(ctx.TC, 1)])
        qload_guard[pq] = [(ctx.TC, ctx.TC.n), (ctx.VC, ctx.VC.n), (ctx.SC, ctx.SC.n)]


def _emit_pv_mm(ctx, i, j, nk, mm_done_at, v_sb, pt_sb, ps_o, pt_sb_free):
    pr = ctx.pr
    s_c, jp = mm_done_at[j]

    def mm(e, j=j, jp=jp, nk=nk):
        e.matmul(ps_o[:, 0:512], pt_sb[jp][:, :], v_sb[:, j, 0:512],
                 start=(j == 0), stop=(j == nk - 1))
        return e.matmul(ps_o[:, 512:1024], pt_sb[jp][:, :], v_sb[:, j, 512:1024],
                        start=(j == 0), stop=(j == nk - 1))
    pr.on("tensor", mm, wait=[(ctx.SC, s_c)], inc=[(ctx.TC, 1)])
    pt_sb_free[jp] = (ctx.TC, ctx.TC.n)


def p2(ctx):
    pr, cst = ctx.pr, ctx.consts
    s_af = ctx.hscratch("s_af")
    s_x1a = ctx.hscratch("s_x1a")
    xo = ctx.hin("xT_own", [D, T])
    wfc = ctx.hin("wfc", [D, D])
    bfc = ctx.hin("bfc_c", [128, 8])

    w_sb = ctx.sbuf("p2_wfc", [128, C, D], BF16)
    bfc_sb = ctx.sbuf("p2_bfc", [128, 8], F32)
    pr.on("gpsimd", lambda e: e.dma_start(
        out=w_sb[:, :, :], in_=wfc.rearrange("(c p) o -> p c o", p=128)), inc=[(ctx.LG, 16)])
    lg_w2 = ctx.LG.n
    pr.on("sync", lambda e: e.dma_start(out=bfc_sb[:, :], in_=bfc), inc=[(ctx.LD, 16)])

    af_tm = [ctx.sbuf(f"p2_af{i}", [128, 4, D], BF16) for i in range(2)]
    af_fm = [ctx.sbuf(f"p2_aff{i}", [128, C, NT], BF16) for i in range(2)]
    x32 = [ctx.sbuf(f"p2_x{i}", [128, C, NT], F32) for i in range(2)]
    x1a = [ctx.sbuf(f"p2_x1a{i}", [128, C, NT], F32) for i in range(2)]
    ps_t = [ctx.psum(f"p2_pt{i}", [128, 128], BF16) for i in range(2)]
    pool = ProjPool(ctx, "p2_po", 2, NT)

    buf_ok = [[], []]
    pt_free = [(None, 0), (None, 0)]
    for t in range(T // NT):
        p = t % 2
        w_guard = list(buf_ok[p])
        pr.on("sync", lambda e, t=t, p=p: e.dma_start(
            out=af_tm[p][:, :, :], in_=tm_tile(s_af, t * NT, NT)),
            wait=w_guard, inc=[(ctx.LD, 16)])
        pr.on("sync", lambda e, t=t, p=p: e.dma_start(
            out=x32[p][:, :, :], in_=fm_tile(xo, t * NT, NT)),
            wait=w_guard, inc=[(ctx.LD, 16)])
        ld_t = ctx.LD.n
        for sub in range(4):
            for c in range(C):
                jp = (sub * C + c) % 2
                w = [(ctx.LD, ld_t)]
                if pt_free[jp][0] is not None:
                    w.append(pt_free[jp])
                pr.on("tensor", lambda e, p=p, sub=sub, c=c, jp=jp: e.transpose(
                    ps_t[jp][:, :], af_tm[p][:, sub, c * 128:(c + 1) * 128],
                    cst["ident16"][:, :]), wait=w, inc=[(ctx.TC, 1)])
                t_t = ctx.TC.n
                pr.on("vector", lambda e, p=p, sub=sub, c=c, jp=jp: e.tensor_copy(
                    af_fm[p][:, c, sub * 128:(sub + 1) * 128], ps_t[jp][:, :]),
                    wait=[(ctx.TC, t_t)], inc=[(ctx.VC, 1)])
                pt_free[jp] = (ctx.VC, ctx.VC.n)
        v_tr = ctx.VC.n

        def ev_fc(e, mo, ps, p=p):
            return e.scalar_tensor_tensor(x1a[p][:, mo, 0:NT], ps[:, 0:NT],
                                          bfc_sb[:, mo:mo + 1], x32[p][:, mo, 0:NT],
                                          ALU.add, ALU.add)
        emit_proj_fm(ctx, w_sb, af_fm[p][:, :, :], NT, pool,
                     [(ctx.VC, v_tr), (ctx.LG, lg_w2)], ev_fc, C,
                     evac_eng="vector", evac_wait=[(ctx.LD, ld_t)])
        v_add = ctx.VC.n
        pr.on("gpsimd", lambda e, t=t, p=p: e.dma_start(
            out=fm_tile(s_x1a, t * NT, NT), in_=x1a[p][:, :, :]),
            wait=[(ctx.VC, v_add)], inc=[(ctx.ST, 16)])
        buf_ok[p] = [(ctx.TC, ctx.TC.n), (ctx.VC, ctx.VC.n), (ctx.SC, ctx.SC.n),
                     (ctx.ST, ctx.ST.n)]


def p3(ctx):
    pr = ctx.pr
    s_xn = ctx.hscratch("s_xn")
    s_x1a = ctx.hscratch("s_x1a")
    s_x1 = ctx.hscratch("s_x1")
    w1 = ctx.hin("w1", [D, 4096])
    w2 = ctx.hin("w2", [4096, D])
    b1 = ctx.hin("b1_c", [128, 32])
    b2 = ctx.hin("b2_c", [128, 8])

    w1_sb = ctx.sbuf("p3_w1", [128, C, 4096], BF16)
    w2_sb = ctx.sbuf("p3_w2", [128, 32, D], BF16)
    b1_sb = ctx.sbuf("p3_b1", [128, 32], F32)
    b2_sb = ctx.sbuf("p3_b2", [128, 8], F32)
    pr.on("gpsimd", lambda e: e.dma_start(
        out=w1_sb[:, :, :], in_=w1.rearrange("(c p) o -> p c o", p=128)), inc=[(ctx.LG, 16)])
    pr.on("gpsimd", lambda e: e.dma_start(
        out=w2_sb[:, :, :], in_=w2.rearrange("(c p) o -> p c o", p=128)), inc=[(ctx.LG, 16)])
    lg_w3 = ctx.LG.n
    pr.on("sync", lambda e: e.dma_start(out=b1_sb[:, :], in_=b1), inc=[(ctx.LD, 16)])
    pr.on("sync", lambda e: e.dma_start(out=b2_sb[:, :], in_=b2), inc=[(ctx.LD, 16)])

    xn_sb = [ctx.sbuf(f"p3_xn{i}", [128, C, NT3], BF16) for i in range(2)]
    h_sb = ctx.sbuf("p3_h", [128, 32, NT3], BF16)
    x1a32 = ctx.sbuf("p3_x1a", [128, C, NT3], F32)
    x1b = ctx.sbuf("p3_x1b", [128, C, NT3], F32)
    pool = ProjPool(ctx, "p3_po", 2, NT3)
    h_guard = [0]
    x1a_guard = [[]]
    x1b_guard = [[]]

    buf_ok = [[], []]
    for t in range(T // NT3):
        p = t % 2
        w_guard = list(buf_ok[p])
        pr.on("sync", lambda e, t=t, p=p: e.dma_start(
            out=xn_sb[p][:, :, :], in_=fm_tile(s_xn, t * NT3, NT3)),
            wait=w_guard, inc=[(ctx.LD, 16)])
        pr.on("sync", lambda e, t=t, p=p: e.dma_start(
            out=x1a32[:, :, :], in_=fm_tile(s_x1a, t * NT3, NT3)),
            wait=w_guard + x1a_guard[0], inc=[(ctx.LD, 16)])
        ld_t = ctx.LD.n

        def ev_h(e, mo, ps, p=p):
            return e.activation(h_sb[:, mo, 0:NT3], ps[:, 0:NT3], AF.Relu,
                                bias=b1_sb[:, mo:mo + 1])
        emit_proj_fm(ctx, w1_sb, xn_sb[p][:, :, :], NT3, pool,
                     [(ctx.LD, ld_t), (ctx.TC, h_guard[0]), (ctx.LG, lg_w3)], ev_h, 32)
        s_h = ctx.SC.n

        def ev_y(e, mo, ps, p=p):
            return e.scalar_tensor_tensor(x1b[:, mo, 0:NT3], ps[:, 0:NT3],
                                          b2_sb[:, mo:mo + 1], x1a32[:, mo, 0:NT3],
                                          ALU.add, ALU.add)
        emit_proj_fm(ctx, w2_sb, h_sb[:, :, :], NT3, pool, [(ctx.SC, s_h)], ev_y, C,
                     evac_eng="vector", evac_wait=[(ctx.LD, ld_t)] + x1b_guard[0])
        v_add = ctx.VC.n
        h_guard[0] = ctx.TC.n
        x1a_guard[0] = [(ctx.VC, v_add)]
        pr.on("gpsimd", lambda e, t=t, p=p: e.dma_start(
            out=fm_tile(s_x1, t * NT3, NT3), in_=x1b[:, :, :]),
            wait=[(ctx.VC, v_add)], inc=[(ctx.ST, 16)])
        x1b_guard[0] = [(ctx.ST, ctx.ST.n)]
        buf_ok[p] = [(ctx.TC, ctx.TC.n), (ctx.VC, ctx.VC.n), (ctx.SC, ctx.SC.n),
                     (ctx.ST, ctx.ST.n)]


def p4(ctx):
    pr, cst = ctx.pr, ctx.consts
    nc = ctx.nc
    s_x1 = ctx.hscratch("s_x1")
    s_xn1 = ctx.hscratch("s_xn1")
    s_u = ctx.hscratch("s_u")
    s_qkf = ctx.hscratch("s_qkf")
    s_vt = ctx.hscratch("s_vt")
    s_lkv = ctx.hscratch("s_lkv")
    wqkT = ctx.hin("wqkT", [D, 256])
    wvT1 = ctx.hin("wvT1", [D, 2048])
    bqk = ctx.hin("bqk_c", [128, 2])
    ln1w = ctx.hin("ln1w_c", [128, 8])
    ln1b = ctx.hin("ln1b_c", [128, 8])
    g3r = ctx.hin("g3_row", [1, 256])
    b3r = ctx.hin("b3_row", [1, 256])
    bvr = ctx.hin("bv_row", [1, 2048])
    cosT = ctx.hin("cosT", [T, 128])
    sinT = ctx.hin("sinT", [T, 128])
    cc_in = nc.dram_tensor("cc_in", [256, 2048], F32, kind="Internal").ap()
    cc_out = nc.dram_tensor("cc_out", [256, 2048], F32, kind="Internal").ap()

    s_gates = ctx.hscratch("s_gates")
    wgmT = ctx.hin("wgmT", [D, 8])
    wgm_sb = ctx.sbuf("p4_wgm", [128, C, 8], BF16)
    gt_sc = ctx.sbuf("p4_gt_sc", [128, 8], F32)
    gt_m1 = ctx.sbuf("p4_gt_m1", [128, 1], F32)
    gt_m1n = ctx.sbuf("p4_gt_m1n", [128, 1], F32)
    gt_m2 = ctx.sbuf("p4_gt_m2", [128, 1], F32)
    gt_tk = ctx.sbuf("p4_gt_tk", [128, 8], F32)
    gt_s2 = ctx.sbuf("p4_gt_s2", [128, 8], F32)
    gt_kp = ctx.sbuf("p4_gt_kp", [128, 8], F32)
    gt_eu = ctx.sbuf("p4_gt_eu", [128, 8], F32)
    gt_gk = ctx.sbuf("p4_gt_gk", [128, 8], F32)
    gt_dn = ctx.sbuf("p4_gt_dn", [128, 1], F32)
    gt_rd = ctx.sbuf("p4_gt_rd", [128, 1], F32)
    gt_out = ctx.sbuf("p4_gt_out", [128, 16, 8], F32)
    wqk_sb = ctx.sbuf("p4_wqk", [128, C, 256], BF16)
    wv_sb = ctx.sbuf("p4_wv", [128, C, 2048], BF16)
    bqk_sb = ctx.sbuf("p4_bqk", [128, 2], F32)
    ln1w_sb = ctx.sbuf("p4_l1w", [128, 8], F32)
    ln1b_sb = ctx.sbuf("p4_l1b", [128, 8], F32)
    g3b = ctx.sbuf("p4_g3b", [128, 256], F32)
    b3b = ctx.sbuf("p4_b3b", [128, 256], F32)
    bvb = ctx.sbuf("p4_bvb", [128, 2048], F32)
    cosT_sb = ctx.sbuf("p4_cosT", [128, 16, 128], BF16)
    sinT_sb = ctx.sbuf("p4_sinT", [128, 16, 128], BF16)
    pr.on("gpsimd", lambda e: e.dma_start(
        out=wgm_sb[:, :, :], in_=wgmT.rearrange("(c p) o -> p c o", p=128)), inc=[(ctx.LG, 16)])
    pr.on("gpsimd", lambda e: e.dma_start(
        out=wqk_sb[:, :, :], in_=wqkT.rearrange("(c p) o -> p c o", p=128)), inc=[(ctx.LG, 16)])
    pr.on("gpsimd", lambda e: e.dma_start(
        out=wv_sb[:, :, :], in_=wvT1.rearrange("(c p) o -> p c o", p=128)), inc=[(ctx.LG, 16)])
    pr.on("gpsimd", lambda e: e.dma_start(
        out=cosT_sb[:, :, :], in_=cosT.rearrange("(s p) f -> p s f", p=128)), inc=[(ctx.LG, 16)])
    pr.on("gpsimd", lambda e: e.dma_start(
        out=sinT_sb[:, :, :], in_=sinT.rearrange("(s p) f -> p s f", p=128)), inc=[(ctx.LG, 16)])
    pr.on("sync", lambda e: e.dma_start(out=bqk_sb[:, :], in_=bqk), inc=[(ctx.LD, 16)])
    pr.on("sync", lambda e: e.dma_start(out=ln1w_sb[:, :], in_=ln1w), inc=[(ctx.LD, 16)])
    pr.on("sync", lambda e: e.dma_start(out=ln1b_sb[:, :], in_=ln1b), inc=[(ctx.LD, 16)])
    pr.on("sync", lambda e: e.dma_start(out=g3b[:, :], in_=g3r.partition_broadcast(128)), inc=[(ctx.LD, 16)])
    pr.on("sync", lambda e: e.dma_start(out=b3b[:, :], in_=b3r.partition_broadcast(128)), inc=[(ctx.LD, 16)])
    pr.on("sync", lambda e: e.dma_start(out=bvb[:, :], in_=bvr.partition_broadcast(128)), inc=[(ctx.LD, 16)])
    lg_w = ctx.LG.n
    ld_w = ctx.LD.n

    x32 = ctx.sbuf("p4_x32", [128, C, NT], F32)
    x16 = ctx.sbuf("p4_x16", [128, C, NT], BF16)
    t2 = ctx.sbuf("p4_t2", [128, C, NT], F32)
    xn1_16 = ctx.sbuf("p4_xn1", [128, C, NT], BF16)
    u16 = ctx.sbuf("p4_u", [128, C, NT], BF16)
    qkf = ctx.sbuf("p4_qkf", [128, 2, NT], BF16)
    qkt = ctx.sbuf("p4_qkt", [128, 4, 256], BF16)
    qg = ctx.sbuf("p4_qg", [128, 256], BF16)
    qh = ctx.sbuf("p4_qh", [128, 256], BF16)
    lkr = ctx.sbuf("p4_lkr", [128, 4, 256], BF16)
    rta = ctx.sbuf("p4_rta", [128, 128], BF16)
    rtb = ctx.sbuf("p4_rtb", [128, 128], BF16)
    vtm = ctx.sbuf("p4_vtm", [128, 4, 2048], BF16)
    vtmp = ctx.sbuf("p4_vtmp", [128, 512], F32)
    lkv = ctx.sbuf("p4_lkv", [128, 2, 2048], F32)
    lkv2 = ctx.sbuf("p4_lkv2", [128, 2, 2048], F32)
    lnb = {"sq": ctx.sbuf("p4_sq", [128, C, NT], BF16),
           "row": ctx.sbuf("p4_row", [1, 4 * NT], F32),
           "t1": ctx.sbuf("p4_t1", [128, NT], F32),
           "t1b": ctx.sbuf("p4_t1b", [128, NT], F32)}
    ps_t = [ctx.psum(f"p4_pt{i}", [128, 128], BF16) for i in range(2)]
    pool = ProjPool(ctx, "p4_po", 5, NT)

    buf_ok = [[]]
    pt_free = [(None, 0), (None, 0)]
    for t in range(T // NT):
        w_guard = list(buf_ok[0])
        pr.on("sync", lambda e, t=t: e.dma_start(
            out=x32[:, :, :], in_=fm_tile(s_x1, t * NT, NT)),
            wait=w_guard, inc=[(ctx.LD, 16)])
        ld_t, lg_t = ctx.LD.n, ctx.LG.n
        pr.on("vector", lambda e: e.tensor_copy(x16[:, :, :], x32[:, :, :]),
              wait=w_guard + [(ctx.LD, ld_t)], inc=[(ctx.VC, 1)])

        v_xn1 = emit_ln_fm(ctx, lnb, x32[:, :, :], x16[:, :, :], xn1_16[:, :, :],
                           NT, t2_out=t2[:, :, :], wait0=w_guard, pool=pool)
        pr.on("gpsimd", lambda e, t=t: e.dma_start(
            out=fm_tile(s_xn1, t * NT, NT), in_=xn1_16[:, :, :]),
            wait=[(ctx.VC, v_xn1)], inc=[(ctx.ST, 16)])
        gt_guard = [0]
        for q in range(4):
            tch = t * 4 + q
            jg, ps_gt, freeg = pool.acquire()

            def mmgt(e, q=q, ps_gt=ps_gt):
                for c in range(C):
                    i = e.matmul(ps_gt[:, 0:8], xn1_16[:, c, q * 128:(q + 1) * 128],
                                 wgm_sb[:, c, 0:8], start=(c == 0), stop=(c == C - 1))
                return i
            wgt = [(ctx.VC, v_xn1), (ctx.LG, lg_w), (ctx.VC, gt_guard[0])]
            if freeg[0] is not None:
                wgt.append(freeg)
            pr.on("tensor", mmgt, wait=wgt, inc=[(ctx.TC, 1)])
            t_gt = ctx.TC.n

            def top2(e, tch=tch, ps_gt=ps_gt):
                e.tensor_copy(gt_sc[:, :], ps_gt[:, 0:8])
                e.drain()
                e.reduce_max(gt_m1[:, 0:1], gt_sc[:, :], axis=AX.X)
                e.drain()
                e.tensor_scalar(gt_m1n[:, 0:1], gt_m1[:, 0:1], -1.0, None, ALU.mult)
                e.tensor_scalar(gt_tk[:, :], gt_sc[:, :], gt_m1[:, 0:1], None, ALU.is_ge)
                e.drain()
                e.scalar_tensor_tensor(gt_s2[:, :], gt_tk[:, :], -1.0e9, gt_sc[:, :],
                                       ALU.mult, ALU.add)
                e.drain()
                e.reduce_max(gt_m2[:, 0:1], gt_s2[:, :], axis=AX.X)
                e.drain()
                return e.tensor_scalar(gt_kp[:, :], gt_sc[:, :], gt_m2[:, 0:1], None, ALU.is_ge)
            pr.on("vector", top2, wait=[(ctx.TC, t_gt)], inc=[(ctx.VC, 1)])
            v_t2 = ctx.VC.n
            pool.release(jg, ctx.VC, v_t2)
            pr.on("scalar", lambda e: e.activation(gt_eu[:, :], gt_sc[:, :], AF.Exp,
                                                   bias=gt_m1n[:, 0:1]),
                  wait=[(ctx.VC, v_t2)], inc=[(ctx.SC, 1)])
            s_eu = ctx.SC.n

            def gfin(e, tch=tch):
                e.tensor_tensor(gt_gk[:, :], gt_eu[:, :], gt_kp[:, :], ALU.mult)
                e.drain()
                e.reduce_sum(gt_dn[:, 0:1], gt_gk[:, :], axis=AX.X)
                e.drain()
                e.reciprocal(gt_rd[:, 0:1], gt_dn[:, 0:1])
                e.drain()
                return e.tensor_scalar(gt_out[:, tch % 4 + (tch // 4) * 4, :],
                                       gt_gk[:, :], gt_rd[:, 0:1], None, ALU.mult)
            pr.on("vector", gfin, wait=[(ctx.SC, s_eu)], inc=[(ctx.VC, 1)])
            gt_guard[0] = ctx.VC.n
        if t == T // NT - 1:
            pass
        v_u = emit_ln_fm(ctx, lnb, t2[:, :, :], xn1_16[:, :, :], u16[:, :, :],
                         NT, w_col=ln1w_sb, b_col=ln1b_sb,
                         wait0=[(ctx.VC, v_xn1), (ctx.TC, ctx.TC.n)], pool=pool)
        pr.on("gpsimd", lambda e, t=t: e.dma_start(
            out=fm_tile(s_u, t * NT, NT), in_=u16[:, :, :]),
            wait=[(ctx.VC, v_u)], inc=[(ctx.ST, 16)])

        def ev_qk(e, mo, ps):
            return e.activation(qkf[:, mo, 0:NT], ps[:, 0:NT], AF.Silu,
                                bias=bqk_sb[:, mo:mo + 1])
        emit_proj_fm(ctx, wqk_sb, u16[:, :, :], NT, pool,
                     [(ctx.VC, v_u), (ctx.LG, lg_w), (ctx.LD, ld_w)], ev_qk, 2)
        s_qk = ctx.SC.n
        pr.on("gpsimd", lambda e, t=t: e.dma_start(
            out=s_qkf[:, t * NT:(t + 1) * NT].rearrange("(c p) n -> p c n", p=128),
            in_=qkf[:, :, :]), wait=[(ctx.SC, s_qk)], inc=[(ctx.ST, 16)])

        for sub in range(4):
            for ch in range(2):
                jp = (sub * 2 + ch) % 2
                w = [(ctx.SC, s_qk)]
                if pt_free[jp][0] is not None:
                    w.append(pt_free[jp])
                pr.on("tensor", lambda e, sub=sub, ch=ch, jp=jp: e.transpose(
                    ps_t[jp][:, :], qkf[:, ch, sub * 128:(sub + 1) * 128],
                    cst["ident16"][:, :]), wait=w, inc=[(ctx.TC, 1)])
                t_t = ctx.TC.n
                pr.on("vector", lambda e, sub=sub, ch=ch, jp=jp: e.tensor_copy(
                    qkt[:, sub, ch * 128:(ch + 1) * 128], ps_t[jp][:, :]),
                    wait=[(ctx.TC, t_t)], inc=[(ctx.VC, 1)])
                pt_free[jp] = (ctx.VC, ctx.VC.n)
        v_qkt = ctx.VC.n

        def mk_lk(e, t=t):
            i = None
            for sub in range(4):
                ts_ = t * 4 + sub
                e.tensor_tensor(qg[:, 0:256], qkt[:, sub, :], g3b[:, :], ALU.mult)
                e.tensor_tensor(qh[:, 0:256], qg[:, 0:256], b3b[:, :], ALU.add)
                e.tensor_tensor(rta[:, :], qh[:, 0:128], cosT_sb[:, ts_, :], ALU.mult)
                e.tensor_tensor(rtb[:, :], qh[:, 128:256], sinT_sb[:, ts_, :], ALU.mult)
                e.tensor_tensor(lkr[:, sub, 0:128], rta[:, :], rtb[:, :], ALU.subtract)
                e.tensor_tensor(rta[:, :], qh[:, 128:256], cosT_sb[:, ts_, :], ALU.mult)
                e.tensor_tensor(rtb[:, :], qh[:, 0:128], sinT_sb[:, ts_, :], ALU.mult)
                i = e.tensor_tensor(lkr[:, sub, 128:256], rta[:, :], rtb[:, :], ALU.add)
            return i
        pr.on("vector", mk_lk, wait=[(ctx.VC, v_qkt), (ctx.LG, lg_w)], inc=[(ctx.VC, 1)])
        v_lk = ctx.VC.n

        for sub in range(4):
            for g in range(4):
                j, ps, free = pool.acquire()
                w = [(ctx.VC, v_u), (ctx.LG, lg_w)]
                if free[0] is not None:
                    w.append(free)

                def mmv(e, sub=sub, g=g, ps=ps):
                    for c in range(C):
                        i = e.matmul(ps[:, 0:512], u16[:, c, sub * 128:(sub + 1) * 128],
                                     wv_sb[:, c, g * 512:(g + 1) * 512],
                                     start=(c == 0), stop=(c == C - 1))
                    return i
                pr.on("tensor", mmv, wait=w, inc=[(ctx.TC, 1)])
                t_mm = ctx.TC.n
                pr.on("vector", lambda e, sub=sub, g=g, ps=ps: e.tensor_tensor(
                    vtmp[:, 0:512], ps[:, 0:512], bvb[:, g * 512:(g + 1) * 512], ALU.add),
                    wait=[(ctx.TC, t_mm), (ctx.LD, ld_w), (ctx.SC, ctx.SC.n)],
                    inc=[(ctx.VC, 1)])
                v_vt = ctx.VC.n
                pr.on("scalar", lambda e, sub=sub, g=g: e.activation(
                    vtm[:, sub, g * 512:(g + 1) * 512], vtmp[:, 0:512], AF.Silu),
                    wait=[(ctx.VC, v_vt)], inc=[(ctx.SC, 1)])
                pool.release(j, ctx.SC, ctx.SC.n)
        s_v = ctx.SC.n
        pr.on("gpsimd", lambda e, t=t: e.dma_start(
            out=tm_tile(s_vt, t * NT, NT), in_=vtm[:, :, :]),
            wait=[(ctx.SC, s_v)], inc=[(ctx.ST, 16)])

        for dc in range(2):
            for g in range(4):
                j, ps, free = pool.acquire()
                w = [(ctx.VC, v_lk), (ctx.SC, s_v)]
                if free[0] is not None:
                    w.append(free)

                def mml(e, dc=dc, g=g, ps=ps):
                    for sub in range(4):
                        i = e.matmul(ps[:, 0:512], lkr[:, sub, dc * 128:(dc + 1) * 128],
                                     vtm[:, sub, g * 512:(g + 1) * 512],
                                     start=(sub == 0), stop=(sub == 3))
                    return i
                pr.on("tensor", mml, wait=w, inc=[(ctx.TC, 1)])
                t_mm = ctx.TC.n
                if t == 0:
                    pr.on("vector", lambda e, dc=dc, g=g, ps=ps: e.tensor_copy(
                        lkv[:, dc, g * 512:(g + 1) * 512], ps[:, 0:512]),
                        wait=[(ctx.TC, t_mm)], inc=[(ctx.VC, 1)])
                else:
                    pr.on("vector", lambda e, dc=dc, g=g, ps=ps: e.tensor_tensor(
                        lkv[:, dc, g * 512:(g + 1) * 512],
                        lkv[:, dc, g * 512:(g + 1) * 512], ps[:, 0:512], ALU.add),
                        wait=[(ctx.TC, t_mm)], inc=[(ctx.VC, 1)])
                pool.release(j, ctx.VC, ctx.VC.n)
        buf_ok[0] = [(ctx.TC, ctx.TC.n), (ctx.VC, ctx.VC.n), (ctx.SC, ctx.SC.n),
                     (ctx.ST, ctx.ST.n)]

    pr.on("gpsimd", lambda e: e.dma_start(
        out=s_gates[:, :], in_=gt_out[:, :, :].rearrange("p a b -> p (a b)")),
        wait=[(ctx.VC, None)], inc=[(ctx.ST, 16)])
    pr.on("vector", lambda e: e.tensor_scalar(
        lkv2[:, :, :], lkv[:, :, :], 1.0 / N, None, ALU.mult), inc=[(ctx.VC, 1)])
    v_sc = ctx.VC.n
    pr.on("gpsimd", lambda e: e.dma_start(
        out=cc_in.rearrange("(c p) o -> p c o", p=128), in_=lkv2[:, :, :]),
        wait=[(ctx.VC, v_sc)], inc=[(ctx.LG, 16)])
    lg_cc0 = ctx.LG.n
    pr.on("gpsimd", lambda e: e.collective_compute(
        "AllReduce", ALU.add, [[0, 1], [2, 3], [4, 5], [6, 7]],
        [cc_in.opt()], [cc_out.opt()]),
        wait=[(ctx.LG, lg_cc0)], inc=[(ctx.CC, 1)])
    pr.on("gpsimd", lambda e: e.dma_start(
        out=lkv2[:, :, :], in_=cc_out.rearrange("(c p) o -> p c o", p=128)),
        wait=[(ctx.CC, 1), (ctx.VC, v_sc)], inc=[(ctx.LG, 16)])
    lg_cc = ctx.LG.n
    pr.on("gpsimd", lambda e: e.dma_start(
        out=s_lkv.rearrange("(c p) o -> p c o", p=128), in_=lkv2[:, :, :]),
        wait=[(ctx.LG, lg_cc)], inc=[(ctx.ST, 16)])


def p5(ctx):
    pr, cst = ctx.pr, ctx.consts
    s_x1 = ctx.hscratch("s_x1")
    s_u = ctx.hscratch("s_u")
    s_qkf = ctx.hscratch("s_qkf")
    s_vt = ctx.hscratch("s_vt")
    s_lkv = ctx.hscratch("s_lkv")
    s_x2a = ctx.hscratch("s_x2a")
    wgT = ctx.hin("wgT", [D, 2048])
    woT = ctx.hin("woT", [2048, D])
    bg = ctx.hin("bg_c", [128, 16])
    bo = ctx.hin("bo_c", [128, 8])
    ln2w = ctx.hin("ln2w_c", [128, 8])
    ln2b = ctx.hin("ln2b_c", [128, 8])
    ghc = ctx.hin("gh_c", [128, 8])
    bhc = ctx.hin("bh_c", [128, 8])
    b64 = ctx.hin("bias64", [64, 64])
    cosF = ctx.hin("cosF", [128, T])
    sinF = ctx.hin("sinF", [128, T])

    wg_sb = ctx.sbuf("p5_wg", [128, C, 2048], BF16)
    wo_sb = ctx.sbuf("p5_wo", [128, 16, D], BF16)
    lkv16 = ctx.sbuf("p5_lkv", [128, 2, 2048], BF16)
    cosF_sb = ctx.sbuf("p5_cosF", [128, T], BF16)
    sinF_sb = ctx.sbuf("p5_sinF", [128, T], BF16)
    bg_sb = ctx.sbuf("p5_bg", [128, 16], F32)
    bo_sb = ctx.sbuf("p5_bo", [128, 8], F32)
    l2w_sb = ctx.sbuf("p5_l2w", [128, 8], F32)
    l2b_sb = ctx.sbuf("p5_l2b", [128, 8], F32)
    ghc_sb = ctx.sbuf("p5_ghc", [128, 8], F32)
    bhc_sb = ctx.sbuf("p5_bhc", [128, 8], F32)
    b64_sb = ctx.sbuf("p5_b64", [64, 64], F32)
    pr.on("gpsimd", lambda e: e.dma_start(
        out=wg_sb[:, :, :], in_=wgT.rearrange("(c p) o -> p c o", p=128)), inc=[(ctx.LG, 16)])
    pr.on("gpsimd", lambda e: e.dma_start(
        out=wo_sb[:, :, :], in_=woT.rearrange("(c p) o -> p c o", p=128)), inc=[(ctx.LG, 16)])
    pr.on("gpsimd", lambda e: e.dma_start(
        out=lkv16[:, :, :], in_=s_lkv.rearrange("(c p) o -> p c o", p=128)), inc=[(ctx.LG, 16)])
    pr.on("gpsimd", lambda e: e.dma_start(out=cosF_sb[:, :], in_=cosF), inc=[(ctx.LG, 16)])
    pr.on("gpsimd", lambda e: e.dma_start(out=sinF_sb[:, :], in_=sinF), inc=[(ctx.LG, 16)])

    for nm, src, dst in (("bg", bg, bg_sb), ("bo", bo, bo_sb), ("l2w", ln2w, l2w_sb),
                         ("l2b", ln2b, l2b_sb), ("ghc", ghc, ghc_sb), ("bhc", bhc, bhc_sb),
                         ("b64", b64, b64_sb)):
        pr.on("sync", lambda e, src=src, dst=dst: e.dma_start(out=dst[:, :], in_=src),
              inc=[(ctx.LD, 16)])
    lg_w = ctx.LG.n
    ld_w = ctx.LD.n

    u16 = ctx.sbuf("p5_u", [128, C, NT], BF16)
    qkf = ctx.sbuf("p5_qkf", [128, 2, NT], BF16)
    x1_32 = ctx.sbuf("p5_x1", [128, C, NT], F32)
    vtm = ctx.sbuf("p5_vtm", [64, 8, 2048], BF16)
    qpre = ctx.sbuf("p5_qpre", [128, 2, NT], BF16)
    qqr = ctx.sbuf("p5_qqr", [128, 2, NT], BF16)
    qkkr = ctx.sbuf("p5_qkkr", [128, 2, NT], BF16)
    lqr = ctx.sbuf("p5_lqr", [128, 2, NT], BF16)
    rta = ctx.sbuf("p5_rta", [128, NT], BF16)
    rtb = ctx.sbuf("p5_rtb", [128, NT], BF16)
    gate_t = [ctx.sbuf(f"p5_gate{i}", [128, NT], BF16) for i in range(2)]
    ag = ctx.sbuf("p5_ag", [64, 64], F32)
    rg = ctx.sbuf("p5_rg", [64, 64], F32)
    a2_sb = ctx.sbuf("p5_a2", [64, 8 * 64], BF16)
    ob = ctx.sbuf("p5_ob", [128, 16, NT], BF16)
    at32 = ctx.sbuf("p5_at32", [128, C, NT], F32)
    x2at = ctx.sbuf("p5_x2at", [128, D], F32)
    lnb = {"row": ctx.sbuf("p5_row", [1, 4 * NT], F32),
           "t1": ctx.sbuf("p5_t1", [128, NT], F32),
           "t1b": ctx.sbuf("p5_t1b", [128, NT], F32)}
    ps_tf = ctx.psum("p5_ps_tf", [128, 128], F32)
    pool = ProjPool(ctx, "p5_po", 5, NT)

    buf_ok = [[]]
    tf_free = [(None, 0)]
    for t in range(T // NT):
        w_guard = list(buf_ok[0])
        pr.on("sync", lambda e, t=t: e.dma_start(
            out=u16[:, :, :], in_=fm_tile(s_u, t * NT, NT)),
            wait=w_guard, inc=[(ctx.LD, 16)])
        pr.on("sync", lambda e, t=t: e.dma_start(
            out=qkf[:, :, :],
            in_=s_qkf[:, t * NT:(t + 1) * NT].rearrange("(c p) n -> p c n", p=128)),
            wait=w_guard, inc=[(ctx.LD, 16)])
        pr.on("sync", lambda e, t=t: e.dma_start(
            out=x1_32[:, :, :], in_=fm_tile(s_x1, t * NT, NT)),
            wait=w_guard, inc=[(ctx.LD, 16)])
        pr.on("sync", lambda e, t=t: e.dma_start(
            out=vtm[:, :, :],
            in_=s_vt[t * NT:(t + 1) * NT, :].rearrange("(s p) d -> p s d", p=64)),
            wait=w_guard, inc=[(ctx.LD, 16)])
        ld_t = ctx.LD.n
        lg_t = lg_w

        def mk_heads(e, t=t):
            i = None
            for head, dst in ((0, qqr), (2, qkkr), (1, lqr)):
                for ch in range(2):
                    e.tensor_scalar(qpre[:, ch, 0:NT], qkf[:, ch, 0:NT],
                                    ghc_sb[:, head * 2 + ch:head * 2 + ch + 1],
                                    bhc_sb[:, head * 2 + ch:head * 2 + ch + 1],
                                    ALU.mult, ALU.add)
                cs = cosF_sb[:, t * NT:(t + 1) * NT]
                sn = sinF_sb[:, t * NT:(t + 1) * NT]
                e.tensor_tensor(rta[:, 0:NT], qpre[:, 0, 0:NT], cs, ALU.mult)
                e.tensor_tensor(rtb[:, 0:NT], qpre[:, 1, 0:NT], sn, ALU.mult)
                e.tensor_tensor(dst[:, 0, 0:NT], rta[:, 0:NT], rtb[:, 0:NT], ALU.subtract)
                e.tensor_tensor(rta[:, 0:NT], qpre[:, 1, 0:NT], cs, ALU.mult)
                e.tensor_tensor(rtb[:, 0:NT], qpre[:, 0, 0:NT], sn, ALU.mult)
                i = e.tensor_tensor(dst[:, 1, 0:NT], rta[:, 0:NT], rtb[:, 0:NT], ALU.add)
            return i
        pr.on("vector", mk_heads, wait=[(ctx.LD, ld_t), (ctx.LG, lg_t),
                                        (ctx.TC, ctx.TC.n)], inc=[(ctx.VC, 1)])
        v_heads = ctx.VC.n

        for g in range(8):
            j, ps, free = pool.acquire()
            w = [(ctx.VC, v_heads)]
            if free[0] is not None:
                w.append(free)

            def mmq(e, g=g, ps=ps):
                for dc in range(2):
                    i = e.matmul(ps[0:64, 0:64], qkkr[:, dc, g * 64:(g + 1) * 64],
                                 qqr[:, dc, g * 64:(g + 1) * 64],
                                 start=(dc == 0), stop=(dc == 1))
                return i
            pr.on("tensor", mmq, wait=w, inc=[(ctx.TC, 1)])
            t_q = ctx.TC.n

            def ev_q(e, g=g, ps=ps):
                e.scalar_tensor_tensor(ag[0:64, 0:64], ps[0:64, 0:64], 1.0 / 64.0,
                                       b64_sb[0:64, 0:64], ALU.mult, ALU.add)
                e.drain()
                e.tensor_relu(rg[0:64, 0:64], ag[0:64, 0:64])
                e.drain()
                return e.tensor_tensor(a2_sb[0:64, g * 64:(g + 1) * 64],
                                       rg[0:64, 0:64], rg[0:64, 0:64], ALU.mult)
            pr.on("vector", ev_q, wait=[(ctx.TC, t_q), (ctx.LD, ld_w)], inc=[(ctx.VC, 1)])
            pool.release(j, ctx.VC, ctx.VC.n)
        v_a2 = ctx.VC.n

        gate_free = [(None, 0), (None, 0)]
        for ec in range(16):
            gp = ec % 2
            jg, psg, freeg = pool.acquire()
            wg = [(ctx.LD, ld_t), (ctx.LG, lg_w)]
            if freeg[0] is not None:
                wg.append(freeg)

            def mmg(e, ec=ec, psg=psg):
                for c in range(C):
                    i = e.matmul(psg[:, 0:NT], wg_sb[:, c, ec * 128:(ec + 1) * 128],
                                 u16[:, c, 0:NT], start=(c == 0), stop=(c == C - 1))
                return i
            pr.on("tensor", mmg, wait=wg, inc=[(ctx.TC, 1)])
            t_g = ctx.TC.n
            wgv = [(ctx.TC, t_g)]
            if gate_free[gp][0] is not None:
                wgv.append(gate_free[gp])
            pr.on("scalar", lambda e, ec=ec, psg=psg, gp=gp: e.activation(
                gate_t[gp][:, 0:NT], psg[:, 0:NT], AF.Silu, bias=bg_sb[:, ec:ec + 1]),
                wait=wgv, inc=[(ctx.SC, 1)])
            pool.release(jg, ctx.SC, ctx.SC.n)
            s_g = ctx.SC.n

            jo, pso, freeo = pool.acquire()
            w = [(ctx.VC, v_a2), (ctx.LG, lg_w)]
            if freeo[0] is not None:
                w.append(freeo)

            def mmo(e, ec=ec, ps=pso):
                for dc in range(2):
                    e.matmul(ps[:, 0:NT], lkv16[:, dc, ec * 128:(ec + 1) * 128],
                             lqr[:, dc, 0:NT], start=(dc == 0), stop=False)
                for g in range(8):
                    i = e.matmul(ps[:, g * 64:(g + 1) * 64],
                                 vtm[0:64, g, ec * 128:(ec + 1) * 128],
                                 a2_sb[0:64, g * 64:(g + 1) * 64],
                                 start=False, stop=(g == 7))
                return i
            pr.on("tensor", mmo, wait=w, inc=[(ctx.TC, 1)])
            t_o = ctx.TC.n
            pr.on("vector", lambda e, ec=ec, ps=pso, gp=gp: e.scalar_tensor_tensor(
                ob[:, ec, 0:NT], ps[:, 0:NT], 1.0, gate_t[gp][:, 0:NT],
                ALU.mult, ALU.mult),
                wait=[(ctx.TC, t_o), (ctx.SC, s_g)], inc=[(ctx.VC, 1)])
            gate_free[gp] = (ctx.VC, ctx.VC.n)
            pool.release(jo, ctx.VC, ctx.VC.n)
        v_ob = ctx.VC.n

        def ev_wo(e, mo, ps):
            return e.tensor_scalar(at32[:, mo, 0:NT], ps[:, 0:NT],
                                   bo_sb[:, mo:mo + 1], None, ALU.add)
        emit_proj_fm(ctx, wo_sb, ob[:, :, :], NT, pool, [(ctx.VC, v_ob)], ev_wo, C,
                     evac_eng="vector")
        s_at = ctx.VC.n

        v_x2 = emit_ln_fm(ctx, lnb, at32[:, :, :], None, at32[:, :, :],
                          NT, w_col=l2w_sb, b_col=l2b_sb, add_in=x1_32[:, :, :],
                          wait0=[(ctx.VC, s_at), (ctx.TC, ctx.TC.n)], stats_fp32=True,
                          pool=pool)

        for sub in range(4):
            st_prev = ctx.ST.n
            for c in range(C):
                w = [(ctx.VC, v_x2)]
                if tf_free[0][0] is not None:
                    w.append(tf_free[0])
                pr.on("tensor", lambda e, sub=sub, c=c: e.transpose(
                    ps_tf[:, :], at32[:, c, sub * 128:(sub + 1) * 128],
                    cst["ident32"][:, :]), wait=w, inc=[(ctx.TC, 1)])
                t_t = ctx.TC.n
                pr.on("vector", lambda e, sub=sub, c=c: e.tensor_copy(
                    x2at[:, c * 128:(c + 1) * 128], ps_tf[:, :]),
                    wait=[(ctx.TC, t_t), (ctx.ST, st_prev)], inc=[(ctx.VC, 1)])
                tf_free[0] = (ctx.VC, ctx.VC.n)
            v_tr = ctx.VC.n
            pr.on("gpsimd", lambda e, t=t, sub=sub: e.dma_start(
                out=s_x2a[t * NT + sub * 128:t * NT + (sub + 1) * 128, :],
                in_=x2at[:, :]), wait=[(ctx.VC, v_tr)], inc=[(ctx.ST, 16)])
        buf_ok[0] = [(ctx.TC, ctx.TC.n), (ctx.VC, ctx.VC.n), (ctx.SC, ctx.SC.n),
                     (ctx.ST, ctx.ST.n)]


def p6(ctx):
    pr, cst = ctx.pr, ctx.consts
    s_xn1 = ctx.hscratch("s_xn1")
    s_x2a = ctx.hscratch("s_x2a")
    s_gates = ctx.hscratch("s_gates")
    out = ctx.out
    w1T = ctx.hin("w1T", [8, D, 2048])
    w2T = ctx.hin("w2T", [8, 2048, D])
    w3T = ctx.hin("w3T", [8, D, 2048])
    a1T = ctx.hin("a1T", [8, D, 16])
    a2T = ctx.hin("a2T", [8, 2048, 16])
    a3T = ctx.hin("a3T", [8, D, 16])
    b1l = ctx.hin("b1l", [8, 16, 2048])
    b2l = ctx.hin("b2l", [8, 16, D])
    b3l = ctx.hin("b3l", [8, 16, 2048])
    bm1 = ctx.hin("bm1_c", [8, 128, 16])
    bm2r = ctx.hin("bm2_row", [8, D])
    bm3 = ctx.hin("bm3_c", [8, 128, 16])

    xn1_sb = ctx.sbuf("p6_xn1", [128, C, T], BF16)
    gates = ctx.sbuf("p6_gates", [128, 16, 8], F32)
    pr.on("gpsimd", lambda e: e.dma_start(
        out=xn1_sb[:, :, :], in_=s_xn1.rearrange("(c p) n -> p c n", p=128)),
        inc=[(ctx.LG, 16)])
    pr.on("sync", lambda e: e.dma_start(
        out=gates[:, :, :].rearrange("p a b -> p (a b)"), in_=s_gates[:, :]),
        inc=[(ctx.LD, 16)])
    lg_0 = ctx.LG.n
    ld_g = ctx.LD.n

    w1e = ctx.sbuf("p6_w1e", [128, C, 2048], BF16)
    w3e = ctx.sbuf("p6_w3e", [128, C, 2048], BF16)
    w2e = ctx.sbuf("p6_w2e", [128, 16, D], BF16)
    a1e = ctx.sbuf("p6_a1e", [128, C, 16], BF16)
    a3e = ctx.sbuf("p6_a3e", [128, C, 16], BF16)
    a2e = ctx.sbuf("p6_a2e", [128, 16, 16], BF16)
    b1le = ctx.sbuf("p6_b1le", [16, 2048], BF16)
    b3le = ctx.sbuf("p6_b3le", [16, 2048], BF16)
    b2le = ctx.sbuf("p6_b2le", [16, D], BF16)
    bm1e = ctx.sbuf("p6_bm1e", [128, 16], F32)
    bm3e = ctx.sbuf("p6_bm3e", [128, 16], F32)
    bm2b = ctx.sbuf("p6_bm2b", [128, D], BF16)
    al1 = ctx.sbuf("p6_al1", [16, 1024], BF16)
    al3 = ctx.sbuf("p6_al3", [16, 1024], BF16)
    al2 = ctx.sbuf("p6_al2", [16, 1024], BF16)
    glu = ctx.sbuf("p6_glu", [128, 16, 1024], BF16)
    s1t = [ctx.sbuf(f"p6_s1t{i}", [128, 512], BF16) for i in range(2)]
    h3t = [ctx.sbuf(f"p6_h3t{i}", [128, 512], BF16) for i in range(2)]
    ytmp = ctx.sbuf("p6_ytmp", [128, D], F32)
    x2t = [ctx.sbuf(f"p6_x2t{i}", [128, D], F32) for i in range(2)]
    yout = [ctx.sbuf(f"p6_yout{i}", [128, D], F32) for i in range(2)]
    ps_al = [ctx.psum(f"p6_ps_al{i}", [16, 512], F32) for i in range(2)]
    ps_y = [ctx.psum(f"p6_ps_y{i}", [128, D], F32) for i in range(2)]
    pool = ProjPool(ctx, "p6_po", 2, 512)

    lg13 = [0] * 8
    lga = [0] * 8
    lg2 = [0] * 8
    ldb13 = [0] * 8
    ldb2 = [0] * 8

    def load_w13(ex, wait):
        for dst, src in ((w1e, w1T), (w3e, w3T)):
            pr.on("gpsimd", lambda e, dst=dst, src=src, ex=ex: e.dma_start(
                out=dst[:, :, :], in_=src[ex].rearrange("(c p) o -> p c o", p=128)),
                wait=wait, inc=[(ctx.LG, 16)])
        for dst, src in ((b1le, b1l), (b3le, b3l)):
            pr.on("gpsimd", lambda e, dst=dst, src=src, ex=ex: e.dma_start(
                out=dst[:, :], in_=src[ex]), wait=wait, inc=[(ctx.LG, 16)])
        lg13[ex] = ctx.LG.n
        for dst, src in ((bm1e, bm1), (bm3e, bm3)):
            pr.on("sync", lambda e, dst=dst, src=src, ex=ex: e.dma_start(
                out=dst[:, :], in_=src[ex]), wait=wait, inc=[(ctx.LD, 16)])
        ldb13[ex] = ctx.LD.n

    def load_a(ex, wait):
        for dst, src in ((a1e, a1T), (a3e, a3T)):
            pr.on("gpsimd", lambda e, dst=dst, src=src, ex=ex: e.dma_start(
                out=dst[:, :, :], in_=src[ex].rearrange("(c p) o -> p c o", p=128)),
                wait=wait, inc=[(ctx.LG, 16)])
        lga[ex] = ctx.LG.n

    def load_w2(ex, wait):
        pr.on("gpsimd", lambda e, ex=ex: e.dma_start(
            out=w2e[:, :, :], in_=w2T[ex].rearrange("(c p) o -> p c o", p=128)),
            wait=wait, inc=[(ctx.LG, 16)])
        pr.on("gpsimd", lambda e, ex=ex: e.dma_start(
            out=a2e[:, :, :], in_=a2T[ex].rearrange("(c p) o -> p c o", p=128)),
            wait=wait, inc=[(ctx.LG, 16)])
        pr.on("gpsimd", lambda e, ex=ex: e.dma_start(
            out=b2le[:, :], in_=b2l[ex]), wait=wait, inc=[(ctx.LG, 16)])
        lg2[ex] = ctx.LG.n
        pr.on("gpsimd", lambda e, ex=ex: e.dma_start(
            out=bm2b[:, :], in_=bm2r[ex:ex + 1, :].partition_broadcast(128)),
            wait=wait, inc=[(ctx.LG, 16)])
        lg2[ex] = ctx.LG.n
        ldb2[ex] = ctx.LD.n

    load_w13(0, [])
    load_a(0, [])
    load_w2(0, [])

    x2_guard = [[]]
    al_free = [(None, 0), (None, 0)]
    for ex in range(8):
        for hf in range(2):
            base = hf * 1024
            for i4, (asrc, dst, tg) in enumerate(
                    [(a, d, tg) for (a, d) in ((a1e, al1), (a3e, al3)) for tg in range(2)]):
                jb = i4 % 2
                off = base + tg * 512

                def mma(e, asrc=asrc, off=off, jb=jb):
                    for c in range(C):
                        i = e.matmul(ps_al[jb][0:16, 0:512], asrc[:, c, 0:16],
                                     xn1_sb[:, c, off:off + 512],
                                     start=(c == 0), stop=(c == C - 1))
                    return i
                wA = [(ctx.LG, max(lga[ex], lg_0))]
                if al_free[jb][0] is not None:
                    wA.append(al_free[jb])
                if i4 == 0:
                    wA.append((ctx.TC, ctx.TC.n))
                pr.on("tensor", mma, wait=wA, inc=[(ctx.TC, 1)])
                t_a = ctx.TC.n
                pr.on("scalar", lambda e, dst=dst, tg=tg, jb=jb: e.activation(
                    dst[0:16, tg * 512:(tg + 1) * 512], ps_al[jb][0:16, 0:512], AF.Copy),
                    wait=[(ctx.TC, t_a)], inc=[(ctx.SC, 1)])
                al_free[jb] = (ctx.SC, ctx.SC.n)
            s_al = ctx.SC.n
            if hf == 1 and ex < 7:
                load_a(ex + 1, [(ctx.TC, ctx.TC.n)])

            glu_guard = (ctx.TC, ctx.TC.n)
            for hc in range(16):
                for tg in range(2):
                    off = base + tg * 512
                    p2 = (hc * 2 + tg) % 2
                    j1, ps1, free1 = pool.acquire()
                    w1_ = [(ctx.SC, s_al), (ctx.LG, lg13[ex]), (ctx.LD, ldb13[ex])]
                    if free1[0] is not None:
                        w1_.append(free1)

                    def mm1(e, hc=hc, off=off, tg=tg, ps=ps1):
                        for c in range(C):
                            e.matmul(ps[:, 0:512], w1e[:, c, hc * 128:(hc + 1) * 128],
                                     xn1_sb[:, c, off:off + 512], start=(c == 0), stop=False)
                        return e.matmul(ps[:, 0:512], b1le[0:16, hc * 128:(hc + 1) * 128],
                                        al1[0:16, tg * 512:(tg + 1) * 512],
                                        start=False, stop=True)
                    pr.on("tensor", mm1, wait=w1_, inc=[(ctx.TC, 1)])
                    t_1 = ctx.TC.n
                    pr.on("scalar", lambda e, hc=hc, p2=p2, ps=ps1: e.activation(
                        s1t[p2][:, :], ps[:, 0:512], AF.Silu, bias=bm1e[:, hc:hc + 1]),
                        wait=[(ctx.TC, t_1), (ctx.VC, ctx.VC.n)], inc=[(ctx.SC, 1)])
                    pool.release(j1, ctx.SC, ctx.SC.n)
                    s_1 = ctx.SC.n

                    j3, ps3, free3 = pool.acquire()
                    w3_ = [(ctx.SC, s_al), (ctx.LG, lg13[ex]), (ctx.LD, ldb13[ex])]
                    if free3[0] is not None:
                        w3_.append(free3)

                    def mm3(e, hc=hc, off=off, tg=tg, ps=ps3):
                        for c in range(C):
                            e.matmul(ps[:, 0:512], w3e[:, c, hc * 128:(hc + 1) * 128],
                                     xn1_sb[:, c, off:off + 512], start=(c == 0), stop=False)
                        return e.matmul(ps[:, 0:512], b3le[0:16, hc * 128:(hc + 1) * 128],
                                        al3[0:16, tg * 512:(tg + 1) * 512],
                                        start=False, stop=True)
                    pr.on("tensor", mm3, wait=w3_, inc=[(ctx.TC, 1)])
                    t_3 = ctx.TC.n
                    pr.on("vector", lambda e, hc=hc, p2=p2, ps=ps3: e.tensor_scalar(
                        h3t[p2][:, :], ps[:, 0:512], bm3e[:, hc:hc + 1], None, ALU.add),
                        wait=[(ctx.TC, t_3), (ctx.SC, ctx.SC.n)], inc=[(ctx.VC, 1)])
                    pool.release(j3, ctx.VC, ctx.VC.n)
                    v_3 = ctx.VC.n
                    pr.on("vector", lambda e, hc=hc, tg=tg, p2=p2: e.tensor_tensor(
                        glu[:, hc, tg * 512:(tg + 1) * 512], s1t[p2][:, :],
                        h3t[p2][:, :], ALU.mult),
                        wait=[(ctx.VC, v_3), (ctx.SC, s_1), glu_guard], inc=[(ctx.VC, 1)])
            v_glu = ctx.VC.n
            if hf == 1 and ex < 7:
                load_w13(ex + 1, [(ctx.TC, ctx.TC.n), (ctx.SC, ctx.SC.n),
                                  (ctx.VC, ctx.VC.n)])

            for tg in range(2):
                jb = tg % 2

                def mma2(e, tg=tg, jb=jb):
                    for hc in range(16):
                        i = e.matmul(ps_al[jb][0:16, 0:512], a2e[:, hc, 0:16],
                                     glu[:, hc, tg * 512:(tg + 1) * 512],
                                     start=(hc == 0), stop=(hc == 15))
                    return i
                wC = [(ctx.VC, v_glu), (ctx.LG, lg2[ex])]
                if al_free[jb][0] is not None:
                    wC.append(al_free[jb])
                pr.on("tensor", mma2, wait=wC, inc=[(ctx.TC, 1)])
                t_a2 = ctx.TC.n
                pr.on("scalar", lambda e, tg=tg, jb=jb: e.activation(
                    al2[0:16, tg * 512:(tg + 1) * 512], ps_al[jb][0:16, 0:512], AF.Copy),
                    wait=[(ctx.TC, t_a2)], inc=[(ctx.SC, 1)])
                al_free[jb] = (ctx.SC, ctx.SC.n)
            s_al2 = ctx.SC.n

            x2t_guard = [(None, 0), (None, 0)]
            yps_free = [(None, 0), (None, 0)]
            yout_free = [0, 0]
            for tc in range(8):
                gch = hf * 8 + tc
                p2 = tc % 2
                gbase = base + tc * 128
                wx = [(ctx.ST, None)] + x2_guard[0]
                if x2t_guard[p2][0] is not None:
                    wx.append(x2t_guard[p2])
                pr.on("sync", lambda e, gbase=gbase, p2=p2: e.dma_start(
                    out=x2t[p2][:, :], in_=s_x2a[gbase:gbase + 128, :]),
                    wait=wx, inc=[(ctx.LD, 16)])
                ld_x2 = ctx.LD.n

                def mmy(e, tc=tc, p2=p2):
                    i = None
                    for ng in range(2):
                        for hc in range(16):
                            e.matmul(ps_y[p2][:, ng * 512:(ng + 1) * 512],
                                     glu[:, hc, tc * 128:(tc + 1) * 128],
                                     w2e[:, hc, ng * 512:(ng + 1) * 512],
                                     start=(hc == 0), stop=False)
                        i = e.matmul(ps_y[p2][:, ng * 512:(ng + 1) * 512],
                                     al2[0:16, tc * 128:(tc + 1) * 128],
                                     b2le[0:16, ng * 512:(ng + 1) * 512],
                                     start=False, stop=True)
                    return i
                wy = [(ctx.SC, s_al2), (ctx.LG, lg2[ex]), (ctx.LD, ldb2[ex])]
                if yps_free[p2][0] is not None:
                    wy.append(yps_free[p2])
                if tc == 0:
                    wy.append((ctx.VC, ctx.VC.n))
                pr.on("tensor", mmy, wait=wy, inc=[(ctx.TC, 1)])
                t_y = ctx.TC.n

                def ev_y(e, gch=gch, p2=p2, ex=ex):
                    e.tensor_tensor(ytmp[:, :], ps_y[p2][:, 0:D], bm2b[:, :], ALU.add)
                    return e.scalar_tensor_tensor(
                        yout[p2][:, :], ytmp[:, :], gates[:, gch, ex:ex + 1],
                        x2t[p2][:, :], ALU.mult, ALU.add)
                pr.on("vector", ev_y, wait=[(ctx.TC, t_y), (ctx.LD, max(ld_x2, ld_g)),
                                            (ctx.ST, yout_free[p2])],
                      inc=[(ctx.VC, 1)])
                v_y = ctx.VC.n
                x2t_guard[p2] = (ctx.VC, v_y)
                yps_free[p2] = (ctx.VC, v_y)
                dst = out if ex == 7 else s_x2a
                pr.on("gpsimd", lambda e, gbase=gbase, p2=p2, dst=dst: e.dma_start(
                    out=dst[gbase:gbase + 128, :], in_=yout[p2][:, :]),
                    wait=[(ctx.VC, v_y)], inc=[(ctx.ST, 16)])
                yout_free[p2] = ctx.ST.n
            x2_guard[0] = [(ctx.VC, ctx.VC.n)]
            if hf == 1 and ex < 7:
                load_w2(ex + 1, [(ctx.TC, ctx.TC.n), (ctx.VC, ctx.VC.n)])


PHASE_EMITTERS = {"p0": p0, "p1": p1, "p2": p2, "p3": p3, "p4": p4, "p5": p5, "p6": p6}



LAUNCH_GROUPS = [["p0", "p1", "p2", "p3", "p4", "p5", "p6"]]
NCORES = 8


def kernel(**inputs):
    per_core = prep_inputs(inputs)
    env = [dict(d) for d in per_core]
    for grp in LAUNCH_GROUPS:
        nc, in_names = build_program(grp)
        in_maps = [{k: np.ascontiguousarray(env[c][k]) for k in in_names}
                   for c in range(NCORES)]
        res = run_bass_kernel_spmd(nc, in_maps, core_ids=list(range(NCORES)))
        for c in range(NCORES):
            for k, v in res.results[c].items():
                env[c][k] = v
    out = np.zeros((B, N, D), np.float32)
    for c in range(NCORES):
        b, h = c // 2, c % 2
        out[b, own_token_idx(h), :] = env[c]["out_x2"]
    return out
